# version 01 (final)
# baseline (speedup 1.0000x reference)
import numpy as np
import ml_dtypes
from contextlib import ExitStack

import concourse.bass as bass
import concourse.mybir as mybir
from concourse.ap import AP
from concourse.bass_utils import run_bass_kernel_spmd
from concourse import library_config

F32 = mybir.dt.float32
BF16 = mybir.dt.bfloat16
ALU = mybir.AluOpType
ACT = mybir.ActivationFunctionType
AX = mybir.AxisListType
BF = ml_dtypes.bfloat16

T = 2048
TT = 16
D = 1024
C = 8
NB = 4
GROUPS = [[0, 1], [2, 3], [4, 5], [6, 7]]
EPS_F32 = float(np.finfo(np.float32).eps)

N_SLOTS = 4
S2CUT = 99
S2TAIL = 99
LNPH = 99
TAPS = []


def bc(col_ap, n):
    return AP(col_ap.tensor, col_ap.offset, [col_ap.ap[0], [0, n]])


def fview(ap_, off, dims):
    return AP(ap_.tensor, ap_.offset + off, [ap_.ap[0]] + dims)


def tok_dram(ap2d, rows=T, width=D):
    return AP(ap2d.tensor, ap2d.offset,
              [[width, 128], [128 * width, rows // 128], [1, width]])


class Prog:
    ENGS = ("t", "v", "s", "g", "y")

    def __init__(self, nc):
        self.nc = nc
        self.ops = {e: [] for e in self.ENGS}
        self.last = {e: None for e in self.ENGS}
        self.ctx = ExitStack()
        self.sems = {}
        self.cnt = {}

    DEDICATED = {"ydma", "gdma", "wload", "ccs", "pbg",
                 "el2", "evv", "evv3", "pev", "rpe", "rps", "rtp", "tpev", "tptr",
                 "bar_t", "bar_v", "bar_s", "bar_g", "bar_y",
                 "fen_t", "fen_v", "fen_s", "fen_g", "fen_y"}

    def sem(self, name):
        if name not in self.cnt:
            self.cnt[name] = 0
        if name in self.DEDICATED:
            key = name
        else:
            import zlib
            key = "sh%02d" % (zlib.crc32(name.encode()) % 40)
        if key not in self.sems:
            self.sems[key] = self.ctx.enter_context(self.nc.semaphore(key))
        return self.sems[key]

    def key(self, name):
        if name in self.DEDICATED:
            return name
        import zlib
        return "sh%02d" % (zlib.crc32(name.encode()) % 40)

    def op(self, e, meth, *args, inc=None, by=1, **kw):
        import sys as _sys
        f = _sys._getframe(1)
        ln = f.f_lineno
        if f.f_code.co_name in ("op", "gdma", "ydma", "mm", "tap"):
            ln = f.f_back.f_lineno
        rec = {"meth": meth, "args": args, "kw": kw, "incs": [], "line": ln}
        tv = None
        if inc is not None:
            sm = self.sem(inc)
            k = self.key(inc)
            self.cnt[k] = self.cnt.get(k, 0) + by
            tv = self.cnt[k]
            rec["incs"].append((sm, by, tv))
        self.ops[e].append(rec)
        self.last[e] = rec
        return tv

    def val(self, name):
        return self.cnt.get(self.key(name), 0)

    def wait(self, e, name, v=None):
        s = self.sem(name)
        tv = self.cnt.get(self.key(name), 0) if v is None else v
        if tv <= 0:
            return
        self.ops[e].append({"wait": (s, tv)})

    DMA_METHS = ("dma_start", "dma_start_transpose", "dma_gather", "dma_scatter_add")

    def tick(self, e, name=None):
        rec = self.last[e]
        if rec is not None and rec["incs"]:
            sm, by, tv = rec["incs"][0]
            return (sm, tv)
        nm = "fen_" + e
        s = self.sem(nm)
        by = 16 if (rec is not None and rec["meth"] in self.DMA_METHS) else 1
        prev = self.cnt.get(nm, 0)
        self.cnt[nm] = prev + by
        tv = self.cnt[nm]
        if rec is not None:
            if e == "g" and prev > 0:
                idx = len(self.ops[e]) - 1 - self.ops[e][::-1].index(rec)
                self.ops[e].insert(idx, {"wait": (s, prev)})
            rec["incs"].append((s, by, tv))
        else:
            if prev > 0:
                self.ops[e].append({"wait": (s, prev)})
            self.ops[e].append({"seminc": (s, 1)})
        return (s, tv)

    def wait_tok(self, e, tok):
        self.ops[e].append({"wait": tok})

    def barrier(self, engs="tvsgy"):
        toks = {e: self.tick(e) for e in engs}
        for e in engs:
            for o in engs:
                if o != e:
                    self.wait_tok(e, toks[o])
        for e in engs:
            self.last[e] = None

    def emit(self):
        with self.nc.Block() as block:
            for attr, key in (("tensor", "t"), ("vector", "v"), ("scalar", "s"),
                              ("gpsimd", "g"), ("sync", "y")):
                ops = self.ops[key]
                autodrain = key in ("v", "s", "g")

                def body(eng, ops=ops, autodrain=autodrain):
                    pending = False
                    for rec in ops:
                        if "wait" in rec:
                            eng.wait_ge(*rec["wait"])
                        elif "seminc" in rec:
                            eng.sem_inc(*rec["seminc"])
                        else:
                            if autodrain and pending and rec["meth"] not in Prog.DMA_METHS:
                                eng.drain()
                            r = getattr(eng, rec["meth"])(*rec["args"], **rec["kw"])
                            for sm, by, _tv in rec["incs"]:
                                r.then_inc(sm, by)
                            if rec["meth"] not in Prog.DMA_METHS:
                                pending = True
                getattr(block, attr)(body)


class Pool:
    def __init__(self, nc, name, cols_f32):
        self.t = nc.alloc_sbuf_tensor(name, [128, cols_f32], F32)
        self.cols = cols_f32
        self.off = 0
        self.marks = []
        self.high = 0

    def alloc(self, cols_f32, dtype=F32):
        cols_f32 = (cols_f32 + 1) & ~1
        off = self.off
        self.off += cols_f32
        self.high = max(self.high, self.off)
        assert self.off <= self.cols, f"pool overflow {self.off} > {self.cols}"
        ap_ = self.t[:, off:off + cols_f32]
        if dtype == BF16:
            ap_ = ap_.bitcast(BF16)
        return ap_

    def mark(self):
        self.marks.append(self.off)

    def release(self):
        self.off = self.marks.pop()


def build_program(dbg_names=()):
    nc = bass.Bass("TRN2", target_bir_lowering=False)
    E = {}

    def inp(name, shape, dtype=F32):
        E[name] = nc.dram_tensor(name, list(shape), dtype, kind="ExternalInput").ap()
        return E[name]

    inp("x_in", (T, D))
    inp("identity", (128, 128))
    inp("identity_bf", (128, 128), BF16)
    inp("ones_bf", (1, 128), BF16)
    inp("ones_cbf", (128, 1), BF16)
    inp("selcol", (64, 2))
    inp("halomask", (128, 2))
    inp("m_w1t", (D, D)); inp("m_w2t", (D, D)); inp("m_wlastt", (D, D))
    inp("m_dwt", (D, 64)); inp("m_bwt", (D, 64)); inp("m_cwt", (64, D))
    inp("m_dwmt", (D, D))
    inp("m_convw", (128, C * 3)); inp("m_convb", (128, C))
    inp("m_A", (64, 1)); inp("m_Bb", (64, 1)); inp("m_db", (64, 1))
    inp("m_Cb", (128, C)); inp("m_Db", (128, C)); inp("m_normw", (128, C))

    inp("g_qwt", (D, D)); inp("g_qb", (128, C))
    inp("g_kwt", (D, 256)); inp("g_kb", (128, 2))
    inp("g_vwt", (D, 256)); inp("g_vbrow", (128, 256))
    inp("g_owt", (D, D)); inp("g_ob", (128, C))
    inp("g_gatew", (D, 8))
    inp("g_nw1", (128, C)); inp("g_nw2", (128, C))
    inp("g_w1", (8, D, 2048)); inp("g_w2", (8, 2048, D))

    inp("a_qwt", (D, D)); inp("a_kwt", (D, D)); inp("a_vwt", (D, D))
    inp("a_owt", (D, D))
    inp("a_cos", (128, TT * 512), BF16); inp("a_sin", (128, TT * 512), BF16)
    inp("a_poison", (128, 32))
    inp("a_diagm", (128, 2560), BF16)
    inp("a_kvmask", (128, 4))
    inp("f_rt", (128, C), BF16)
    inp("f_w1lt", (D, 512)); inp("f_b1l", (128, 4))
    inp("f_w2lt", (512, D)); inp("f_b2lrow", (128, D))
    inp("f_w1ht", (D, 4096)); inp("f_b1h", (128, 32))
    inp("f_w2ht", (4096, D)); inp("f_b2hrow", (128, D))
    inp("f_gl", (128, C)); inp("f_gh", (128, C))
    inp("f_selrow", (1, 2))
    inp("ones_sq", (128, 128))

    out_x = nc.dram_tensor("out", [T, D], F32, kind="ExternalOutput").ap()
    dbg = {}
    for name in dbg_names:
        dbg[name] = nc.dram_tensor("dbg_" + name, [128, TT * D], F32,
                                   kind="ExternalOutput").ap()

    d_h_in = nc.dram_tensor("d_h_in", [2, 128, 16], F32).ap()
    d_h_out = nc.dram_tensor("d_h_out", [128, 16], F32).ap()
    d_sc_in = nc.dram_tensor("d_sc_in", [2, 64, T], F32).ap()
    d_sc_out = nc.dram_tensor("d_sc_out", [2, 2, 64, T], F32).ap()
    d_xsp = nc.dram_tensor("d_xsp", [T, D], F32).ap()
    d_kv_in = nc.dram_tensor("d_kv_in", [128, 4096 + 4160], BF16).ap()
    d_kv_out = nc.dram_tensor("d_kv_out", [2, 128, 4096 + 4160], BF16).ap()
    KVW = 16384 + 16640
    d_rs_in = nc.dram_tensor("d_rs_in", [2, 128, 2 * KVW], BF16).ap()
    d_rs_out = nc.dram_tensor("d_rs_out", [128, 2 * KVW], BF16).ap()
    d_s_in = nc.dram_tensor("d_s_in", [2, 1, T], F32).ap()
    d_s_out = nc.dram_tensor("d_s_out", [2, 1, T], F32).ap()

    p = Prog(nc)

    X = nc.alloc_sbuf_tensor("X", [128, TT * D], F32)
    ident = nc.alloc_sbuf_tensor("ident", [128, 128], F32)
    identb = nc.alloc_sbuf_tensor("identb", [128, 128], BF16)
    onesb = nc.alloc_sbuf_tensor("onesb", [1, 128], BF16)
    onescb = nc.alloc_sbuf_tensor("onescb", [128, 1], BF16)
    onesq_s = nc.alloc_sbuf_tensor("onesq_s", [128, 128], F32)
    pool = Pool(nc, "pool", 29000)
    wpool = Pool(nc, "wpool", 4736)
    cols = nc.alloc_sbuf_tensor("cols", [128, 384], F32)
    dump = nc.alloc_sbuf_tensor("dump", [128, 2048], F32)
    ccur = [0]

    def col(n=1):
        o = ccur[0]
        ccur[0] += n
        assert ccur[0] <= 384
        return cols[:, o:o + n]

    pA = nc.alloc_psum_tensor("pA", [128, 2048], F32)
    pB = nc.alloc_psum_tensor("pB", [128, 2048], F32)

    def X_t(j):
        return X[:, j * D:(j + 1) * D]

    wslot = [0]

    def load_weight(hbm_ap, K, O):
        tok = p.tick("t")
        p.wait_tok("g", tok)
        if K % 128 == 0:
            kc_n = K // 128
            buf = wpool.alloc(kc_n * O // 2, BF16)
            rs = hbm_ap.ap[-2][0]
            src = AP(hbm_ap.tensor, hbm_ap.offset,
                     [[rs, 128], [128 * rs, kc_n], [1, O]])
            tv = p.op("g", "dma_start", inc="wload", by=16, out=buf, in_=src)
            return (lambda kc, o0, ow, b=buf, O=O: b[:, kc * O + o0:kc * O + o0 + ow]), tv
        else:
            buf0 = wpool.alloc(O // 2, BF16)
            buf = AP(buf0.tensor, buf0.offset, [[buf0.ap[0][0], K], [1, O]])
            tv = p.op("g", "dma_start", inc="wload", by=16, out=buf, in_=hbm_ap)
            return (lambda kc, o0, ow, b=buf: b[0:K, o0:o0 + ow]), tv

    def vs_sync():
        a = p.tick("v")
        b = p.tick("s")
        p.wait_tok("s", a)
        p.wait_tok("v", b)

    def ln_cols(srcfn, ntiles, width, eps):
        ssum = col(ntiles)
        qsum = col(ntiles)
        for j in range(ntiles):
            p.op("s", "activation", dump[:, 0:width], srcfn(j), ACT.Identity,
                 accum_out=ssum[:, j:j + 1])
            p.op("v", "scalar_tensor_tensor", dump[:, 1024:1024 + width], srcfn(j), 1.0,
                 srcfn(j), ALU.mult, ALU.mult, accum_out=qsum[:, j:j + 1])
        sv = p.tick("s")
        p.wait_tok("v", sv)
        mean = col(ntiles); var = col(ntiles); r = col(ntiles); nmr = col(ntiles)
        inv = 1.0 / width
        p.op("v", "tensor_scalar", mean, ssum, inv, None, ALU.mult)
        p.op("v", "tensor_scalar", var, qsum, inv, None, ALU.mult)
        p.op("v", "tensor_tensor", nmr, mean, mean, ALU.mult)
        p.op("v", "tensor_tensor", var, var, nmr, ALU.subtract)
        p.op("v", "tensor_scalar", r, var, eps, None, ALU.add)
        p.op("v", "reciprocal", r, r)
        vs_sync()
        p.op("s", "activation", r, r, ACT.Sqrt)
        return r, nmr, mean, var

    def finish_nmr(r, nmr, mean):
        p.op("v", "tensor_tensor", nmr, mean, r, ALU.mult)
        p.op("v", "tensor_scalar", nmr, nmr, -1.0, None, ALU.mult)

    def transpose_tok_fm(src_tile, dst_fm, ntiles=TT, nchunks=C, dst_w=T):
        for j in range(ntiles):
            half = j % 2
            pdst = pT[:, half * 1024:half * 1024 + nchunks * 128]
            if j >= 2:
                p.wait("t", "tpev", p.val("tpev") - 1)
            for c in range(nchunks):
                inc = "tptr" if c == nchunks - 1 else None
                p.op("t", "transpose", pdst[:, c * 128:(c + 1) * 128],
                     src_tile(j, c), ident[:], inc=inc)
            p.wait("v", "tptr", p.val("tptr"))
            dst_ap = fview(dst_fm, j * 128, [[dst_w, nchunks], [1, 128]])
            p.op("v", "tensor_copy", dst_ap, pdst, inc="tpev")
        p.wait("t", "tpev", p.val("tpev"))

    def ln_transpose(srcfn, r, nmr, dst_fm, ntiles=TT, nchunks=C, dst_w=T):
        stage = dump[:, 0:2048]
        for j in range(ntiles):
            st = stage[:, (j % 2) * 1024:(j % 2) * 1024 + 1024]
            if j >= 2:
                p.wait("s", "tptr", p.val("tptr") - 1)
            p.op("s", "activation", st[:, 0:nchunks * 128], srcfn(j), ACT.Identity,
                 scale=r[:, j:j + 1], bias=nmr[:, j:j + 1], inc="lnn")
            p.wait("t", "lnn", p.val("lnn"))
            half = j % 2
            pdst = pB[:, half * 1024:half * 1024 + 1024]
            if j >= 2:
                p.wait("t", "tpev", p.val("tpev") - 1)
            for c in range(nchunks):
                p.op("t", "transpose", pdst[:, c * 128:(c + 1) * 128],
                     st[:, c * 128:(c + 1) * 128], ident[:],
                     inc="tptr" if c == nchunks - 1 else None)
            p.wait("v", "tptr", p.val("tptr"))
            dst_ap = fview(dst_fm, j * 128, [[dst_w, nchunks], [1, 128]])
            p.op("v", "tensor_copy", dst_ap, pdst[:, 0:nchunks * 128], inc="tpev")
        p.wait("t", "tpev", p.val("tpev"))

    def mm_transpose_bf(src_tile_bf, out_psum):
        return mm(out_psum, src_tile_bf, identb[:], True, True)

    def ln_add_tile(src_ap, j, sj):
        if LNPH < 1:
            return
        p.op("v", "tensor_reduce", sj[:, 0:1], src_ap, AX.X, ALU.add, inc="lrd")
        if LNPH < 1.5:
            return
        p.op("s", "activation", dump[:, 1024:2048], src_ap, ACT.Square,
             accum_out=sj[:, 1:2], inc="lsq")
        p.wait("v", "lsq", p.val("lsq"))
        if LNPH < 2:
            return
        p.op("v", "tensor_scalar", sj[:, 0:1], sj[:, 0:1], 1.0 / D, None, ALU.mult)
        p.op("v", "tensor_scalar", sj[:, 1:2], sj[:, 1:2], 1.0 / D, None, ALU.mult)
        p.op("v", "tensor_tensor", sj[:, 2:3], sj[:, 0:1], sj[:, 0:1], ALU.mult)
        p.op("v", "tensor_tensor", sj[:, 1:2], sj[:, 1:2], sj[:, 2:3], ALU.subtract)
        p.op("v", "tensor_scalar", sj[:, 1:2], sj[:, 1:2], 1e-5, None, ALU.add)
        p.op("v", "reciprocal", sj[:, 1:2], sj[:, 1:2])
        p.op("v", "tensor_scalar", sj[:, 0:1], sj[:, 0:1], -1.0, None, ALU.mult,
             inc="lov")
        p.wait("s", "lov", p.val("lov"))
        if LNPH < 3:
            return
        p.op("s", "activation", sj[:, 1:2], sj[:, 1:2], ACT.Sqrt)
        p.op("s", "activation", sj[:, 3:4], sj[:, 0:1], ACT.Identity,
             scale=sj[:, 1:2])
        if LNPH < 4:
            return
        p.op("s", "activation", dump[:, 0:D], src_ap, ACT.Identity,
             scale=sj[:, 1:2], bias=sj[:, 3:4], inc="los")
        p.wait("v", "los", p.val("los"))
        if LNPH < 5:
            return
        p.op("v", "tensor_tensor", X_t(j), X_t(j), dump[:, 0:D], ALU.add, inc="loa")
        p.wait("s", "loa", p.val("loa"))
        p.wait("t", "loa", p.val("loa"))

    def cc(kind, ins, outs, alu=ALU.bypass):
        tv = p.op("g", "collective_compute", kind, alu, inc="ccs",
                  replica_groups=GROUPS, ins=ins, outs=outs)
        p.wait("g", "ccs", tv)

    def gdma(out, in_):
        return p.op("g", "dma_start", inc="gdma", by=16, out=out, in_=in_)

    def ydma(out, in_):
        p.wait("y", "ydma")
        return p.op("y", "dma_start", inc="ydma", by=16, out=out, in_=in_)

    taps = {}

    def tap(name, ap_, shape, dtype, prod_eng, prod_sem=None):
        if name not in TAPS:
            return
        o = nc.dram_tensor("tap_" + name, list(shape), dtype,
                           kind="ExternalOutput").ap()
        tok = p.tick(prod_eng)
        p.wait_tok("y", tok)
        tv = ydma(o, ap_)
        for e in "tvsg":
            p.wait(e, "ydma", tv)

    def mm(out, w, r, start, stop, inc=None, tp=None):
        kw = {}
        if tp is not None:
            kw["tile_position"] = tp
        return p.op("t", "matmul", out, w, r, inc=inc, start=start, stop=stop, **kw)

    ydma(X[:], tok_dram(E["x_in"]))
    ydma(ident[:], E["identity"])
    ydma(identb[:], E["identity_bf"])
    ydma(onesb[:], E["ones_bf"])
    ydma(onescb[:], E["ones_cbf"])
    ydma(onesq_s[:], E["ones_sq"])
    selc = nc.alloc_sbuf_tensor("selc", [64, 2], F32)
    ydma(selc[:], E["selcol"])
    hmask = nc.alloc_sbuf_tensor("hmask", [128, 2], F32)
    ydma(hmask[:], E["halomask"])
    convw = nc.alloc_sbuf_tensor("convw", [128, C * 3], F32)
    convb = nc.alloc_sbuf_tensor("convb", [128, C], F32)
    m_nw = nc.alloc_sbuf_tensor("m_nw", [128, C], F32)
    mCb = nc.alloc_sbuf_tensor("mCb", [128, C], F32)
    mDb = nc.alloc_sbuf_tensor("mDb", [128, C], F32)
    smA = nc.alloc_sbuf_tensor("smA", [64, 4], F32)
    ydma(convw[:], E["m_convw"])
    ydma(convb[:], E["m_convb"])
    ydma(m_nw[:], E["m_normw"])
    ydma(mCb[:], E["m_Cb"])
    ydma(mDb[:], E["m_Db"])
    ydma(smA[:, 0:1], E["m_A"])
    ydma(smA[:, 2:3], E["m_Bb"])
    ydma(smA[:, 3:4], E["m_db"])
    for e in "tvsg":
        p.wait(e, "ydma")
    p.op("s", "activation", smA[:, 1:2], smA[:, 0:1], ACT.Sigmoid, scale=-1.0)
    p.op("s", "activation", smA[:, 1:2], smA[:, 1:2], ACT.Ln)
    sv = p.tick("s")
    p.wait_tok("v", sv)
    p.op("v", "tensor_scalar", smA[:, 1:2], smA[:, 1:2], -1.0, None, ALU.mult)
    p.op("v", "tensor_tensor", mCb[:], mCb[:], mDb[:], ALU.add)
    p.barrier()

    def slot1():
        ccur[0] = 0
        p.barrier()
        pool.mark()
        r, nmr, mean, var = ln_cols(X_t, TT, D, 1e-5)
        t2 = col(TT)
        p.op("v", "tensor_scalar", t2, var, 1e-5, None, ALU.add)
        p.op("v", "reciprocal", t2, t2)
        p.op("v", "tensor_tensor", t2, var, t2, ALU.mult)
        p.op("v", "tensor_scalar", t2, t2, EPS_F32, None, ALU.add)
        p.op("v", "reciprocal", t2, t2)
        vs_sync()
        p.op("s", "activation", t2, t2, ACT.Sqrt)
        sv = p.tick("s")
        p.wait_tok("v", sv)
        r1c = col(TT)
        nm1c = col(TT)
        p.op("v", "tensor_copy", r1c, r)
        p.op("v", "tensor_tensor", nm1c, mean, r1c, ALU.mult)
        p.op("v", "tensor_scalar", nm1c, nm1c, -1.0, None, ALU.mult)
        p.op("v", "tensor_tensor", r, r, t2, ALU.mult)
        finish_nmr(r, nmr, mean)
        tap("cols", cols[:, 0:128], [128, 128], F32, "v", "tapv")
        vs_sync()

        sz_bf = pool.alloc(C * T // 2, BF16)
        xc_bf = pool.alloc(C * T // 2, BF16)
        pool.mark()
        xn_fm = pool.alloc(C * T // 2, BF16)

        def fmc(c):
            return xn_fm[:, c * T:(c + 1) * T]

        ln_transpose(X_t, r, nmr, xn_fm)
        p.barrier()
        for c in range(C):
            p.op("v", "tensor_scalar", fmc(c), fmc(c), m_nw[:, c:c + 1], None,
                 ALU.mult, inc="nwm" if c == C - 1 else None)
        p.wait("g", "nwm")
        p.wait("t", "nwm")
        tap("xn_fm", xn_fm, [128, C * T], BF16, "v", "tapv")
        halos = col(32)
        xnhalo = fview(xn_fm, 2046, [[T, C], [1, 2]])
        for reg in range(2):
            p.op("v", "tensor_scalar", halos[:, reg * 16:reg * 16 + 16], xnhalo,
                 hmask[:, reg:reg + 1], None, ALU.mult,
                 inc="hal" if reg == 1 else None)
        p.wait("g", "hal")
        gdma(d_h_in[0], halos[:, 0:16])
        gdma(d_h_in[1], halos[:, 16:32])
        p.wait("g", "gdma")
        cc("ReduceScatter", [d_h_in], [d_h_out], ALU.add)
        xnh = col(16)
        tvh = gdma(xnh, d_h_out)
        p.wait("v", "gdma", tvh)
        xnh_bf = col(8).bitcast(BF16)
        p.op("v", "tensor_copy", xnh_bf, xnh, inc="halb")
        p.wait("t", "halb")

        wpool.mark()
        w2t, w2v = load_weight(E["m_w2t"], D, D)
        p.wait("t", "wload")
        for c in range(C):
            if c > 0:
                p.wait("t", "evz", p.val("evz"))
            for tb in range(NB):
                o = pA[:, tb * 512:(tb + 1) * 512]
                for kc in range(C):
                    mm(o, w2t(kc, c * 128, 128), fmc(kc)[:, tb * 512:(tb + 1) * 512],
                       kc == 0, kc == C - 1, inc="mmz" if kc == C - 1 else None)
            p.wait("s", "mmz", p.val("mmz"))
            for tb in range(NB):
                p.op("s", "activation", sz_bf[:, c * T + tb * 512:c * T + (tb + 1) * 512],
                     pA[:, tb * 512:(tb + 1) * 512], ACT.Silu, inc="evz")
        p.wait("t", "evz", p.val("evz"))
        tap("sz", sz_bf, [128, C * T], BF16, "s", "taps")
        wpool.release()

        wpool.mark()
        w1t, w1v = load_weight(E["m_w1t"], D, D)
        pool.mark()
        xb = pool.alloc(2050)
        cvt = pool.alloc(2048)
        p.wait("t", "wload")
        for c in range(C):
            for tb in range(NB):
                o = pA[:, tb * 512:(tb + 1) * 512]
                for kc in range(C):
                    mm(o, w1t(kc, c * 128, 128), fmc(kc)[:, tb * 512:(tb + 1) * 512],
                       kc == 0, kc == C - 1, inc="mmb" if kc == C - 1 else None)
            for kc in range(C):
                mm(pB[:, 0:2], w1t(kc, c * 128, 128), xnh_bf[:, kc:kc + 1][:, 0:1] if False else
                   AP(xnh_bf.tensor, xnh_bf.offset + kc * 2, [xnh_bf.ap[0], [1, 2]]),
                   kc == 0, kc == C - 1, inc="mmb" if kc == C - 1 else None)
            p.wait("s", "mmb", p.val("mmb"))
            p.op("s", "activation", xb[:, 0:2], pB[:, 0:2], ACT.Identity)
            p.op("s", "activation", xb[:, 2:2050], pA[:, 0:2048], ACT.Identity,
                 inc="evb")
            p.wait("v", "evb", p.val("evb"))
            p.wait("t", "evb", p.val("evb"))
            p.op("v", "tensor_scalar", cvt[:], xb[:, 0:2048],
                 convw[:, c * 3:c * 3 + 1], None, ALU.mult)
            p.op("v", "scalar_tensor_tensor", cvt[:], xb[:, 1:2049],
                 convw[:, c * 3 + 1:c * 3 + 2], cvt[:], ALU.mult, ALU.add)
            p.op("v", "scalar_tensor_tensor", cvt[:], xb[:, 2:2050],
                 convw[:, c * 3 + 2:c * 3 + 3], cvt[:], ALU.mult, ALU.add,
                 inc="cvv")
            p.wait("s", "cvv", p.val("cvv"))
            p.op("s", "activation", xc_bf[:, c * T:(c + 1) * T], cvt[:], ACT.Silu,
                 bias=convb[:, c:c + 1], inc="evc")
            p.wait("v", "evc", p.val("evc"))
        tap("xc", xc_bf, [128, C * T], BF16, "s", "taps")
        pool.release()
        wpool.release()
        pool.release()
        p.barrier()

        wpool.mark()
        dwt, dwv = load_weight(E["m_dwt"], D, 64)
        bwt, bwv = load_weight(E["m_bwt"], D, 64)
        pool.mark()
        fullsA = pool.alloc(2 * T)
        fullsB = pool.alloc(2 * T)
        wks = pool.alloc(2 * T)

        def rows64(buf, o, n):
            return AP(buf.tensor, buf.offset + o, [[buf.ap[0][0], 64], [1, n]])

        abar_l = rows64(fullsA, 0, T)
        bbar_l = rows64(fullsB, 0, T)
        abar_f = rows64(fullsA, 0, 2 * T)
        bbar_f = rows64(fullsB, 0, 2 * T)
        delta = rows64(dump[:], 0, T)
        negA = rows64(smA[:], 1, 1)
        Bb64 = rows64(smA[:], 2, 1)
        db64 = rows64(smA[:], 3, 1)
        ndb = col(1)
        ndb64 = AP(ndb.tensor, ndb.offset, [[ndb.ap[0][0], 64], [1, 1]])
        p.op("v", "tensor_scalar", ndb64, db64, -1.0, None, ALU.mult, inc="ndbv")
        p.wait("s", "ndbv")
        p.wait("t", "wload")
        for tb in range(NB):
            o = pA[0:64, tb * 512:(tb + 1) * 512]
            for kc in range(C):
                mm(o, dwt(kc, 0, 64), xc_bf[:, kc * T + tb * 512:kc * T + (tb + 1) * 512],
                   kc == 0, kc == C - 1, inc="mmd" if kc == C - 1 else None)
        p.wait("s", "mmd", p.val("mmd"))
        p.op("s", "activation", delta, pA[0:64, 0:2048], ACT.Sigmoid,
             scale=-1.0, bias=ndb64)
        p.op("s", "activation", delta, delta, ACT.Ln, inc="evd")
        p.wait("v", "evd", p.val("evd"))
        p.wait("t", "evd", p.val("evd"))
        p.op("v", "tensor_scalar", abar_l, delta, negA, None, ALU.mult)
        p.op("v", "tensor_scalar", delta, delta, -1.0, None, ALU.mult, inc="dlv")
        p.wait("t", "dlv")
        p.wait("t", "wload")
        for tb in range(NB):
            o = pB[0:64, (tb % 2) * 512:(tb % 2) * 512 + 512]
            if tb >= 2:
                p.wait("t", "evB", p.val("evB") - 1)
            for kc in range(C):
                mm(o, bwt(kc, 0, 64), xc_bf[:, kc * T + tb * 512:kc * T + (tb + 1) * 512],
                   kc == 0, kc == C - 1, inc="mmB" if kc == C - 1 else None)
            p.wait("v", "mmB", p.val("mmB"))
            p.op("v", "scalar_tensor_tensor",
                 AP(bbar_l.tensor, bbar_l.offset + tb * 512, [bbar_l.ap[0], [1, 512]]),
                 o, Bb64,
                 AP(delta.tensor, delta.offset + tb * 512, [delta.ap[0], [1, 512]]),
                 ALU.add, ALU.mult, inc="evB")
        p.wait("g", "evB", p.val("evB"))
        tvv = p.tick("v")
        p.wait_tok("g", tvv)
        gdma(d_sc_in[0], abar_l)
        gdma(d_sc_in[1], bbar_l)
        p.wait("g", "gdma")
        cc("AllGather", [d_sc_in], [d_sc_out])
        gdma(rows64(fullsA, 0, T), d_sc_out[0][0])
        gdma(rows64(fullsA, T, T), d_sc_out[1][0])
        gdma(rows64(fullsB, 0, T), d_sc_out[0][1])
        gdma(rows64(fullsB, T, T), d_sc_out[1][1])
        p.wait("v", "gdma", p.val("gdma"))
        tap("abarf", fullsA, [128, 2 * T], F32, "g", "tapg")
        tap("bbarf", fullsB, [128, 2 * T], F32, "g", "tapg")
        zc = col(1)
        zc64 = AP(zc.tensor, zc.offset, [[zc.ap[0][0], 64], [1, 1]])
        scum = rows64(wks, 0, 2 * T)
        p.op("v", "memset", zc64, 0.0)
        p.op("v", "tensor_tensor_scan", scum, abar_f, bc(zc64, 2 * T), 0.0,
             ALU.add, ALU.add)
        p.op("v", "tensor_scalar", scum, scum, -20.0, None, ALU.max)
        adel = abar_f
        p.op("v", "tensor_copy",
             AP(adel.tensor, adel.offset, [adel.ap[0], [1, 1]]),
             AP(scum.tensor, scum.offset, [scum.ap[0], [1, 1]]))
        p.op("v", "tensor_tensor",
             AP(adel.tensor, adel.offset + 1, [adel.ap[0], [1, 2 * T - 1]]),
             AP(scum.tensor, scum.offset + 1, [scum.ap[0], [1, 2 * T - 1]]),
             AP(scum.tensor, scum.offset, [scum.ap[0], [1, 2 * T - 1]]),
             ALU.subtract, inc="scv")
        p.wait("s", "scv", p.val("scv"))
        p.op("s", "activation", adel, adel, ACT.Exp, inc="sce")
        p.wait("v", "sce", p.val("sce"))
        hfull = scum
        p.op("v", "tensor_tensor_scan", hfull, bbar_f, adel, 0.0, ALU.add, ALU.mult)
        tap("hfull", wks, [128, 2 * T], F32, "v", "tapv")
        hmine_f32cols = dump[:, 0:1024]
        hmine = hmine_f32cols.bitcast(BF16)
        hm64 = AP(hmine.tensor, hmine.offset, [[hmine.ap[0][0], 64], [1, T]])
        sel0 = AP(selc[:, 0:1].tensor, selc[:, 0:1].offset, [[selc[:, 0:1].ap[0][0], 64], [1, 1]])
        sel1 = AP(selc[:, 1:2].tensor, selc[:, 1:2].offset, [[selc[:, 1:2].ap[0][0], 64], [1, 1]])
        tmp64 = rows64(fullsB, 0, T)
        p.op("v", "tensor_scalar", tmp64,
             AP(hfull.tensor, hfull.offset, [hfull.ap[0], [1, T]]), sel0, None, ALU.mult)
        p.op("v", "scalar_tensor_tensor", hm64,
             AP(hfull.tensor, hfull.offset + T, [hfull.ap[0], [1, T]]), sel1, tmp64,
             ALU.mult, ALU.add, inc="hmv")
        p.wait("t", "hmv")
        pool.release()
        wpool.release()

        wpool.mark()
        cwt, cwv = load_weight(E["m_cwt"], 64, D)
        dmt, dmv = load_weight(E["m_dwmt"], D, D)
        gated = pool.alloc(C * T // 2, BF16)
        p.wait("t", "wload")
        for c in range(C):
            if c > 0:
                p.wait("t", "evy", p.val("evy"))
            for tb in range(NB):
                o = pA[:, tb * 512:(tb + 1) * 512]
                mm(o, cwt(0, c * 128, 128),
                   AP(hm64.tensor, hm64.offset + tb * 512, [hm64.ap[0], [1, 512]]),
                   True, False)
                for kc in range(C):
                    mm(o, dmt(kc, c * 128, 128),
                       xc_bf[:, kc * T + tb * 512:kc * T + (tb + 1) * 512],
                       False, kc == C - 1, inc="mmy" if kc == C - 1 else None)
            p.wait("v", "mmy", p.val("mmy"))
            for tb in range(NB):
                p.op("v", "scalar_tensor_tensor",
                     gated[:, c * T + tb * 512:c * T + (tb + 1) * 512],
                     pA[:, tb * 512:(tb + 1) * 512], mCb[:, c:c + 1],
                     sz_bf[:, c * T + tb * 512:c * T + (tb + 1) * 512],
                     ALU.add, ALU.mult, inc="evy")
        p.wait("t", "evy", p.val("evy"))
        tap("gated", gated, [128, C * T], BF16, "v", "tapv")
        wpool.release()
        p.barrier()

        wpool.mark()
        wlt, wlv = load_weight(E["m_wlastt"], D, D)
        p.wait("t", "wload")
        slot_out(lambda kc, j: gated[:, kc * T + j * 128:kc * T + (j + 1) * 128],
                 wlt, C, res_scale=(r1c, nm1c))
        wpool.release()
        pool.release()

    def slot_out(act_tile, wt, kc_n, res_scale=None, res_tok=None, otok_ap=None):
        otok = pool.alloc(D) if otok_ap is None else otok_ap
        sj = col(4)
        for j in range(TT):
            for oh in range(2):
                o = pA[:, oh * 512:oh * 512 + 512]
                for kc in range(kc_n):
                    mm(o, act_tile(kc, j), wt(kc, oh * 512, 512),
                       kc == 0, kc == kc_n - 1,
                       inc="mmo" if kc == kc_n - 1 else None)
            if res_scale is not None:
                r1c, nm1c = res_scale
                p.op("s", "activation", dump[:, 1024:2048], X_t(j), ACT.Identity,
                     scale=r1c[:, j:j + 1], bias=nm1c[:, j:j + 1], inc="res")
            p.wait("v", "mmo", p.val("mmo"))
            p.wait("v", "res", p.val("res"))
            if res_scale is not None:
                p.op("v", "tensor_tensor", otok[:, 0:512], pA[:, 0:512],
                     dump[:, 1024:1536], ALU.add)
                p.op("v", "tensor_tensor", otok[:, 512:1024], pA[:, 512:1024],
                     dump[:, 1536:2048], ALU.add, inc="evo")
            elif res_tok is not None:
                p.op("v", "tensor_tensor", otok[:, 0:512], pA[:, 0:512],
                     res_tok(j)[:, 0:512], ALU.add)
                p.op("v", "tensor_tensor", otok[:, 512:1024], pA[:, 512:1024],
                     res_tok(j)[:, 512:1024], ALU.add, inc="evo")
            else:
                p.op("v", "tensor_copy", otok[:, 0:512], pA[:, 0:512])
                p.op("v", "tensor_copy", otok[:, 512:1024], pA[:, 512:1024],
                     inc="evo")
            p.wait("t", "evo", p.val("evo"))
            p.wait("s", "evo", p.val("evo"))
            ln_add_tile(otok[:], j, sj)



    def slot2():
        ccur[0] = 0
        p.barrier()
        pool.mark()
        reg = pool.alloc(29000)

        def seg(off, n, dtype=F32):
            a = reg[:, off:off + n]
            return a.bitcast(BF16) if dtype == BF16 else a
        SA, SB, SC, SD = 0, 8192, 16384, 18984
        r, nmr, mean, var = ln_cols(X_t, TT, D, 1e-5)
        r2c = col(TT)
        nm2c = col(TT)
        sv = p.tick("s")
        p.wait_tok("v", sv)
        p.op("v", "tensor_copy", r2c, r)
        finish_nmr(r, nmr, mean)
        p.op("v", "tensor_copy", nm2c, nmr, inc="cpv")
        p.wait("s", "cpv")
        xn_fm = seg(SA, 8192, BF16)
        ln_transpose(X_t, r, nmr, xn_fm)
        p.barrier()

        def fmc(c):
            return xn_fm[:, c * T:(c + 1) * T]

        qb = seg(SC, C); kb = seg(SC + 8, 2); vbrow = seg(SC + 12, 256)
        ob = seg(SC + 268, C); nw1 = seg(SC + 276, C); nw2 = seg(SC + 284, C)
        for dst, name in ((qb, "g_qb"), (kb, "g_kb"), (vbrow, "g_vbrow"),
                          (ob, "g_ob"), (nw1, "g_nw1"), (nw2, "g_nw2")):
            ydma(dst, E[name])
        tvc = p.val("ydma")
        for e in "tvs":
            p.wait(e, "ydma", tvc)

        qfm = seg(SB, 8192, BF16)
        kfull = seg(SD, 4096, BF16)
        vfull = seg(SD + 4096, 4160, BF16)
        wpool.mark()
        qwt, qwv = load_weight(E["g_qwt"], D, D)
        p.wait("t", "wload")
        for c in range(C):
            if c > 0:
                p.wait("t", "evq", p.val("evq"))
            for tb in range(NB):
                o = pA[:, tb * 512:(tb + 1) * 512]
                for kc in range(C):
                    mm(o, qwt(kc, c * 128, 128), fmc(kc)[:, tb * 512:(tb + 1) * 512],
                       kc == 0, kc == C - 1, inc="mmq" if kc == C - 1 else None)
            p.wait("s", "mmq", p.val("mmq"))
            for tb in range(NB):
                p.op("s", "activation", qfm[:, c * T + tb * 512:c * T + (tb + 1) * 512],
                     pA[:, tb * 512:(tb + 1) * 512], ACT.Identity,
                     bias=qb[:, c:c + 1], inc="evq")
        p.wait("t", "evq", p.val("evq"))
        wpool.release()
        wpool.mark()
        kwt, kwv = load_weight(E["g_kwt"], D, 256)
        vwt, vwv = load_weight(E["g_vwt"], D, 256)
        p.wait("t", "wload")
        for c in range(2):
            for tb in range(NB):
                o = pA[:, tb * 512:(tb + 1) * 512]
                for kc in range(C):
                    mm(o, kwt(kc, c * 128, 128), fmc(kc)[:, tb * 512:(tb + 1) * 512],
                       kc == 0, kc == C - 1, inc="mmk" if kc == C - 1 else None)
            p.wait("s", "mmk", p.val("mmk"))
            for tb in range(NB):
                p.op("s", "activation",
                     kfull[:, c * 4096 + tb * 512:c * 4096 + (tb + 1) * 512],
                     pA[:, tb * 512:(tb + 1) * 512], ACT.Identity,
                     bias=kb[:, c:c + 1], inc="evk")
            p.wait("t", "evk", p.val("evk"))
        p.op("v", "memset", fview(vfull, 260 * 16 * 0 + 64, [[260, 32], [65, 4], [1, 1]]), 1.0)
        for j in range(TT):
            o = pB[:, (j % 2) * 1024:(j % 2) * 1024 + 256]
            if j >= 2:
                p.wait("t", "evv", p.val("evv") - 1)
            for kc in range(C):
                mm(o, fmc(kc)[:, j * 128:(j + 1) * 128], vwt(kc, 0, 256),
                   kc == 0, kc == C - 1, inc="mmv" if kc == C - 1 else None)
            p.wait("v", "mmv", p.val("mmv"))
            dst = fview(vfull, j * 260, [[65, 4], [1, 64]])
            p.op("v", "scalar_tensor_tensor", dst, o, 1.0, vbrow[:, 0:256],
                 ALU.mult, ALU.add, inc="evv")
        p.wait("t", "evv", p.val("evv"))
        p.wait("g", "evv", p.val("evv"))
        p.wait("g", "evk", p.val("evk"))
        wpool.release()

        tap("s2xn", xn_fm, [128, C * T], BF16, "v")
        tap("s2q", qfm, [128, C * T], BF16, "s")
        gdma(d_kv_in[:, 0:2048], kfull[:, 0:2048])
        gdma(d_kv_in[:, 2048:4096], kfull[:, 4096:4096 + 2048])
        gdma(d_kv_in[:, 4096:4096 + 4160], vfull[:, 0:16 * 260])
        p.wait("g", "gdma")
        cc("AllGather", [d_kv_in], [d_kv_out])
        for rr in range(2):
            gdma(kfull[:, rr * 2048:rr * 2048 + 2048], d_kv_out[rr][:, 0:2048])
            gdma(kfull[:, 4096 + rr * 2048:4096 + rr * 2048 + 2048],
                 d_kv_out[rr][:, 2048:4096])
            gdma(vfull[:, rr * 16 * 260:(rr + 1) * 16 * 260],
                 d_kv_out[rr][:, 4096:4096 + 4160])
        tvkv = p.val("gdma")
        p.wait("t", "gdma", tvkv)
        p.wait("v", "gdma", tvkv)
        p.wait("s", "gdma", tvkv)

        if S2CUT <= 1:
            pool.release()
            return
        tap("s2k", kfull, [128, 2 * 4096], BF16, "g")
        tap("s2v", vfull, [128, 32 * 260], BF16, "g")
        P_sb = seg(SD + 8256, 1024, BF16)
        rcp = dump[:, 0:512]
        bc64 = dump[:, 512:1024]
        for i in range(C):
            kvc = i // 4
            for w in range(2):
                for tkc in range(32):
                    lhA = kfull[0:64, kvc * 4096 + tkc * 128:kvc * 4096 + (tkc + 1) * 128]
                    lhB = kfull[64:128, kvc * 4096 + tkc * 128:kvc * 4096 + (tkc + 1) * 128]
                    for tq in range(2):
                        qA = qfm[0:64, i * T + w * 1024 + tq * 512:i * T + w * 1024 + (tq + 1) * 512]
                        qB = qfm[64:128, i * T + w * 1024 + tq * 512:i * T + w * 1024 + (tq + 1) * 512]
                        mm(pA[:, tq * 512:(tq + 1) * 512], lhA, qA, True, True, tp=(0, 0))
                        mm(pA[:, 1024 + tq * 512:1024 + (tq + 1) * 512], lhB, qB,
                           True, True, tp=(64, 0),
                           inc="mms" if tq == 1 else None)
                    p.wait("s", "mms", p.val("mms"))
                    p.op("s", "activation", P_sb[:, 0:2048], pA[:, 0:2048], ACT.Exp,
                         scale=0.125, inc="mme")
                    p.wait("t", "mme", p.val("mme"))
                    for hh in range(2):
                        kv = kvc * 2 + hh
                        lhv = vfull[:, tkc * 260 + kv * 65:tkc * 260 + kv * 65 + 65]
                        for tq in range(2):
                            mm(pB[0:65, (hh * 2 + tq) * 512:(hh * 2 + tq) * 512 + 512],
                               lhv, P_sb[:, hh * 1024 + tq * 512:hh * 1024 + (tq + 1) * 512],
                               tkc == 0, tkc == 31,
                               inc="mma" if (tkc == 31 and hh == 1 and tq == 1) else None)
                p.wait("v", "mma", p.val("mma"))
                for hh in range(2):
                    for tq in range(2):
                        s0 = (hh * 2 + tq) * 512
                        p.op("v", "reciprocal", rcp[0:1, 0:512], pB[64:65, s0:s0 + 512],
                             inc="rcpv")
                        p.wait("t", "rcpv", p.val("rcpv"))
                        mm(pA[0:64, s0:s0 + 512], onesq_s[0:1, 0:64],
                           rcp[0:1, 0:512], True, True, inc="bcm")
                        p.wait("v", "bcm", p.val("bcm"))
                        p.op("v", "tensor_copy",
                             AP(bc64.tensor, bc64.offset, [[bc64.ap[0][0], 64], [1, 512]]),
                             pA[0:64, s0:s0 + 512])
                        p.op("v", "tensor_tensor",
                             qfm[hh * 64:hh * 64 + 64,
                                 i * T + w * 1024 + tq * 512:i * T + w * 1024 + (tq + 1) * 512],
                             pB[0:64, s0:s0 + 512],
                             AP(bc64.tensor, bc64.offset, [[bc64.ap[0][0], 64], [1, 512]]),
                             ALU.mult, inc="nrm")
                p.wait("t", "nrm", p.val("nrm"))
        if S2CUT <= 2:
            pool.release()
            return
        tap("s2att", qfm, [128, C * T], BF16, "v")
        p.barrier()

        wpool.mark()
        owt, owv = load_weight(E["g_owt"], D, D)
        src_fm = seg(SD, 8192, BF16)
        p.wait("t", "wload")
        for c in range(C):
            if c > 0:
                p.wait("t", "evo2", p.val("evo2"))
            for tb in range(NB):
                o = pA[:, tb * 512:(tb + 1) * 512]
                for kc in range(C):
                    mm(o, owt(kc, c * 128, 128), qfm[:, kc * T + tb * 512:kc * T + (tb + 1) * 512],
                       kc == 0, kc == C - 1, inc="mmo2" if kc == C - 1 else None)
            p.wait("v", "mmo2", p.val("mmo2"))
            for tb in range(NB):
                sl = slice(c * T + tb * 512, c * T + (tb + 1) * 512)
                p.op("v", "scalar_tensor_tensor", src_fm[:, sl],
                     pA[:, tb * 512:(tb + 1) * 512], ob[:, c:c + 1],
                     xn_fm[:, sl], ALU.add, ALU.add, inc="evo2")
        p.wait("t", "evo2", p.val("evo2"))
        p.wait("s", "evo2", p.val("evo2"))
        wpool.release()

        tap("s2src", src_fm, [128, C * T], BF16, "v")
        sqb = dump[:].bitcast(BF16)
        for c in range(C):
            p.op("s", "activation", sqb[:, 0:2048], src_fm[:, c * T:(c + 1) * T],
                 ACT.Square, inc="sq")
            p.wait("t", "sq", p.val("sq"))
            for tb in range(NB):
                mm(pB[0:1, tb * 512:(tb + 1) * 512], onescb[:],
                   sqb[:, tb * 512:(tb + 1) * 512], c == 0, c == C - 1,
                   inc="sqm" if (c == C - 1 and tb == NB - 1) else None)
            if c < C - 1:
                tok = p.tick("t")
                p.wait_tok("s", tok)
        p.wait("v", "sqm", p.val("sqm"))
        rr1_row = dump[0:1, 0:2048]
        p.op("v", "tensor_scalar", rr1_row, pB[0:1, 0:2048], 1.0 / D, None, ALU.mult)
        p.op("v", "tensor_scalar", rr1_row, rr1_row, 1e-6, None, ALU.add)
        p.op("v", "reciprocal", rr1_row, rr1_row, inc="rr1v")
        p.wait("s", "rr1v")
        p.op("s", "activation", rr1_row, rr1_row, ACT.Sqrt, inc="rr1s")
        p.wait("t", "rr1s", p.val("rr1s"))
        rmsbc = seg(SB, 2048)
        for w0 in range(0, 2048, 512):
            mm(pA[:, w0:w0 + 512], onesq_s[0:1, 0:128], rr1_row[0:1, w0:w0 + 512],
               True, True, inc="bc1" if w0 == 1536 else None)
        p.wait("v", "bc1", p.val("bc1"))
        p.op("v", "tensor_copy", rmsbc[:, 0:2048], pA[:, 0:2048], inc="bc1c")
        p.wait("t", "bc1c", p.val("bc1c"))
        p.wait("s", "bc1c", p.val("bc1c"))
        srcn_fm = seg(SA, 8192, BF16)
        for c in range(C):
            for tb in range(NB):
                sl = slice(c * T + tb * 512, c * T + (tb + 1) * 512)
                p.op("v", "scalar_tensor_tensor", srcn_fm[:, sl], src_fm[:, sl],
                     nw1[:, c:c + 1], rmsbc[:, tb * 512:(tb + 1) * 512],
                     ALU.mult, ALU.mult, inc="sn" if (c == C - 1 and tb == NB - 1) else None)
        p.wait("t", "sn", p.val("sn"))
        p.wait("g", "sn", p.val("sn"))

        if S2CUT <= 3:
            pool.release()
            return
        tap("s2srcn", srcn_fm, [128, C * T], BF16, "v")
        gwt, gwv = load_weight(E["g_gatew"], D, 8)
        wpool.mark()
        p.wait("t", "wload")
        for tb in range(NB):
            o = pB[0:8, tb * 8:(tb + 1) * 8] if False else pB[0:8, tb * 512:(tb + 1) * 512]
            for kc in range(C):
                mm(o, gwt(kc, 0, 8), srcn_fm[:, kc * T + tb * 512:kc * T + (tb + 1) * 512],
                   kc == 0, kc == C - 1, inc="mmg" if kc == C - 1 else None)
        p.wait("s", "mmg", p.val("mmg"))
        probs_fm = AP(dump[:].tensor, dump[:].offset, [[2048, 8], [1, 2048]])
        probs_fm8 = AP(probs_fm.tensor, probs_fm.offset, [[probs_fm.ap[0][0], 8], [1, 2048]])
        for tb in range(NB):
            p.op("s", "activation",
                 AP(probs_fm.tensor, probs_fm.offset + tb * 512, [[probs_fm.ap[0][0], 8], [1, 512]]),
                 pB[0:8, tb * 512:(tb + 1) * 512], ACT.Sigmoid,
                 inc="evg")
        p.wait("t", "evg", p.val("evg"))
        ptok = seg(SC + 1308, 128)
        for j in range(TT):
            p.op("t", "transpose", pB[:, (j % 4) * 128:(j % 4) * 128 + 8],
                 AP(probs_fm.tensor, probs_fm.offset + j * 128, [[probs_fm.ap[0][0], 8], [1, 128]]),
                 ident[0:8, 0:8], inc="ptr")
            p.wait("v", "ptr", p.val("ptr"))
            p.op("v", "tensor_copy", ptok[:, j * 8:(j + 1) * 8],
                 pB[:, (j % 4) * 128:(j % 4) * 128 + 8], inc="pev")
            if j >= 3:
                p.wait("t", "pev", p.val("pev") - 3)
        p.wait("t", "pev", p.val("pev"))
        G = seg(SC + 1436, 128)
        mx = col(8); w1n = col(1); w2n = col(1); msk = col(8)
        for j in range(TT):
            pj = ptok[:, j * 8:(j + 1) * 8]
            p.op("v", "max", mx, pj)
            p.op("v", "tensor_tensor", w1n, mx[:, 0:1], mx[:, 1:2], ALU.add)
            p.op("v", "tensor_scalar", w1n, w1n, 1e-6, None, ALU.add)
            p.op("v", "reciprocal", w1n, w1n)
            p.op("v", "tensor_tensor", w2n, mx[:, 1:2], w1n, ALU.mult)
            p.op("v", "tensor_tensor", w1n, mx[:, 0:1], w1n, ALU.mult)
            p.op("v", "tensor_scalar", msk, pj, mx[:, 0:1], None, ALU.is_equal)
            p.op("v", "tensor_scalar", G[:, j * 8:(j + 1) * 8], msk, w1n, None, ALU.mult)
            p.op("v", "tensor_scalar", msk, pj, mx[:, 1:2], None, ALU.is_equal)
            p.op("v", "scalar_tensor_tensor", G[:, j * 8:(j + 1) * 8], msk, w2n,
                 G[:, j * 8:(j + 1) * 8], ALU.mult, ALU.add,
                 inc="gw" if j == TT - 1 else None)
        p.wait("t", "gw", p.val("gw"))
        wpool.release()

        if S2CUT <= 4:
            pool.release()
            return
        tap("s2G", G, [128, 128], F32, "v")
        p.barrier()
        acc = seg(SB, 8192, BF16)
        p.wait("y", "loa", p.val("loa"))
        tvx = ydma(tok_dram(d_xsp), X[:])
        for e_ in "tvs":
            p.wait(e_, "ydma", tvx)
        h1f = X[:].bitcast(BF16)
        gb_sb = seg(SD + 8192, 1824)
        gb_sb = None
        gbuf = seg(SD + 2048, 2048)
        for e in range(8):
            for j in range(TT):
                p.op("t", "transpose", pB[0:1, j * 128:(j + 1) * 128],
                     G[:, j * 8 + e:j * 8 + e + 1], ident[:],
                     inc="gtr" if j == TT - 1 else None)
            p.wait("v", "gtr", p.val("gtr"))
            p.op("v", "tensor_copy", dump[0:1, 0:2048], pB[0:1, 0:2048], inc="gcp")
            p.wait("t", "gcp", p.val("gcp"))
            for w0 in range(0, 2048, 512):
                mm(pA[:, w0:w0 + 512], onesq_s[0:1, 0:128], dump[0:1, w0:w0 + 512],
                   True, True, inc="gbm" if w0 == 1536 else None)
            p.wait("v", "gbm", p.val("gbm"))
            p.op("v", "tensor_copy", gbuf[:, 0:2048], pA[:, 0:2048], inc="gbc")
            p.wait("t", "gbc", p.val("gbc"))
            for ocg in range(4):
                wpool.mark()
                w1s, _ = load_weight(E["g_w1"][e][:, ocg * 512:(ocg + 1) * 512], D, 512)
                p.wait("t", "wload")
                for oci in range(4):
                    oc = ocg * 4 + oci
                    for tb in range(NB):
                        o = pA[:, tb * 512:(tb + 1) * 512]
                        for kc in range(C):
                            mm(o, w1s(kc, oci * 128, 128),
                               srcn_fm[:, kc * T + tb * 512:kc * T + (tb + 1) * 512],
                               kc == 0, kc == C - 1, inc="mh1" if kc == C - 1 else None)
                    p.wait("s", "mh1", p.val("mh1"))
                    for tb in range(NB):
                        p.op("s", "activation",
                             h1f[:, oc * 2048 + tb * 512:oc * 2048 + (tb + 1) * 512],
                             pA[:, tb * 512:(tb + 1) * 512], ACT.Gelu, inc="eh1")
                    p.wait("t", "eh1", p.val("eh1"))
                wpool.release()
            for oc in range(C):
                wpool.mark()
                w2s, _ = load_weight(E["g_w2"][e][:, oc * 128:(oc + 1) * 128], 2048, 128)
                p.wait("t", "wload")
                for tb in range(NB):
                    o = pA[:, tb * 512:(tb + 1) * 512]
                    for hc in range(16):
                        mm(o, w2s(hc, 0, 128),
                           h1f[:, hc * 2048 + tb * 512:hc * 2048 + (tb + 1) * 512],
                           hc == 0, hc == 15, inc="ma" if hc == 15 else None)
                p.wait("v", "ma", p.val("ma"))
                for tb in range(NB):
                    sl = slice(oc * T + tb * 512, oc * T + (tb + 1) * 512)
                    if e == 0:
                        p.op("v", "tensor_tensor", acc[:, sl],
                             pA[:, tb * 512:(tb + 1) * 512],
                             gbuf[:, tb * 512:(tb + 1) * 512], ALU.mult, inc="ea")
                    else:
                        p.op("v", "tensor_tensor", seg(SD, 512)[:, 0:512],
                             pA[:, tb * 512:(tb + 1) * 512],
                             gbuf[:, tb * 512:(tb + 1) * 512], ALU.mult)
                        p.op("v", "tensor_tensor", acc[:, sl], acc[:, sl],
                             seg(SD, 512)[:, 0:512], ALU.add, inc="ea")
                p.wait("t", "ea", p.val("ea"))
                wpool.release()
        p.barrier()

        if S2CUT <= 5:
            tvr5 = ydma(X[:], tok_dram(d_xsp))
            for e_ in "tvsg":
                p.wait(e_, "ydma", tvr5)
            pool.release()
            return
        tap("s2acc", acc, [128, TT * D], BF16, "v")
        for c in range(C):
            p.op("v", "tensor_tensor", acc[:, c * T:(c + 1) * T],
                 acc[:, c * T:(c + 1) * T], srcn_fm[:, c * T:(c + 1) * T], ALU.add,
                 inc="s2" if c == C - 1 else None)
        p.wait("s", "s2", p.val("s2"))
        p.wait("t", "s2", p.val("s2"))
        sqb2 = dump[:].bitcast(BF16)
        for c in range(C):
            p.op("s", "activation", sqb2[:, 0:2048], acc[:, c * T:(c + 1) * T],
                 ACT.Square, inc="sq2")
            p.wait("t", "sq2", p.val("sq2"))
            for tb in range(NB):
                mm(pB[0:1, tb * 512:(tb + 1) * 512], onescb[:],
                   sqb2[:, tb * 512:(tb + 1) * 512], c == 0, c == C - 1,
                   inc="sqm2" if (c == C - 1 and tb == NB - 1) else None)
            if c < C - 1:
                tok = p.tick("t")
                p.wait_tok("s", tok)
        p.wait("v", "sqm2", p.val("sqm2"))
        rr2_row = dump[0:1, 0:2048]
        p.op("v", "tensor_scalar", rr2_row, pB[0:1, 0:2048], 1.0 / D, None, ALU.mult)
        p.op("v", "tensor_scalar", rr2_row, rr2_row, 1e-6, None, ALU.add)
        p.op("v", "reciprocal", rr2_row, rr2_row, inc="rr2v")
        p.wait("s", "rr2v")
        p.op("s", "activation", rr2_row, rr2_row, ACT.Sqrt, inc="rr2s")
        p.wait("t", "rr2s", p.val("rr2s"))
        rms2bc = seg(SD, 2048)
        for w0 in range(0, 2048, 512):
            mm(pA[:, w0:w0 + 512], onesq_s[0:1, 0:128], rr2_row[0:1, w0:w0 + 512],
               True, True, inc="bc2" if w0 == 1536 else None)
        p.wait("v", "bc2", p.val("bc2"))
        p.op("v", "tensor_copy", rms2bc[:, 0:2048], pA[:, 0:2048], inc="bc2c")
        p.wait("t", "bc2c", p.val("bc2c"))
        p.wait("s", "bc2c", p.val("bc2c"))
        out2 = seg(SA, 8192, BF16)
        for c in range(C):
            for tb in range(NB):
                sl = slice(c * T + tb * 512, c * T + (tb + 1) * 512)
                p.op("v", "scalar_tensor_tensor", out2[:, sl], acc[:, sl],
                     nw2[:, c:c + 1], rms2bc[:, tb * 512:(tb + 1) * 512],
                     ALU.mult, ALU.mult,
                     inc="o2" if (c == C - 1 and tb == NB - 1) else None)
        p.wait("t", "o2", p.val("o2"))
        p.wait("g", "o2", p.val("o2"))
        tap("s2out2", out2, [128, C * T], BF16, "v")
        tvr = ydma(X[:], tok_dram(d_xsp))
        for e_ in "tvs":
            p.wait(e_, "ydma", tvr)
        p.barrier()
        if S2CUT <= 6:
            pool.release()
            return
        sj = col(4)
        ot2 = seg(SD + 2048, 1024)
        for j in range(min(TT, S2TAIL)):
            for c in range(C):
                mm_transpose_bf(out2[:, c * T + j * 128:c * T + (j + 1) * 128],
                                pA[:, c * 128:(c + 1) * 128])
                if c == C - 1:
                    tok = p.tick("t")
                    p.wait_tok("v", tok)
            p.op("v", "tensor_copy", ot2, pA[:, 0:1024], inc="oc2")
            p.wait("s", "oc2", p.val("oc2"))
            p.wait("t", "oc2", p.val("oc2"))
            ln_add_tile(ot2, j, sj)
        pool.release()


    def slot3():
        ccur[0] = 0
        p.barrier()
        pool.mark()
        reg = pool.alloc(29000)

        def seg(off, n, dtype=F32):
            a = reg[:, off:off + n]
            return a.bitcast(BF16) if dtype == BF16 else a
        r, nmr, mean, var = ln_cols(X_t, TT, D, 1e-5)
        sv3 = p.tick("s")
        p.wait_tok("v", sv3)
        finish_nmr(r, nmr, mean)
        vs_sync()
        xn_fm = seg(0, 8192, BF16)
        ln_transpose(X_t, r, nmr, xn_fm)
        tap("xn3", xn_fm, [128, 16384], BF16, "v")
        p.wait("y", "loa", p.val("loa"))
        tvx = ydma(tok_dram(d_xsp), X[:])
        for e_ in "tvsg":
            p.wait(e_, "ydma", tvx)
        kfull = X[:].bitcast(BF16)
        qfm = seg(8192, 8192, BF16)
        vloc = seg(16384, 8320, BF16)
        cstile = seg(24704, 1024, BF16)
        P_sb = dump[:, 0:1024].bitcast(BF16)

        def fmc(c):
            return xn_fm[:, c * T:(c + 1) * T]

        wpool.mark()
        qwt, _ = load_weight(E["a_qwt"], D, D)
        p.wait("t", "wload")

        def qk_pass(wt, dst_fm):
            for j in range(TT):
                half = j % 2
                ct = cstile[:, half * 1024:half * 1024 + 512]
                st_ = cstile[:, half * 1024 + 512:half * 1024 + 1024]
                if p.val("rps") > 1:
                    p.wait("y", "rps", p.val("rps") - 1)
                ydma(ct, E["a_cos"][:, j * 512:(j + 1) * 512])
                tvcs = ydma(st_, E["a_sin"][:, j * 512:(j + 1) * 512])
                o = pA[:, (j % 2) * 1024:(j % 2) * 1024 + 1024]
                if j >= 2:
                    p.wait("t", "rtp", p.val("rtp") - 1)
                for kc in range(C):
                    mm(o[:, 0:512], fmc(kc)[:, j * 128:(j + 1) * 128], wt(kc, 0, 512),
                       kc == 0, kc == C - 1)
                    mm(o[:, 512:1024], fmc(kc)[:, j * 128:(j + 1) * 128], wt(kc, 512, 512),
                       kc == 0, kc == C - 1, inc="mmr" if kc == C - 1 else None)
                p.wait("v", "mmr", p.val("mmr"))
                p.wait("v", "ydma", tvcs)
                ev = AP(o.tensor, o.offset, [o.ap[0], [2, 512]])
                od = AP(o.tensor, o.offset + 1, [o.ap[0], [2, 512]])
                cv = ct
                sv_ = st_
                t1 = dump[:, 0:512]
                t2 = dump[:, 512:1024]
                rt_ = dump[:, 1024:2048].bitcast(BF16)
                rte = AP(rt_.tensor, rt_.offset, [rt_.ap[0], [2, 512]])
                rto = AP(rt_.tensor, rt_.offset + 1, [rt_.ap[0], [2, 512]])
                p.op("v", "tensor_tensor", t1, ev, cv, ALU.mult)
                p.op("v", "tensor_tensor", t2, od, sv_, ALU.mult)
                p.op("v", "tensor_tensor", t1, t1, t2, ALU.subtract)
                p.op("v", "tensor_copy", rte, t1)
                p.op("v", "tensor_tensor", t1, od, cv, ALU.mult)
                p.op("v", "tensor_tensor", t2, ev, sv_, ALU.mult)
                p.op("v", "tensor_tensor", t1, t1, t2, ALU.add)
                p.op("v", "tensor_copy", rto, t1, inc="rps")
                p.wait("t", "rps", p.val("rps"))
                half = j % 2
                pdst = pB[:, half * 1024:half * 1024 + 1024]
                if j >= 2:
                    p.wait("t", "rpe", p.val("rpe") - 1)
                for c in range(C):
                    mm(pdst[:, c * 128:(c + 1) * 128],
                       rt_[:, c * 128:(c + 1) * 128], identb[:],
                       True, True, inc="rtp" if c == C - 1 else None)
                p.wait("v", "rtp", p.val("rtp"))
                dst_ap = AP(dst_fm.tensor, dst_fm.offset + j * 128,
                            [dst_fm.ap[0], [dst_fm_w[0], C], [1, 128]])
                p.op("v", "tensor_copy", dst_ap, pdst[:, 0:C * 128], inc="rpe")
            p.wait("t", "rpe", p.val("rpe"))

        dst_fm_w = [T]
        qk_pass(qwt, qfm)
        tap("qfm3", qfm, [128, 16384], BF16, "v")
        wpool.release()
        wpool.mark()
        kwt, _ = load_weight(E["a_kwt"], D, D)
        p.wait("t", "wload")
        dst_fm_w[0] = 4096
        klocal = kfull
        qk_pass(kwt, kfull)
        wpool.release()
        wpool.mark()
        vwt, _ = load_weight(E["a_vwt"], D, D)
        p.wait("t", "wload")
        p.op("v", "memset", fview(vloc, 64, [[1040, 16], [65, 16], [1, 1]]), 1.0)
        for j in range(TT):
            o = pA[:, (j % 2) * 1024:(j % 2) * 1024 + 1024]
            if j >= 2:
                p.wait("t", "evv3", p.val("evv3") - 1)
            for kc in range(C):
                mm(o[:, 0:512], fmc(kc)[:, j * 128:(j + 1) * 128], vwt(kc, 0, 512),
                   kc == 0, kc == C - 1)
                mm(o[:, 512:1024], fmc(kc)[:, j * 128:(j + 1) * 128], vwt(kc, 512, 512),
                   kc == 0, kc == C - 1, inc="mmv3" if kc == C - 1 else None)
            p.wait("v", "mmv3", p.val("mmv3"))
            dst = fview(vloc, j * 1040, [[65, 16], [1, 64]])
            p.op("v", "tensor_copy", dst, o[:, 0:1024], inc="evv3")
        p.wait("t", "evv3", p.val("evv3"))
        wpool.release()
        p.barrier()

        kvm = seg(28752, 4)
        tvm = ydma(kvm, E["a_kvmask"])
        p.wait("v", "ydma", tvm)
        stg = seg(8192, 4096, BF16)
        for rr2 in range(2):
            for hf in range(2):
                mcol = kvm[:, rr2 * 2 + hf:rr2 * 2 + hf + 1]
                for c in range(C):
                    srcap = fview(kfull, c * 4096, [[1, 2048]])
                    dst = dump[:, 0:1024].bitcast(BF16)
                    p.op("v", "tensor_scalar", dst[:, 0:2048], srcap, mcol, None,
                         ALU.mult, inc="msk")
                    p.wait("g", "msk", p.val("msk"))
                    gdma(d_rs_in[rr2][:, hf * KVW + c * 2048:hf * KVW + (c + 1) * 2048],
                         dst[:, 0:2048])
                    tokg = p.tick("g")
                    p.wait_tok("v", tokg)
                off = 0
                while off < 16640:
                    wlen = min(2048, 16640 - off)
                    srcap = fview(vloc, off, [[1, wlen]])
                    dst = dump[:, 0:1024].bitcast(BF16)
                    p.op("v", "tensor_scalar", dst[:, 0:wlen], srcap, mcol, None,
                         ALU.mult, inc="msk")
                    p.wait("g", "msk", p.val("msk"))
                    gdma(d_rs_in[rr2][:, hf * KVW + 16384 + off:hf * KVW + 16384 + off + wlen],
                         dst[:, 0:wlen])
                    tokg = p.tick("g")
                    p.wait_tok("v", tokg)
                    off += wlen
        p.wait("g", "gdma")
        cc("ReduceScatter", [d_rs_in], [d_rs_out], ALU.add)
        vfA = seg(0, 6240, BF16)
        vfB = seg(16384, 10400, BF16)

        def vfh(tile, hd):
            base, off = ((vfA, tile * 1040) if tile < 12 else
                         (vfB, (tile - 12) * 1040))
            return fview(base, off + hd * 65, [[1, 65]])
        for hf in range(2):
            for c in range(C):
                gdma(fview(kfull, c * 4096 + hf * 2048, [[1, 2048]]),
                     d_rs_out[:, hf * KVW + c * 2048:hf * KVW + (c + 1) * 2048])
            if hf == 0:
                gdma(fview(vfA, 0, [[1, 12480]]), d_rs_out[:, 16384:16384 + 12480])
                gdma(fview(vfB, 0, [[1, 4160]]),
                     d_rs_out[:, 16384 + 12480:16384 + 16640])
            else:
                gdma(fview(vfB, 4160, [[1, 16640]]),
                     d_rs_out[:, KVW + 16384:KVW + 16384 + 16640])
        tvkv = p.val("gdma")
        for e_ in "tvs":
            p.wait(e_, "gdma", tvkv)
        tap("kfull3", kfull, [128, 32768], BF16, "g")
        tap("vfA3", vfA, [128, 12480], BF16, "g")
        tap("vfB3", vfB, [128, 20800], BF16, "g")

        poi = seg(28756, 32)
        ydma(poi, E["a_poison"])
        diagm = seg(6240, 1280, BF16)
        tvp = ydma(diagm, E["a_diagm"])
        p.wait("s", "ydma", tvp)
        p.wait("v", "ydma", tvp)
        att_fm = qfm
        rcp3 = dump[:, 1024:1536]
        bc643 = dump[:, 1536:2048]
        for i in range(C):
            for w in range(2):
                cklist = [c_ for c_ in range(32)
                          if not (c_ >= 16 and (c_ - 16) * 128 > w * 1024 + 1023)]
                cklast = cklist[-1]
                for ckg in range(32):
                    own = ckg >= 16
                    if own:
                        krel = (ckg - 16) * 128
                        qlo = w * 1024
                        if krel > qlo + 1023:
                            continue
                    lhA = kfull[0:64, i * 4096 + ckg * 128:i * 4096 + (ckg + 1) * 128]
                    lhB = kfull[64:128, i * 4096 + ckg * 128:i * 4096 + (ckg + 1) * 128]
                    for tq in range(2):
                        qA = qfm[0:64, i * T + w * 1024 + tq * 512:i * T + w * 1024 + (tq + 1) * 512]
                        qB = qfm[64:128, i * T + w * 1024 + tq * 512:i * T + w * 1024 + (tq + 1) * 512]
                        mm(pA[:, tq * 512:(tq + 1) * 512], lhA, qA, True, True, tp=(0, 0))
                        mm(pA[:, 1024 + tq * 512:1024 + (tq + 1) * 512], lhB, qB,
                           True, True, tp=(64, 0), inc="mms3" if tq == 1 else None)
                    p.wait("s", "mms3", p.val("mms3"))
                    p.op("s", "activation", P_sb[:, 0:2048], pA[:, 0:2048], ACT.Exp,
                         scale=0.125, bias=poi[:, ckg:ckg + 1], inc="mme3")
                    need_mask = own and any(
                        krel <= w * 1024 + tq * 512 + 511 and krel + 127 > w * 1024 + tq * 512
                        for tq in range(2))
                    p.wait("t", "mme3", p.val("mme3"))
                    if own:
                        p.wait("v", "mme3", p.val("mme3"))
                        done_any = False
                        for hh in range(2):
                            for tq in range(2):
                                base = w * 1024 + tq * 512 - krel
                                if base >= 128:
                                    continue
                                m4 = min((-base) // 128, 4) if base < 0 else 0
                                sl = P_sb[:, hh * 1024 + tq * 512:hh * 1024 + (tq + 1) * 512]
                                p.op("v", "tensor_tensor", sl, sl,
                                     diagm[:, m4 * 512:(m4 + 1) * 512], ALU.mult,
                                     inc="msk3")
                                done_any = True
                        if done_any:
                            p.wait("t", "msk3", p.val("msk3"))
                            p.wait("s", "msk3", p.val("msk3"))
                    for hh in range(2):
                        h = 2 * i + hh
                        lhv = vfh(ckg, h)
                        for tq in range(2):
                            mm(pB[0:65, (hh * 2 + tq) * 512:(hh * 2 + tq) * 512 + 512],
                               lhv, P_sb[:, hh * 1024 + tq * 512:hh * 1024 + (tq + 1) * 512],
                               ckg == 0, ckg == cklast,
                               inc="mma3" if (ckg == cklast and hh == 1 and tq == 1) else None)
                p.wait("v", "mma3", p.val("mma3"))
                for hh in range(2):
                    for tq in range(2):
                        s0 = (hh * 2 + tq) * 512
                        p.op("v", "reciprocal", rcp3[0:1, 0:512], pB[64:65, s0:s0 + 512],
                             inc="rcp3")
                        p.wait("t", "rcp3", p.val("rcp3"))
                        mm(pA[0:64, s0 % 2048:s0 % 2048 + 512], onesq_s[0:1, 0:64],
                           rcp3[0:1, 0:512], True, True, inc="bcm3")
                        p.wait("v", "bcm3", p.val("bcm3"))
                        p.op("v", "tensor_copy",
                             AP(bc643.tensor, bc643.offset, [[bc643.ap[0][0], 64], [1, 512]]),
                             pA[0:64, s0 % 2048:s0 % 2048 + 512])
                        p.op("v", "tensor_tensor",
                             att_fm[hh * 64:hh * 64 + 64,
                                    i * T + w * 1024 + tq * 512:i * T + w * 1024 + (tq + 1) * 512],
                             pB[0:64, s0:s0 + 512],
                             AP(bc643.tensor, bc643.offset, [[bc643.ap[0][0], 64], [1, 512]]),
                             ALU.mult, inc="nrm3")
                p.wait("t", "nrm3", p.val("nrm3"))
        tap("att3", att_fm, [128, 16384], BF16, "v")
        p.barrier()
        tvr = ydma(X[:], tok_dram(d_xsp))
        for e_ in "tvsg":
            p.wait(e_, "ydma", tvr)
        p.barrier()

        wpool.mark()
        owt, _ = load_weight(E["a_owt"], D, D)
        p.wait("t", "wload")
        slot_out(lambda kc, j: att_fm[:, kc * T + j * 128:kc * T + (j + 1) * 128],
                 owt, C, otok_ap=seg(24704, 1024))
        wpool.release()
        pool.release()

    def slot4():
        ccur[0] = 0
        p.barrier()
        pool.mark()
        reg = pool.alloc(29000)

        def seg(off, n, dtype=F32):
            a = reg[:, off:off + n]
            return a.bitcast(BF16) if dtype == BF16 else a
        r, nmr, mean, var = ln_cols(X_t, TT, D, 1e-5)
        t2 = col(TT)
        qn = col(TT)
        p.op("v", "tensor_scalar", t2, var, 1e-5, None, ALU.add)
        p.op("v", "reciprocal", t2, t2)
        p.op("v", "tensor_tensor", t2, var, t2, ALU.mult)
        p.op("v", "tensor_scalar", t2, t2, float(D), None, ALU.mult, inc="s4a")
        p.wait("s", "s4a", p.val("s4a"))
        p.op("s", "activation", qn, t2, ACT.Sqrt, inc="s4b")
        p.wait("v", "s4b", p.val("s4b"))
        p.op("v", "tensor_scalar", t2, qn, 1e-12, None, ALU.max)
        p.op("v", "reciprocal", t2, t2)
        p.op("v", "tensor_scalar", t2, t2, 32.0, None, ALU.mult)
        rhat = col(TT)
        nmh = col(TT)
        p.op("v", "tensor_tensor", rhat, r, t2, ALU.mult)
        p.op("v", "tensor_tensor", nmh, mean, rhat, ALU.mult)
        p.op("v", "tensor_scalar", nmh, nmh, -1.0, None, ALU.mult)
        finish_nmr(r, nmr, mean)
        vs_sync()
        xhat = seg(0, 8192, BF16)
        ln_transpose(X_t, rhat, nmh, xhat)
        p.barrier()

        def fmx(c):
            return xhat[:, c * T:(c + 1) * T]

        b1l = seg(8192, 4); gl = seg(8200, C); gh = seg(8208, C)
        b1h = seg(8216, 32); b2l = seg(8248, D); b2h = seg(9272, D)
        rtc = seg(10296, C // 2, BF16)
        selr = seg(10300, 2)
        for dst, name in ((b1l, "f_b1l"), (gl, "f_gl"), (gh, "f_gh"),
                          (b1h, "f_b1h"), (b2l, "f_b2lrow"), (b2h, "f_b2hrow"),
                          (selr, "f_selrow")):
            ydma(dst if name != "f_selrow" else selr[0:1, 0:2], E[name])
        ydma(rtc, E["f_rt"])
        tvc4 = p.val("ydma")
        for e_ in "tvsg":
            p.wait(e_, "ydma", tvc4)

        for tb in range(NB):
            for kc in range(C):
                mm(pB[0:1, tb * 512:(tb + 1) * 512],
                   rtc[:, kc:kc + 1].bitcast(BF16) if False else rtc[:, kc:kc + 1],
                   fmx(kc)[:, tb * 512:(tb + 1) * 512],
                   kc == 0, kc == C - 1, inc="mmsr" if kc == C - 1 else None)
        p.wait("v", "mmsr", p.val("mmsr"))
        for j in range(TT):
            p.op("t", "transpose", pA[0:1, j * 128:(j + 1) * 128],
                 qn[:, j:j + 1], ident[:], inc="qtr" if j == TT - 1 else None)
        p.wait("v", "qtr", p.val("qtr"))
        srow = dump[0:1, 0:2048]
        p.op("v", "tensor_scalar", srow, pA[0:1, 0:2048], 1.0 / 32.0, None, ALU.mult)
        p.op("v", "tensor_tensor", srow, srow, pB[0:1, 0:2048], ALU.mult, inc="sr4")
        p.wait("g", "sr4", p.val("sr4"))
        p.wait("t", "sr4", p.val("sr4"))
        gdma(d_s_in[0][0:1, :], srow)
        gdma(d_s_in[1][0:1, :], srow)
        p.wait("g", "gdma")
        cc("AllGather", [d_s_in[0]], [d_s_out])
        sfull = seg(18500, 4096)[0:1, 0:4096]
        tvsf = gdma(sfull, AP(d_s_out.tensor, d_s_out.offset, [[1, 1], [1, 4096]]))
        p.wait("v", "gdma", tvsf)
        p.wait("t", "gdma", tvsf)

        s32 = seg(22596, 32)
        for f in range(32):
            p.op("t", "transpose", pA[:, (f % 16) * 128:(f % 16) * 128 + 1] if False
                 else pA[:, f:f + 1], sfull[0:1, f * 128:(f + 1) * 128], ident[0:1, 0:1],
                 inc="str" if f == 31 else None)
        p.wait("v", "str", p.val("str"))
        p.op("v", "tensor_copy", s32, pA[:, 0:32], inc="sev")
        p.wait("t", "sev", p.val("sev"))
        acol = col(1)
        ecol = col(1)
        LOGK = float(np.log(1152.0))
        LOGN = float(np.log(4096.0))
        p.op("v", "memset", acol, LOGK - LOGN)
        e32 = seg(22628, 32)
        for it in range(49):
            p.op("v", "tensor_scalar", e32, s32, acol, 0.0, ALU.add, op1=ALU.min,
                 inc="cdv")
            p.wait("s", "cdv", p.val("cdv"))
            p.op("s", "activation", e32, e32, ACT.Exp, accum_out=ecol, inc="cds")
            p.wait("t", "cds", p.val("cds"))
            mm(pB[:, 0:1], onesq_s[:], ecol, True, True, inc="cdm")
            p.wait("s", "cdm", p.val("cdm"))
            p.op("s", "activation", ecol, pB[:, 0:1], ACT.Ln, scale=1.0 / 1152.0,
                 inc="cd2")
            p.wait("v", "cd2", p.val("cd2"))
            p.op("v", "tensor_tensor", acol, acol, ecol, ALU.subtract, inc="cd3")
            p.wait("s", "cd3", p.val("cd3"))
            p.wait("t", "cd3", p.val("cd3"))
        arow = col(1)[0:1, 0:1]
        p.op("v", "tensor_scalar", sfull, sfull, acol[0:1, 0:1], 0.0, ALU.add,
             op1=ALU.min, inc="se1")
        p.wait("s", "se1", p.val("se1"))
        p.op("s", "activation", sfull, sfull, ACT.Exp, inc="se2")
        p.wait("v", "se2", p.val("se2"))
        mrow = seg(22660, 4096)[0:1, 0:4096]
        p.op("v", "tensor_scalar", mrow, sfull, 1.0, None, ALU.is_ge)
        csr = seg(18500, 4096)[0:1, 0:4096]
        zc1 = col(1)[0:1, 0:1]
        p.op("v", "memset", zc1, 0.0)
        p.op("v", "tensor_tensor_scan", csr, mrow, bc(zc1, 4096), 0.0,
             ALU.add, ALU.add)
        p.op("v", "tensor_scalar", csr, csr, 1024.0, None, ALU.is_le)
        p.op("v", "tensor_tensor", mrow, mrow, csr, ALU.mult)
        mloc = csr[0:1, 0:2048]
        p.op("v", "tensor_scalar", mloc, mrow[0:1, 0:2048], selr[0:1, 0:1], None,
             ALU.mult)
        p.op("v", "scalar_tensor_tensor", mloc, mrow[0:1, 2048:4096],
             selr[0:1, 1:2], mloc, ALU.mult, ALU.add, inc="se3")
        p.wait("t", "se3", p.val("se3"))
        mcols = col(TT)
        for j in range(TT):
            p.op("t", "transpose", pA[:, j:j + 1], mloc[0:1, j * 128:(j + 1) * 128],
                 ident[0:1, 0:1], inc="mtr" if j == TT - 1 else None)
        p.wait("v", "mtr", p.val("mtr"))
        p.op("v", "tensor_copy", mcols, pA[:, 0:TT], inc="mev")
        p.wait("t", "mev", p.val("mev"))
        p.barrier()

        wpool.mark()
        w1l, _ = load_weight(E["f_w1lt"], D, 512)
        hl = seg(18500, 4096, BF16)
        p.wait("t", "wload")
        p.wait("v", "wload")
        for kc in range(C):
            p.op("v", "tensor_scalar", w1l(kc, 0, 512), w1l(kc, 0, 512),
                 gl[:, kc:kc + 1], None, ALU.mult, inc="glw" if kc == C - 1 else None)
        p.wait("t", "glw", p.val("glw"))
        for oc4 in range(4):
            for tb in range(NB):
                o = pA[:, tb * 512:(tb + 1) * 512]
                for kc in range(C):
                    mm(o, w1l(kc, oc4 * 128, 128), fmx(kc)[:, tb * 512:(tb + 1) * 512],
                       kc == 0, kc == C - 1, inc="ml1" if kc == C - 1 else None)
            p.wait("s", "ml1", p.val("ml1"))
            for tb in range(NB):
                p.op("s", "activation", hl[:, oc4 * 2048 + tb * 512:oc4 * 2048 + (tb + 1) * 512],
                     pA[:, tb * 512:(tb + 1) * 512], ACT.Gelu,
                     bias=b1l[:, oc4 // 2 * 0 + 0:1] if False else b1l[:, oc4:oc4 + 1],
                     inc="el1")
            p.wait("t", "el1", p.val("el1"))
        wpool.release()
        wpool.mark()
        w2l, _ = load_weight(E["f_w2lt"], 512, D)
        light = seg(10308, TT * D // 2, BF16)
        p.wait("t", "wload")
        for j in range(TT):
            ob2 = pB[:, (j % 2) * 1024:(j % 2) * 1024 + 1024]
            if j >= 2:
                p.wait("t", "el2", p.val("el2") - 1)
            for oh in range(2):
                for hc in range(4):
                    mm(ob2[:, oh * 512:oh * 512 + 512],
                       hl[:, hc * 2048 + j * 128:hc * 2048 + j * 128 + 128],
                       w2l(hc, oh * 512, 512), hc == 0, hc == 3,
                       inc="ml2" if (hc == 3 and oh == 1) else None)
            p.wait("v", "ml2", p.val("ml2"))
            p.op("v", "tensor_tensor", light[:, j * D:(j + 1) * D], ob2[:, 0:1024],
                 b2l[:, 0:D], ALU.add, inc="el2")
        p.wait("t", "el2", p.val("el2"))
        wpool.release()
        p.barrier()

        p.wait("y", "loa", p.val("loa"))
        tvx = ydma(tok_dram(d_xsp), X[:])
        for e_ in "tvsg":
            p.wait(e_, "ydma", tvx)
        hh_ = X[:].bitcast(BF16)
        heavy = seg(18500, 8192, BF16)
        for th in range(2):
            for ocg in range(8):
                wpool.mark()
                w1h, _ = load_weight(E["f_w1ht"][:, ocg * 512:(ocg + 1) * 512], D, 512)
                p.wait("t", "wload")
                p.wait("v", "wload")
                for kc in range(C):
                    p.op("v", "tensor_scalar", w1h(kc, 0, 512), w1h(kc, 0, 512),
                         gh[:, kc:kc + 1], None, ALU.mult,
                         inc="ghw" if kc == C - 1 else None)
                p.wait("t", "ghw", p.val("ghw"))
                for oci in range(4):
                    oc = ocg * 4 + oci
                    for tb in range(2):
                        o = pA[:, tb * 512:(tb + 1) * 512]
                        for kc in range(C):
                            mm(o, w1h(kc, oci * 128, 128),
                               fmx(kc)[:, th * 1024 + tb * 512:th * 1024 + (tb + 1) * 512],
                               kc == 0, kc == C - 1, inc="mh4" if kc == C - 1 else None)
                    p.wait("s", "mh4", p.val("mh4"))
                    for tb in range(2):
                        p.op("s", "activation",
                             hh_[:, oc * 1024 + tb * 512:oc * 1024 + (tb + 1) * 512],
                             pA[:, tb * 512:(tb + 1) * 512], ACT.Gelu,
                             bias=b1h[:, oc:oc + 1], inc="eh4")
                    p.wait("t", "eh4", p.val("eh4"))
                wpool.release()
            for oc in range(C):
                wpool.mark()
                w2h, _ = load_weight(E["f_w2ht"][:, oc * 128:(oc + 1) * 128], 4096, 128)
                p.wait("t", "wload")
                for tb in range(2):
                    o = pA[:, tb * 512:(tb + 1) * 512]
                    for hc in range(32):
                        mm(o, w2h(hc, 0, 128),
                           hh_[:, hc * 1024 + tb * 512:hc * 1024 + (tb + 1) * 512],
                           hc == 0, hc == 31, inc="mo4" if hc == 31 else None)
                    p.wait("s", "mo4", p.val("mo4"))
                    p.op("s", "activation",
                         heavy[:, oc * T + th * 1024 + tb * 512:
                               oc * T + th * 1024 + (tb + 1) * 512],
                         pA[:, tb * 512:(tb + 1) * 512], ACT.Identity, inc="eo4")
                p.wait("t", "eo4", p.val("eo4"))
                wpool.release()
        p.barrier()
        tvr = ydma(X[:], tok_dram(d_xsp))
        for e_ in "tvsg":
            p.wait(e_, "ydma", tvr)
        p.barrier()
        sj = col(4)
        otok4 = seg(26756, 1024)
        for j in range(TT):
            for c in range(C):
                mm_transpose_bf(heavy[:, c * T + j * 128:c * T + (j + 1) * 128],
                                pA[:, c * 128:(c + 1) * 128])
                if c == C - 1:
                    tok = p.tick("t")
                    p.wait_tok("v", tok)
            p.op("v", "tensor_tensor", otok4[:, 0:D], pA[:, 0:D], b2h[:, 0:D], ALU.add)
            p.op("v", "tensor_scalar", otok4[:, 0:D], otok4[:, 0:D], mcols[:, j:j + 1],
                 None, ALU.mult)
            p.op("v", "tensor_tensor", otok4[:, 0:D], otok4[:, 0:D],
                 light[:, j * D:(j + 1) * D], ALU.add, inc="cb4")
            p.wait("t", "cb4", p.val("cb4"))
            p.wait("s", "cb4", p.val("cb4"))
            ln_add_tile(otok4[:, 0:D], j, sj)
        pool.release()

    if N_SLOTS >= 1:
        slot1()
    if N_SLOTS >= 2:
        slot2()
    if N_SLOTS >= 3:
        slot3()
    if N_SLOTS >= 4:
        slot4()

    p.barrier()
    for name, ap_ in dbg.items():
        ydma(ap_, X[:])
    ydma(tok_dram(out_x), X[:])
    p.wait("y", "ydma")
    p.emit()
    print(f"[build] pool high {pool.high * 4 / 1024:.0f}KB/p, wpool {wpool.high * 4 / 1024:.0f}KB/p, "
          f"nops: {[len(p.ops[e]) for e in p.ENGS]}, sbuf left {nc.sbuf_bytes_remaining}")
    p.ctx.close()
    return nc


def make_inputs(inputs, core):
    b, h = core // 2, core % 2
    t = lambda a: np.ascontiguousarray(a.T)
    m = {
        "x_in": np.ascontiguousarray(inputs["L0_attn_x"][b, h * T:(h + 1) * T, :]),
        "identity": np.eye(128, dtype=np.float32),
        "identity_bf": np.eye(128, dtype=BF),
        "ones_bf": np.ones((1, 128), BF),
        "ones_cbf": np.ones((128, 1), BF),
        "selcol": np.ascontiguousarray(
            np.broadcast_to(np.array([[1.0 - h, float(h)]], np.float32), (64, 2))),
        "halomask": np.ascontiguousarray(np.broadcast_to(
            np.array([[0.0, 1.0]] if h == 0 else [[0.0, 0.0]], np.float32), (128, 2))),
        "m_w1t": t(inputs["L0_attn_W1"]),
        "m_w2t": t(inputs["L0_attn_W2"]),
        "m_wlastt": t(inputs["L0_attn_Wlast"]),
        "m_dwt": t(inputs["L0_attn_dw"]),
        "m_bwt": t(inputs["L0_attn_Bw"]),
        "m_cwt": t(inputs["L0_attn_Cw"]),
        "m_dwmt": t(inputs["L0_attn_Dw"]),
        "m_convw": np.ascontiguousarray(
            inputs["L0_attn_conv_w"].reshape(C, 128, 3).transpose(1, 0, 2).reshape(128, C * 3)),
        "m_convb": np.ascontiguousarray(inputs["L0_attn_conv_b"].reshape(C, 128).T),
        "m_A": np.ascontiguousarray(inputs["L0_attn_A"].reshape(64, 1)),
        "m_Bb": np.ascontiguousarray(inputs["L0_attn_Bb"].reshape(64, 1)),
        "m_db": np.ascontiguousarray(inputs["L0_attn_db"].reshape(64, 1)),
        "m_Cb": np.ascontiguousarray(inputs["L0_attn_Cb"].reshape(C, 128).T),
        "m_Db": np.ascontiguousarray(inputs["L0_attn_Db"].reshape(C, 128).T),
        "m_normw": np.ascontiguousarray(inputs["L0_attn_norm_w"].reshape(C, 128).T),
    }
    perm = [0, 4, 1, 5, 2, 6, 3, 7, 8, 12, 9, 13, 10, 14, 11, 15]
    qw = inputs["L0_ffn_qw"]
    qcols = np.concatenate([qw[:, 64 * h:64 * h + 64] for h in perm], axis=1)
    m["g_qwt"] = np.ascontiguousarray(qcols)
    qb = inputs["L0_ffn_qb"]
    qbp = np.concatenate([qb[64 * h:64 * h + 64] for h in perm])
    m["g_qb"] = np.ascontiguousarray(qbp.reshape(C, 128).T)
    m["g_kwt"] = np.ascontiguousarray(inputs["L0_ffn_kw"])
    m["g_kb"] = np.ascontiguousarray(inputs["L0_ffn_kb"].reshape(2, 128).T)
    m["g_vwt"] = np.ascontiguousarray(inputs["L0_ffn_vw"])
    m["g_vbrow"] = np.ascontiguousarray(
        np.broadcast_to(inputs["L0_ffn_vb"][None, :], (128, 256)))
    ow = inputs["L0_ffn_ow"]
    orows = np.concatenate([ow[64 * h:64 * h + 64, :] for h in perm], axis=0)
    m["g_owt"] = np.ascontiguousarray(orows)
    m["g_ob"] = np.ascontiguousarray(inputs["L0_ffn_ob"].reshape(C, 128).T)
    m["g_gatew"] = np.ascontiguousarray(inputs["L0_ffn_gate_w"])
    m["g_nw1"] = np.ascontiguousarray(inputs["L0_ffn_norm1_w"].reshape(C, 128).T)
    m["g_nw2"] = np.ascontiguousarray(inputs["L0_ffn_norm2_w"].reshape(C, 128).T)
    m["g_w1"] = np.ascontiguousarray(inputs["L0_ffn_w1"])
    m["g_w2"] = np.ascontiguousarray(inputs["L0_ffn_w2"])
    for nm, key in (("a_qwt", "L1_attn_Wq"), ("a_kwt", "L1_attn_Wk"),
                    ("a_vwt", "L1_attn_Wv")):
        m[nm] = np.ascontiguousarray(
            inputs[key].transpose(1, 0, 2).reshape(D, D))
    m["a_owt"] = np.ascontiguousarray(inputs["L1_attn_Wo"])
    inv = 10000.0 ** (-(np.arange(0, 64, 2, dtype=np.float64) / 64.0))
    pos = (h * T + np.arange(T, dtype=np.float64))
    ang = pos[:, None] * inv[None, :]
    for nm, fn in (("a_cos", np.cos), ("a_sin", np.sin)):
        v3 = fn(ang).astype(np.float32).reshape(TT, 128, 1, 32)
        v3 = np.broadcast_to(v3, (TT, 128, 16, 32))
        m[nm] = np.ascontiguousarray(
            v3.reshape(TT, 128, 512).transpose(1, 0, 2).reshape(128, TT * 512)
        ).astype(BF)
    poi = np.zeros((128, 32), np.float32)
    if h == 0:
        poi[:, 0:16] = -1e30
    m["a_poison"] = poi
    ff = np.arange(512)[None, :]
    pp = np.arange(128)[:, None]
    dm = np.zeros((128, 2560), np.float32)
    for m4 in range(4):
        dm[:, m4 * 512:(m4 + 1) * 512] = (ff - pp - m4 * 128 >= 0)
    m["a_diagm"] = dm.astype(BF)
    kvm = np.array([0.0, 1.0, 1.0, 0.0] if h == 0 else [0.0, 0.0, 0.0, 1.0],
                   np.float32)
    m["a_kvmask"] = np.ascontiguousarray(np.broadcast_to(kvm[None, :], (128, 4)))
    m["f_rt"] = np.ascontiguousarray(
        inputs["L1_ffn_routing_token"].reshape(C, 128).T).astype(BF)
    m["f_w1lt"] = t(inputs["L1_ffn_w1_l"])
    m["f_b1l"] = np.ascontiguousarray(inputs["L1_ffn_b1_l"].reshape(4, 128).T)
    m["f_w2lt"] = t(inputs["L1_ffn_w2_l"])
    m["f_b2lrow"] = np.ascontiguousarray(
        np.broadcast_to(inputs["L1_ffn_b2_l"][None, :], (128, D)))
    m["f_w1ht"] = t(inputs["L1_ffn_w1_h"])
    m["f_b1h"] = np.ascontiguousarray(inputs["L1_ffn_b1_h"].reshape(32, 128).T)
    m["f_w2ht"] = t(inputs["L1_ffn_w2_h"])
    m["f_b2hrow"] = np.ascontiguousarray(
        np.broadcast_to(inputs["L1_ffn_b2_h"][None, :], (128, D)))
    m["f_gl"] = np.ascontiguousarray(inputs["L1_ffn_gamma_l"].reshape(C, 128).T)
    m["f_gh"] = np.ascontiguousarray(inputs["L1_ffn_gamma_h"].reshape(C, 128).T)
    m["f_selrow"] = np.array([[1.0, 0.0]] if h == 0 else [[0.0, 1.0]], np.float32)
    m["ones_sq"] = np.ones((128, 128), np.float32)
    return {k: np.asarray(v) for k, v in m.items()}


_NC_CACHE = {}


def kernel(**inputs):
    key = "prog" + ",".join(TAPS)
    if key not in _NC_CACHE:
        _NC_CACHE[key] = build_program()
    nc = _NC_CACHE[key]
    in_maps = [make_inputs(inputs, c) for c in range(8)]
    res = run_bass_kernel_spmd(nc, in_maps, core_ids=list(range(8)))
    out = np.zeros((4, 4096, 1024), np.float32)
    for c in range(8):
        b, h = c // 2, c % 2
        o = res.results[c]["out"]
        out[b, h * T:(h + 1) * T, :] = o
    if TAPS:
        kernel.tapped = [{k: v for k, v in res.results[c].items() if k.startswith("tap_")}
                         for c in range(8)]
    return out


# version 02
# speedup vs baseline: 1.0036x; 1.0036x over previous
import numpy as np
import ml_dtypes
from contextlib import ExitStack

import concourse.bass as bass
import concourse.mybir as mybir
from concourse.ap import AP
from concourse.bass_utils import run_bass_kernel_spmd
from concourse import library_config

F32 = mybir.dt.float32
BF16 = mybir.dt.bfloat16
ALU = mybir.AluOpType
ACT = mybir.ActivationFunctionType
AX = mybir.AxisListType
BF = ml_dtypes.bfloat16

T = 2048
TT = 16
D = 1024
C = 8
NB = 4
GROUPS = [[0, 1], [2, 3], [4, 5], [6, 7]]
EPS_F32 = float(np.finfo(np.float32).eps)

N_SLOTS = 4
S2CUT = 99
S2TAIL = 99
LNPH = 99
TAPS = []


def bc(col_ap, n):
    return AP(col_ap.tensor, col_ap.offset, [col_ap.ap[0], [0, n]])


def fview(ap_, off, dims):
    return AP(ap_.tensor, ap_.offset + off, [ap_.ap[0]] + dims)


def tok_dram(ap2d, rows=T, width=D):
    return AP(ap2d.tensor, ap2d.offset,
              [[width, 128], [128 * width, rows // 128], [1, width]])


class Prog:
    ENGS = ("t", "v", "s", "g", "y")

    def __init__(self, nc):
        self.nc = nc
        self.ops = {e: [] for e in self.ENGS}
        self.last = {e: None for e in self.ENGS}
        self.ctx = ExitStack()
        self.sems = {}
        self.cnt = {}

    DEDICATED = {"ydma", "gdma", "wload", "ccs", "pbg",
                 "el2", "evv", "evv3", "pev", "rpe", "rps", "rtp", "tpev", "tptr",
                 "bar_t", "bar_v", "bar_s", "bar_g", "bar_y",
                 "fen_t", "fen_v", "fen_s", "fen_g", "fen_y"}

    def sem(self, name):
        if name not in self.cnt:
            self.cnt[name] = 0
        if name in self.DEDICATED:
            key = name
        else:
            import zlib
            key = "sh%02d" % (zlib.crc32(name.encode()) % 40)
        if key not in self.sems:
            self.sems[key] = self.ctx.enter_context(self.nc.semaphore(key))
        return self.sems[key]

    def key(self, name):
        if name in self.DEDICATED:
            return name
        import zlib
        return "sh%02d" % (zlib.crc32(name.encode()) % 40)

    def op(self, e, meth, *args, inc=None, by=1, **kw):
        import sys as _sys
        f = _sys._getframe(1)
        ln = f.f_lineno
        if f.f_code.co_name in ("op", "gdma", "ydma", "mm", "tap"):
            ln = f.f_back.f_lineno
        rec = {"meth": meth, "args": args, "kw": kw, "incs": [], "line": ln}
        tv = None
        if inc is not None:
            sm = self.sem(inc)
            k = self.key(inc)
            self.cnt[k] = self.cnt.get(k, 0) + by
            tv = self.cnt[k]
            rec["incs"].append((sm, by, tv))
        self.ops[e].append(rec)
        self.last[e] = rec
        return tv

    def val(self, name):
        return self.cnt.get(self.key(name), 0)

    def wait(self, e, name, v=None):
        s = self.sem(name)
        tv = self.cnt.get(self.key(name), 0) if v is None else v
        if tv <= 0:
            return
        self.ops[e].append({"wait": (s, tv)})

    DMA_METHS = ("dma_start", "dma_start_transpose", "dma_gather", "dma_scatter_add")

    def tick(self, e, name=None):
        rec = self.last[e]
        if rec is not None and rec["incs"]:
            sm, by, tv = rec["incs"][0]
            return (sm, tv)
        nm = "fen_" + e
        s = self.sem(nm)
        by = 16 if (rec is not None and rec["meth"] in self.DMA_METHS) else 1
        prev = self.cnt.get(nm, 0)
        self.cnt[nm] = prev + by
        tv = self.cnt[nm]
        if rec is not None:
            if e == "g" and prev > 0:
                idx = len(self.ops[e]) - 1 - self.ops[e][::-1].index(rec)
                self.ops[e].insert(idx, {"wait": (s, prev)})
            rec["incs"].append((s, by, tv))
        else:
            if prev > 0:
                self.ops[e].append({"wait": (s, prev)})
            self.ops[e].append({"seminc": (s, 1)})
        return (s, tv)

    def wait_tok(self, e, tok):
        self.ops[e].append({"wait": tok})

    def barrier(self, engs="tvsgy"):
        toks = {e: self.tick(e) for e in engs}
        for e in engs:
            for o in engs:
                if o != e:
                    self.wait_tok(e, toks[o])
        for e in engs:
            self.last[e] = None

    def emit(self):
        with self.nc.Block() as block:
            for attr, key in (("tensor", "t"), ("vector", "v"), ("scalar", "s"),
                              ("gpsimd", "g"), ("sync", "y")):
                ops = self.ops[key]
                autodrain = key in ("v", "s", "g")

                def body(eng, ops=ops, autodrain=autodrain):
                    pending = False
                    for rec in ops:
                        if "wait" in rec:
                            eng.wait_ge(*rec["wait"])
                        elif "seminc" in rec:
                            eng.sem_inc(*rec["seminc"])
                        else:
                            if autodrain and pending and rec["meth"] not in Prog.DMA_METHS:
                                eng.drain()
                            r = getattr(eng, rec["meth"])(*rec["args"], **rec["kw"])
                            for sm, by, _tv in rec["incs"]:
                                r.then_inc(sm, by)
                            if rec["meth"] not in Prog.DMA_METHS:
                                pending = True
                getattr(block, attr)(body)


class Pool:
    def __init__(self, nc, name, cols_f32):
        self.t = nc.alloc_sbuf_tensor(name, [128, cols_f32], F32)
        self.cols = cols_f32
        self.off = 0
        self.marks = []
        self.high = 0

    def alloc(self, cols_f32, dtype=F32):
        cols_f32 = (cols_f32 + 1) & ~1
        off = self.off
        self.off += cols_f32
        self.high = max(self.high, self.off)
        assert self.off <= self.cols, f"pool overflow {self.off} > {self.cols}"
        ap_ = self.t[:, off:off + cols_f32]
        if dtype == BF16:
            ap_ = ap_.bitcast(BF16)
        return ap_

    def mark(self):
        self.marks.append(self.off)

    def release(self):
        self.off = self.marks.pop()


def build_program(dbg_names=()):
    nc = bass.Bass("TRN2", target_bir_lowering=False)
    E = {}

    def inp(name, shape, dtype=F32):
        E[name] = nc.dram_tensor(name, list(shape), dtype, kind="ExternalInput").ap()
        return E[name]

    inp("x_in", (T, D))
    inp("identity", (128, 128))
    inp("identity_bf", (128, 128), BF16)
    inp("ones_bf", (1, 128), BF16)
    inp("ones_cbf", (128, 1), BF16)
    inp("selcol", (64, 2))
    inp("halomask", (128, 2))
    inp("m_w1t", (D, D)); inp("m_w2t", (D, D)); inp("m_wlastt", (D, D))
    inp("m_dwt", (D, 64)); inp("m_bwt", (D, 64)); inp("m_cwt", (64, D))
    inp("m_dwmt", (D, D))
    inp("m_convw", (128, C * 3)); inp("m_convb", (128, C))
    inp("m_A", (64, 1)); inp("m_Bb", (64, 1)); inp("m_db", (64, 1))
    inp("m_Cb", (128, C)); inp("m_Db", (128, C)); inp("m_normw", (128, C))

    inp("g_qwt", (D, D)); inp("g_qb", (128, C))
    inp("g_kwt", (D, 256)); inp("g_kb", (128, 2))
    inp("g_vwt", (D, 256)); inp("g_vbrow", (128, 256))
    inp("g_owt", (D, D)); inp("g_ob", (128, C))
    inp("g_gatew", (D, 8))
    inp("g_nw1", (128, C)); inp("g_nw2", (128, C))
    inp("g_w1", (8, D, 2048)); inp("g_w2", (8, 2048, D))

    inp("a_qwt", (D, D)); inp("a_kwt", (D, D)); inp("a_vwt", (D, D))
    inp("a_owt", (D, D))
    inp("a_cos", (128, TT * 512), BF16); inp("a_sin", (128, TT * 512), BF16)
    inp("a_poison", (128, 32))
    inp("a_diagm", (128, 2560), BF16)
    inp("a_kvmask", (128, 4))
    inp("f_rt", (128, C), BF16)
    inp("f_w1lt", (D, 512)); inp("f_b1l", (128, 4))
    inp("f_w2lt", (512, D)); inp("f_b2lrow", (128, D))
    inp("f_w1ht", (D, 4096)); inp("f_b1h", (128, 32))
    inp("f_w2ht", (4096, D)); inp("f_b2hrow", (128, D))
    inp("f_gl", (128, C)); inp("f_gh", (128, C))
    inp("f_selrow", (1, 2))
    inp("ones_sq", (128, 128))

    out_x = nc.dram_tensor("out", [T, D], F32, kind="ExternalOutput").ap()
    dbg = {}
    for name in dbg_names:
        dbg[name] = nc.dram_tensor("dbg_" + name, [128, TT * D], F32,
                                   kind="ExternalOutput").ap()

    d_h_in = nc.dram_tensor("d_h_in", [2, 128, 16], F32).ap()
    d_h_out = nc.dram_tensor("d_h_out", [128, 16], F32).ap()
    d_sc_in = nc.dram_tensor("d_sc_in", [2, 64, T], F32).ap()
    d_sc_out = nc.dram_tensor("d_sc_out", [2, 2, 64, T], F32).ap()
    d_xsp = nc.dram_tensor("d_xsp", [T, D], F32).ap()
    d_kv_in = nc.dram_tensor("d_kv_in", [128, 4096 + 4160], BF16).ap()
    d_kv_out = nc.dram_tensor("d_kv_out", [2, 128, 4096 + 4160], BF16).ap()
    KVW = 16384 + 16640
    d_rs_in = nc.dram_tensor("d_rs_in", [2, 128, 2 * KVW], BF16).ap()
    d_rs_out = nc.dram_tensor("d_rs_out", [128, 2 * KVW], BF16).ap()
    d_s_in = nc.dram_tensor("d_s_in", [2, 1, T], F32).ap()
    d_s_out = nc.dram_tensor("d_s_out", [2, 1, T], F32).ap()

    p = Prog(nc)

    X = nc.alloc_sbuf_tensor("X", [128, TT * D], F32)
    ident = nc.alloc_sbuf_tensor("ident", [128, 128], F32)
    identb = nc.alloc_sbuf_tensor("identb", [128, 128], BF16)
    onesb = nc.alloc_sbuf_tensor("onesb", [1, 128], BF16)
    onescb = nc.alloc_sbuf_tensor("onescb", [128, 1], BF16)
    onesq_s = nc.alloc_sbuf_tensor("onesq_s", [128, 128], F32)
    pool = Pool(nc, "pool", 29000)
    wpool = Pool(nc, "wpool", 4736)
    cols = nc.alloc_sbuf_tensor("cols", [128, 384], F32)
    dump = nc.alloc_sbuf_tensor("dump", [128, 2048], F32)
    ccur = [0]

    def col(n=1):
        o = ccur[0]
        ccur[0] += n
        assert ccur[0] <= 384
        return cols[:, o:o + n]

    pA = nc.alloc_psum_tensor("pA", [128, 2048], F32)
    pB = nc.alloc_psum_tensor("pB", [128, 2048], F32)

    def X_t(j):
        return X[:, j * D:(j + 1) * D]

    wslot = [0]

    def load_weight(hbm_ap, K, O):
        tok = p.tick("t")
        p.wait_tok("g", tok)
        if K % 128 == 0:
            kc_n = K // 128
            buf = wpool.alloc(kc_n * O // 2, BF16)
            rs = hbm_ap.ap[-2][0]
            src = AP(hbm_ap.tensor, hbm_ap.offset,
                     [[rs, 128], [128 * rs, kc_n], [1, O]])
            tv = p.op("g", "dma_start", inc="wload", by=16, out=buf, in_=src)
            return (lambda kc, o0, ow, b=buf, O=O: b[:, kc * O + o0:kc * O + o0 + ow]), tv
        else:
            buf0 = wpool.alloc(O // 2, BF16)
            buf = AP(buf0.tensor, buf0.offset, [[buf0.ap[0][0], K], [1, O]])
            tv = p.op("g", "dma_start", inc="wload", by=16, out=buf, in_=hbm_ap)
            return (lambda kc, o0, ow, b=buf: b[0:K, o0:o0 + ow]), tv

    def vs_sync():
        a = p.tick("v")
        b = p.tick("s")
        p.wait_tok("s", a)
        p.wait_tok("v", b)

    def ln_cols(srcfn, ntiles, width, eps):
        ssum = col(ntiles)
        qsum = col(ntiles)
        for j in range(ntiles):
            p.op("s", "activation", dump[:, 0:width], srcfn(j), ACT.Identity,
                 accum_out=ssum[:, j:j + 1])
            p.op("v", "scalar_tensor_tensor", dump[:, 1024:1024 + width], srcfn(j), 1.0,
                 srcfn(j), ALU.mult, ALU.mult, accum_out=qsum[:, j:j + 1])
        sv = p.tick("s")
        p.wait_tok("v", sv)
        mean = col(ntiles); var = col(ntiles); r = col(ntiles); nmr = col(ntiles)
        inv = 1.0 / width
        p.op("v", "tensor_scalar", mean, ssum, inv, None, ALU.mult)
        p.op("v", "tensor_scalar", var, qsum, inv, None, ALU.mult)
        p.op("v", "tensor_tensor", nmr, mean, mean, ALU.mult)
        p.op("v", "tensor_tensor", var, var, nmr, ALU.subtract)
        p.op("v", "tensor_scalar", r, var, eps, None, ALU.add)
        p.op("v", "reciprocal", r, r)
        vs_sync()
        p.op("s", "activation", r, r, ACT.Sqrt)
        return r, nmr, mean, var

    def finish_nmr(r, nmr, mean):
        p.op("v", "tensor_tensor", nmr, mean, r, ALU.mult)
        p.op("v", "tensor_scalar", nmr, nmr, -1.0, None, ALU.mult)

    def transpose_tok_fm(src_tile, dst_fm, ntiles=TT, nchunks=C, dst_w=T):
        for j in range(ntiles):
            half = j % 2
            pdst = pT[:, half * 1024:half * 1024 + nchunks * 128]
            if j >= 2:
                p.wait("t", "tpev", p.val("tpev") - 1)
            for c in range(nchunks):
                inc = "tptr" if c == nchunks - 1 else None
                p.op("t", "transpose", pdst[:, c * 128:(c + 1) * 128],
                     src_tile(j, c), ident[:], inc=inc)
            p.wait("v", "tptr", p.val("tptr"))
            dst_ap = fview(dst_fm, j * 128, [[dst_w, nchunks], [1, 128]])
            p.op("v", "tensor_copy", dst_ap, pdst, inc="tpev")
        p.wait("t", "tpev", p.val("tpev"))

    def ln_transpose(srcfn, r, nmr, dst_fm, ntiles=TT, nchunks=C, dst_w=T):
        stage = dump[:, 0:2048]
        for j in range(ntiles):
            st = stage[:, (j % 2) * 1024:(j % 2) * 1024 + 1024]
            if j >= 2:
                p.wait("s", "tptr", p.val("tptr") - 1)
            p.op("s", "activation", st[:, 0:nchunks * 128], srcfn(j), ACT.Identity,
                 scale=r[:, j:j + 1], bias=nmr[:, j:j + 1], inc="lnn")
            p.wait("t", "lnn", p.val("lnn"))
            half = j % 2
            pdst = pB[:, half * 1024:half * 1024 + 1024]
            if j >= 2:
                p.wait("t", "tpev", p.val("tpev") - 1)
            for c in range(nchunks):
                p.op("t", "transpose", pdst[:, c * 128:(c + 1) * 128],
                     st[:, c * 128:(c + 1) * 128], ident[:],
                     inc="tptr" if c == nchunks - 1 else None)
            p.wait("v", "tptr", p.val("tptr"))
            dst_ap = fview(dst_fm, j * 128, [[dst_w, nchunks], [1, 128]])
            p.op("v", "tensor_copy", dst_ap, pdst[:, 0:nchunks * 128], inc="tpev")
        p.wait("t", "tpev", p.val("tpev"))

    def mm_transpose_bf(src_tile_bf, out_psum):
        return mm(out_psum, src_tile_bf, identb[:], True, True)

    def ln_add_tile(src_ap, j, sj):
        if LNPH < 1:
            return
        p.op("v", "tensor_reduce", sj[:, 0:1], src_ap, AX.X, ALU.add, inc="lrd")
        if LNPH < 1.5:
            return
        p.op("s", "activation", dump[:, 1024:2048], src_ap, ACT.Square,
             accum_out=sj[:, 1:2], inc="lsq")
        p.wait("v", "lsq", p.val("lsq"))
        if LNPH < 2:
            return
        p.op("v", "tensor_scalar", sj[:, 0:1], sj[:, 0:1], 1.0 / D, None, ALU.mult)
        p.op("v", "tensor_scalar", sj[:, 1:2], sj[:, 1:2], 1.0 / D, None, ALU.mult)
        p.op("v", "tensor_tensor", sj[:, 2:3], sj[:, 0:1], sj[:, 0:1], ALU.mult)
        p.op("v", "tensor_tensor", sj[:, 1:2], sj[:, 1:2], sj[:, 2:3], ALU.subtract)
        p.op("v", "tensor_scalar", sj[:, 1:2], sj[:, 1:2], 1e-5, None, ALU.add)
        p.op("v", "reciprocal", sj[:, 1:2], sj[:, 1:2])
        p.op("v", "tensor_scalar", sj[:, 0:1], sj[:, 0:1], -1.0, None, ALU.mult,
             inc="lov")
        p.wait("s", "lov", p.val("lov"))
        if LNPH < 3:
            return
        p.op("s", "activation", sj[:, 1:2], sj[:, 1:2], ACT.Sqrt)
        p.op("s", "activation", sj[:, 3:4], sj[:, 0:1], ACT.Identity,
             scale=sj[:, 1:2])
        if LNPH < 4:
            return
        p.op("s", "activation", dump[:, 0:D], src_ap, ACT.Identity,
             scale=sj[:, 1:2], bias=sj[:, 3:4], inc="los")
        p.wait("v", "los", p.val("los"))
        if LNPH < 5:
            return
        p.op("v", "tensor_tensor", X_t(j), X_t(j), dump[:, 0:D], ALU.add, inc="loa")
        p.wait("s", "loa", p.val("loa"))
        p.wait("t", "loa", p.val("loa"))

    def cc(kind, ins, outs, alu=ALU.bypass):
        tv = p.op("g", "collective_compute", kind, alu, inc="ccs",
                  replica_groups=GROUPS, ins=ins, outs=outs)
        p.wait("g", "ccs", tv)

    def gdma(out, in_):
        return p.op("g", "dma_start", inc="gdma", by=16, out=out, in_=in_)

    def ydma(out, in_):
        p.wait("y", "ydma")
        return p.op("y", "dma_start", inc="ydma", by=16, out=out, in_=in_)

    taps = {}

    def tap(name, ap_, shape, dtype, prod_eng, prod_sem=None):
        if name not in TAPS:
            return
        o = nc.dram_tensor("tap_" + name, list(shape), dtype,
                           kind="ExternalOutput").ap()
        tok = p.tick(prod_eng)
        p.wait_tok("y", tok)
        tv = ydma(o, ap_)
        for e in "tvsg":
            p.wait(e, "ydma", tv)

    def mm(out, w, r, start, stop, inc=None, tp=None):
        kw = {}
        if tp is not None:
            kw["tile_position"] = tp
        return p.op("t", "matmul", out, w, r, inc=inc, start=start, stop=stop, **kw)

    ydma(X[:], tok_dram(E["x_in"]))
    ydma(ident[:], E["identity"])
    ydma(identb[:], E["identity_bf"])
    ydma(onesb[:], E["ones_bf"])
    ydma(onescb[:], E["ones_cbf"])
    ydma(onesq_s[:], E["ones_sq"])
    selc = nc.alloc_sbuf_tensor("selc", [64, 2], F32)
    ydma(selc[:], E["selcol"])
    hmask = nc.alloc_sbuf_tensor("hmask", [128, 2], F32)
    ydma(hmask[:], E["halomask"])
    convw = nc.alloc_sbuf_tensor("convw", [128, C * 3], F32)
    convb = nc.alloc_sbuf_tensor("convb", [128, C], F32)
    m_nw = nc.alloc_sbuf_tensor("m_nw", [128, C], F32)
    mCb = nc.alloc_sbuf_tensor("mCb", [128, C], F32)
    mDb = nc.alloc_sbuf_tensor("mDb", [128, C], F32)
    smA = nc.alloc_sbuf_tensor("smA", [64, 4], F32)
    ydma(convw[:], E["m_convw"])
    ydma(convb[:], E["m_convb"])
    ydma(m_nw[:], E["m_normw"])
    ydma(mCb[:], E["m_Cb"])
    ydma(mDb[:], E["m_Db"])
    ydma(smA[:, 0:1], E["m_A"])
    ydma(smA[:, 2:3], E["m_Bb"])
    ydma(smA[:, 3:4], E["m_db"])
    for e in "tvsg":
        p.wait(e, "ydma")
    p.op("s", "activation", smA[:, 1:2], smA[:, 0:1], ACT.Sigmoid, scale=-1.0)
    p.op("s", "activation", smA[:, 1:2], smA[:, 1:2], ACT.Ln)
    sv = p.tick("s")
    p.wait_tok("v", sv)
    p.op("v", "tensor_scalar", smA[:, 1:2], smA[:, 1:2], -1.0, None, ALU.mult)
    p.op("v", "tensor_tensor", mCb[:], mCb[:], mDb[:], ALU.add)
    p.barrier()

    def slot1():
        ccur[0] = 0
        p.barrier()
        pool.mark()
        r, nmr, mean, var = ln_cols(X_t, TT, D, 1e-5)
        t2 = col(TT)
        p.op("v", "tensor_scalar", t2, var, 1e-5, None, ALU.add)
        p.op("v", "reciprocal", t2, t2)
        p.op("v", "tensor_tensor", t2, var, t2, ALU.mult)
        p.op("v", "tensor_scalar", t2, t2, EPS_F32, None, ALU.add)
        p.op("v", "reciprocal", t2, t2)
        vs_sync()
        p.op("s", "activation", t2, t2, ACT.Sqrt)
        sv = p.tick("s")
        p.wait_tok("v", sv)
        r1c = col(TT)
        nm1c = col(TT)
        p.op("v", "tensor_copy", r1c, r)
        p.op("v", "tensor_tensor", nm1c, mean, r1c, ALU.mult)
        p.op("v", "tensor_scalar", nm1c, nm1c, -1.0, None, ALU.mult)
        p.op("v", "tensor_tensor", r, r, t2, ALU.mult)
        finish_nmr(r, nmr, mean)
        tap("cols", cols[:, 0:128], [128, 128], F32, "v", "tapv")
        vs_sync()

        sz_bf = pool.alloc(C * T // 2, BF16)
        xc_bf = pool.alloc(C * T // 2, BF16)
        pool.mark()
        xn_fm = pool.alloc(C * T // 2, BF16)

        def fmc(c):
            return xn_fm[:, c * T:(c + 1) * T]

        ln_transpose(X_t, r, nmr, xn_fm)
        p.barrier()
        for c in range(C):
            p.op("v", "tensor_scalar", fmc(c), fmc(c), m_nw[:, c:c + 1], None,
                 ALU.mult, inc="nwm" if c == C - 1 else None)
        p.wait("g", "nwm")
        p.wait("t", "nwm")
        tap("xn_fm", xn_fm, [128, C * T], BF16, "v", "tapv")
        halos = col(32)
        xnhalo = fview(xn_fm, 2046, [[T, C], [1, 2]])
        for reg in range(2):
            p.op("v", "tensor_scalar", halos[:, reg * 16:reg * 16 + 16], xnhalo,
                 hmask[:, reg:reg + 1], None, ALU.mult,
                 inc="hal" if reg == 1 else None)
        p.wait("g", "hal")
        gdma(d_h_in[0], halos[:, 0:16])
        gdma(d_h_in[1], halos[:, 16:32])
        p.wait("g", "gdma")
        cc("ReduceScatter", [d_h_in], [d_h_out], ALU.add)
        xnh = col(16)
        tvh = gdma(xnh, d_h_out)
        p.wait("v", "gdma", tvh)
        xnh_bf = col(8).bitcast(BF16)
        p.op("v", "tensor_copy", xnh_bf, xnh, inc="halb")
        p.wait("t", "halb")

        wpool.mark()
        w2t, w2v = load_weight(E["m_w2t"], D, D)
        p.wait("t", "wload")
        for c in range(C):
            if c > 0:
                p.wait("t", "evz", p.val("evz"))
            for tb in range(NB):
                o = pA[:, tb * 512:(tb + 1) * 512]
                for kc in range(C):
                    mm(o, w2t(kc, c * 128, 128), fmc(kc)[:, tb * 512:(tb + 1) * 512],
                       kc == 0, kc == C - 1, inc="mmz" if kc == C - 1 else None)
            p.wait("s", "mmz", p.val("mmz"))
            for tb in range(NB):
                p.op("s", "activation", sz_bf[:, c * T + tb * 512:c * T + (tb + 1) * 512],
                     pA[:, tb * 512:(tb + 1) * 512], ACT.Silu, inc="evz")
        p.wait("t", "evz", p.val("evz"))
        tap("sz", sz_bf, [128, C * T], BF16, "s", "taps")
        wpool.release()

        wpool.mark()
        w1t, w1v = load_weight(E["m_w1t"], D, D)
        pool.mark()
        xb = pool.alloc(2050)
        cvt = pool.alloc(2048)
        p.wait("t", "wload")
        for c in range(C):
            for tb in range(NB):
                o = pA[:, tb * 512:(tb + 1) * 512]
                for kc in range(C):
                    mm(o, w1t(kc, c * 128, 128), fmc(kc)[:, tb * 512:(tb + 1) * 512],
                       kc == 0, kc == C - 1, inc="mmb" if kc == C - 1 else None)
            for kc in range(C):
                mm(pB[:, 0:2], w1t(kc, c * 128, 128), xnh_bf[:, kc:kc + 1][:, 0:1] if False else
                   AP(xnh_bf.tensor, xnh_bf.offset + kc * 2, [xnh_bf.ap[0], [1, 2]]),
                   kc == 0, kc == C - 1, inc="mmb" if kc == C - 1 else None)
            p.wait("s", "mmb", p.val("mmb"))
            p.op("s", "activation", xb[:, 0:2], pB[:, 0:2], ACT.Identity)
            p.op("s", "activation", xb[:, 2:2050], pA[:, 0:2048], ACT.Identity,
                 inc="evb")
            p.wait("v", "evb", p.val("evb"))
            p.wait("t", "evb", p.val("evb"))
            p.op("v", "tensor_scalar", cvt[:], xb[:, 0:2048],
                 convw[:, c * 3:c * 3 + 1], None, ALU.mult)
            p.op("v", "scalar_tensor_tensor", cvt[:], xb[:, 1:2049],
                 convw[:, c * 3 + 1:c * 3 + 2], cvt[:], ALU.mult, ALU.add)
            p.op("v", "scalar_tensor_tensor", cvt[:], xb[:, 2:2050],
                 convw[:, c * 3 + 2:c * 3 + 3], cvt[:], ALU.mult, ALU.add,
                 inc="cvv")
            p.wait("s", "cvv", p.val("cvv"))
            p.op("s", "activation", xc_bf[:, c * T:(c + 1) * T], cvt[:], ACT.Silu,
                 bias=convb[:, c:c + 1], inc="evc")
            p.wait("v", "evc", p.val("evc"))
        tap("xc", xc_bf, [128, C * T], BF16, "s", "taps")
        pool.release()
        wpool.release()
        pool.release()
        p.barrier()

        wpool.mark()
        dwt, dwv = load_weight(E["m_dwt"], D, 64)
        bwt, bwv = load_weight(E["m_bwt"], D, 64)
        pool.mark()
        fullsA = pool.alloc(2 * T)
        fullsB = pool.alloc(2 * T)
        wks = pool.alloc(2 * T)

        def rows64(buf, o, n):
            return AP(buf.tensor, buf.offset + o, [[buf.ap[0][0], 64], [1, n]])

        abar_l = rows64(fullsA, 0, T)
        bbar_l = rows64(fullsB, 0, T)
        abar_f = rows64(fullsA, 0, 2 * T)
        bbar_f = rows64(fullsB, 0, 2 * T)
        delta = rows64(dump[:], 0, T)
        negA = rows64(smA[:], 1, 1)
        Bb64 = rows64(smA[:], 2, 1)
        db64 = rows64(smA[:], 3, 1)
        ndb = col(1)
        ndb64 = AP(ndb.tensor, ndb.offset, [[ndb.ap[0][0], 64], [1, 1]])
        p.op("v", "tensor_scalar", ndb64, db64, -1.0, None, ALU.mult, inc="ndbv")
        p.wait("s", "ndbv")
        p.wait("t", "wload")
        for tb in range(NB):
            o = pA[0:64, tb * 512:(tb + 1) * 512]
            for kc in range(C):
                mm(o, dwt(kc, 0, 64), xc_bf[:, kc * T + tb * 512:kc * T + (tb + 1) * 512],
                   kc == 0, kc == C - 1, inc="mmd" if kc == C - 1 else None)
        p.wait("s", "mmd", p.val("mmd"))
        p.op("s", "activation", delta, pA[0:64, 0:2048], ACT.Sigmoid,
             scale=-1.0, bias=ndb64)
        p.op("s", "activation", delta, delta, ACT.Ln, inc="evd")
        p.wait("v", "evd", p.val("evd"))
        p.wait("t", "evd", p.val("evd"))
        p.op("v", "tensor_scalar", abar_l, delta, negA, None, ALU.mult)
        p.op("v", "tensor_scalar", delta, delta, -1.0, None, ALU.mult, inc="dlv")
        p.wait("t", "dlv")
        p.wait("t", "wload")
        for tb in range(NB):
            o = pB[0:64, (tb % 2) * 512:(tb % 2) * 512 + 512]
            if tb >= 2:
                p.wait("t", "evB", p.val("evB") - 1)
            for kc in range(C):
                mm(o, bwt(kc, 0, 64), xc_bf[:, kc * T + tb * 512:kc * T + (tb + 1) * 512],
                   kc == 0, kc == C - 1, inc="mmB" if kc == C - 1 else None)
            p.wait("v", "mmB", p.val("mmB"))
            p.op("v", "scalar_tensor_tensor",
                 AP(bbar_l.tensor, bbar_l.offset + tb * 512, [bbar_l.ap[0], [1, 512]]),
                 o, Bb64,
                 AP(delta.tensor, delta.offset + tb * 512, [delta.ap[0], [1, 512]]),
                 ALU.add, ALU.mult, inc="evB")
        p.wait("g", "evB", p.val("evB"))
        tvv = p.tick("v")
        p.wait_tok("g", tvv)
        gdma(d_sc_in[0], abar_l)
        gdma(d_sc_in[1], bbar_l)
        p.wait("g", "gdma")
        cc("AllGather", [d_sc_in], [d_sc_out])
        gdma(rows64(fullsA, 0, T), d_sc_out[0][0])
        gdma(rows64(fullsA, T, T), d_sc_out[1][0])
        gdma(rows64(fullsB, 0, T), d_sc_out[0][1])
        gdma(rows64(fullsB, T, T), d_sc_out[1][1])
        p.wait("v", "gdma", p.val("gdma"))
        tap("abarf", fullsA, [128, 2 * T], F32, "g", "tapg")
        tap("bbarf", fullsB, [128, 2 * T], F32, "g", "tapg")
        zc = col(1)
        zc64 = AP(zc.tensor, zc.offset, [[zc.ap[0][0], 64], [1, 1]])
        scum = rows64(wks, 0, 2 * T)
        p.op("v", "memset", zc64, 0.0)
        p.op("v", "tensor_tensor_scan", scum, abar_f, bc(zc64, 2 * T), 0.0,
             ALU.add, ALU.add)
        p.op("v", "tensor_scalar", scum, scum, -20.0, None, ALU.max)
        adel = abar_f
        p.op("v", "tensor_copy",
             AP(adel.tensor, adel.offset, [adel.ap[0], [1, 1]]),
             AP(scum.tensor, scum.offset, [scum.ap[0], [1, 1]]))
        p.op("v", "tensor_tensor",
             AP(adel.tensor, adel.offset + 1, [adel.ap[0], [1, 2 * T - 1]]),
             AP(scum.tensor, scum.offset + 1, [scum.ap[0], [1, 2 * T - 1]]),
             AP(scum.tensor, scum.offset, [scum.ap[0], [1, 2 * T - 1]]),
             ALU.subtract, inc="scv")
        p.wait("s", "scv", p.val("scv"))
        p.op("s", "activation", adel, adel, ACT.Exp, inc="sce")
        p.wait("v", "sce", p.val("sce"))
        hfull = scum
        p.op("v", "tensor_tensor_scan", hfull, bbar_f, adel, 0.0, ALU.add, ALU.mult)
        tap("hfull", wks, [128, 2 * T], F32, "v", "tapv")
        hmine_f32cols = dump[:, 0:1024]
        hmine = hmine_f32cols.bitcast(BF16)
        hm64 = AP(hmine.tensor, hmine.offset, [[hmine.ap[0][0], 64], [1, T]])
        sel0 = AP(selc[:, 0:1].tensor, selc[:, 0:1].offset, [[selc[:, 0:1].ap[0][0], 64], [1, 1]])
        sel1 = AP(selc[:, 1:2].tensor, selc[:, 1:2].offset, [[selc[:, 1:2].ap[0][0], 64], [1, 1]])
        tmp64 = rows64(fullsB, 0, T)
        p.op("v", "tensor_scalar", tmp64,
             AP(hfull.tensor, hfull.offset, [hfull.ap[0], [1, T]]), sel0, None, ALU.mult)
        p.op("v", "scalar_tensor_tensor", hm64,
             AP(hfull.tensor, hfull.offset + T, [hfull.ap[0], [1, T]]), sel1, tmp64,
             ALU.mult, ALU.add, inc="hmv")
        p.wait("t", "hmv")
        pool.release()
        wpool.release()

        wpool.mark()
        cwt, cwv = load_weight(E["m_cwt"], 64, D)
        dmt, dmv = load_weight(E["m_dwmt"], D, D)
        gated = pool.alloc(C * T // 2, BF16)
        p.wait("t", "wload")
        for c in range(C):
            if c > 0:
                p.wait("t", "evy", p.val("evy"))
            for tb in range(NB):
                o = pA[:, tb * 512:(tb + 1) * 512]
                mm(o, cwt(0, c * 128, 128),
                   AP(hm64.tensor, hm64.offset + tb * 512, [hm64.ap[0], [1, 512]]),
                   True, False)
                for kc in range(C):
                    mm(o, dmt(kc, c * 128, 128),
                       xc_bf[:, kc * T + tb * 512:kc * T + (tb + 1) * 512],
                       False, kc == C - 1, inc="mmy" if kc == C - 1 else None)
            p.wait("v", "mmy", p.val("mmy"))
            for tb in range(NB):
                p.op("v", "scalar_tensor_tensor",
                     gated[:, c * T + tb * 512:c * T + (tb + 1) * 512],
                     pA[:, tb * 512:(tb + 1) * 512], mCb[:, c:c + 1],
                     sz_bf[:, c * T + tb * 512:c * T + (tb + 1) * 512],
                     ALU.add, ALU.mult, inc="evy")
        p.wait("t", "evy", p.val("evy"))
        tap("gated", gated, [128, C * T], BF16, "v", "tapv")
        wpool.release()
        p.barrier()

        wpool.mark()
        wlt, wlv = load_weight(E["m_wlastt"], D, D)
        p.wait("t", "wload")
        slot_out(lambda kc, j: gated[:, kc * T + j * 128:kc * T + (j + 1) * 128],
                 wlt, C, res_scale=(r1c, nm1c))
        wpool.release()
        pool.release()

    def slot_out(act_tile, wt, kc_n, res_scale=None, res_tok=None, otok_ap=None):
        otok = pool.alloc(D) if otok_ap is None else otok_ap
        sj = col(4)
        for j in range(TT):
            for oh in range(2):
                o = pA[:, oh * 512:oh * 512 + 512]
                for kc in range(kc_n):
                    mm(o, act_tile(kc, j), wt(kc, oh * 512, 512),
                       kc == 0, kc == kc_n - 1,
                       inc="mmo" if kc == kc_n - 1 else None)
            if res_scale is not None:
                r1c, nm1c = res_scale
                p.op("s", "activation", dump[:, 1024:2048], X_t(j), ACT.Identity,
                     scale=r1c[:, j:j + 1], bias=nm1c[:, j:j + 1], inc="res")
            p.wait("v", "mmo", p.val("mmo"))
            p.wait("v", "res", p.val("res"))
            if res_scale is not None:
                p.op("v", "tensor_tensor", otok[:, 0:512], pA[:, 0:512],
                     dump[:, 1024:1536], ALU.add)
                p.op("v", "tensor_tensor", otok[:, 512:1024], pA[:, 512:1024],
                     dump[:, 1536:2048], ALU.add, inc="evo")
            elif res_tok is not None:
                p.op("v", "tensor_tensor", otok[:, 0:512], pA[:, 0:512],
                     res_tok(j)[:, 0:512], ALU.add)
                p.op("v", "tensor_tensor", otok[:, 512:1024], pA[:, 512:1024],
                     res_tok(j)[:, 512:1024], ALU.add, inc="evo")
            else:
                p.op("v", "tensor_copy", otok[:, 0:512], pA[:, 0:512])
                p.op("v", "tensor_copy", otok[:, 512:1024], pA[:, 512:1024],
                     inc="evo")
            p.wait("t", "evo", p.val("evo"))
            p.wait("s", "evo", p.val("evo"))
            ln_add_tile(otok[:], j, sj)



    def slot2():
        ccur[0] = 0
        p.barrier()
        pool.mark()
        reg = pool.alloc(29000)

        def seg(off, n, dtype=F32):
            a = reg[:, off:off + n]
            return a.bitcast(BF16) if dtype == BF16 else a
        SA, SB, SC, SD = 0, 8192, 16384, 18984
        r, nmr, mean, var = ln_cols(X_t, TT, D, 1e-5)
        r2c = col(TT)
        nm2c = col(TT)
        sv = p.tick("s")
        p.wait_tok("v", sv)
        p.op("v", "tensor_copy", r2c, r)
        finish_nmr(r, nmr, mean)
        p.op("v", "tensor_copy", nm2c, nmr, inc="cpv")
        p.wait("s", "cpv")
        xn_fm = seg(SA, 8192, BF16)
        ln_transpose(X_t, r, nmr, xn_fm)
        p.barrier()

        def fmc(c):
            return xn_fm[:, c * T:(c + 1) * T]

        qb = seg(SC, C); kb = seg(SC + 8, 2); vbrow = seg(SC + 12, 256)
        ob = seg(SC + 268, C); nw1 = seg(SC + 276, C); nw2 = seg(SC + 284, C)
        for dst, name in ((qb, "g_qb"), (kb, "g_kb"), (vbrow, "g_vbrow"),
                          (ob, "g_ob"), (nw1, "g_nw1"), (nw2, "g_nw2")):
            ydma(dst, E[name])
        tvc = p.val("ydma")
        for e in "tvs":
            p.wait(e, "ydma", tvc)

        qfm = seg(SB, 8192, BF16)
        kfull = seg(SD, 4096, BF16)
        vfull = seg(SD + 4096, 4160, BF16)
        wpool.mark()
        qwt, qwv = load_weight(E["g_qwt"], D, D)
        p.wait("t", "wload")
        for c in range(C):
            if c > 0:
                p.wait("t", "evq", p.val("evq"))
            for tb in range(NB):
                o = pA[:, tb * 512:(tb + 1) * 512]
                for kc in range(C):
                    mm(o, qwt(kc, c * 128, 128), fmc(kc)[:, tb * 512:(tb + 1) * 512],
                       kc == 0, kc == C - 1, inc="mmq" if kc == C - 1 else None)
            p.wait("s", "mmq", p.val("mmq"))
            for tb in range(NB):
                p.op("s", "activation", qfm[:, c * T + tb * 512:c * T + (tb + 1) * 512],
                     pA[:, tb * 512:(tb + 1) * 512], ACT.Identity,
                     bias=qb[:, c:c + 1], inc="evq")
        p.wait("t", "evq", p.val("evq"))
        wpool.release()
        wpool.mark()
        kwt, kwv = load_weight(E["g_kwt"], D, 256)
        vwt, vwv = load_weight(E["g_vwt"], D, 256)
        p.wait("t", "wload")
        for c in range(2):
            for tb in range(NB):
                o = pA[:, tb * 512:(tb + 1) * 512]
                for kc in range(C):
                    mm(o, kwt(kc, c * 128, 128), fmc(kc)[:, tb * 512:(tb + 1) * 512],
                       kc == 0, kc == C - 1, inc="mmk" if kc == C - 1 else None)
            p.wait("s", "mmk", p.val("mmk"))
            for tb in range(NB):
                p.op("s", "activation",
                     kfull[:, c * 4096 + tb * 512:c * 4096 + (tb + 1) * 512],
                     pA[:, tb * 512:(tb + 1) * 512], ACT.Identity,
                     bias=kb[:, c:c + 1], inc="evk")
            p.wait("t", "evk", p.val("evk"))
        p.op("v", "memset", fview(vfull, 260 * 16 * 0 + 64, [[260, 32], [65, 4], [1, 1]]), 1.0)
        for j in range(TT):
            o = pB[:, (j % 2) * 1024:(j % 2) * 1024 + 256]
            if j >= 2:
                p.wait("t", "evv", p.val("evv") - 1)
            for kc in range(C):
                mm(o, fmc(kc)[:, j * 128:(j + 1) * 128], vwt(kc, 0, 256),
                   kc == 0, kc == C - 1, inc="mmv" if kc == C - 1 else None)
            p.wait("v", "mmv", p.val("mmv"))
            dst = fview(vfull, j * 260, [[65, 4], [1, 64]])
            p.op("v", "scalar_tensor_tensor", dst, o, 1.0, vbrow[:, 0:256],
                 ALU.mult, ALU.add, inc="evv")
        p.wait("t", "evv", p.val("evv"))
        p.wait("g", "evv", p.val("evv"))
        p.wait("g", "evk", p.val("evk"))
        wpool.release()

        tap("s2xn", xn_fm, [128, C * T], BF16, "v")
        tap("s2q", qfm, [128, C * T], BF16, "s")
        gdma(d_kv_in[:, 0:2048], kfull[:, 0:2048])
        gdma(d_kv_in[:, 2048:4096], kfull[:, 4096:4096 + 2048])
        gdma(d_kv_in[:, 4096:4096 + 4160], vfull[:, 0:16 * 260])
        p.wait("g", "gdma")
        cc("AllGather", [d_kv_in], [d_kv_out])
        for rr in range(2):
            gdma(kfull[:, rr * 2048:rr * 2048 + 2048], d_kv_out[rr][:, 0:2048])
            gdma(kfull[:, 4096 + rr * 2048:4096 + rr * 2048 + 2048],
                 d_kv_out[rr][:, 2048:4096])
            gdma(vfull[:, rr * 16 * 260:(rr + 1) * 16 * 260],
                 d_kv_out[rr][:, 4096:4096 + 4160])
        tvkv = p.val("gdma")
        p.wait("t", "gdma", tvkv)
        p.wait("v", "gdma", tvkv)
        p.wait("s", "gdma", tvkv)

        if S2CUT <= 1:
            pool.release()
            return
        tap("s2k", kfull, [128, 2 * 4096], BF16, "g")
        tap("s2v", vfull, [128, 32 * 260], BF16, "g")
        P_sb = seg(SD + 8256, 1024, BF16)
        rcp = dump[:, 0:512]
        bc64 = dump[:, 512:1024]
        for i in range(C):
            kvc = i // 4
            for w in range(2):
                for tkc in range(32):
                    lhA = kfull[0:64, kvc * 4096 + tkc * 128:kvc * 4096 + (tkc + 1) * 128]
                    lhB = kfull[64:128, kvc * 4096 + tkc * 128:kvc * 4096 + (tkc + 1) * 128]
                    for tq in range(2):
                        qA = qfm[0:64, i * T + w * 1024 + tq * 512:i * T + w * 1024 + (tq + 1) * 512]
                        qB = qfm[64:128, i * T + w * 1024 + tq * 512:i * T + w * 1024 + (tq + 1) * 512]
                        mm(pA[:, tq * 512:(tq + 1) * 512], lhA, qA, True, True, tp=(0, 0))
                        mm(pA[:, 1024 + tq * 512:1024 + (tq + 1) * 512], lhB, qB,
                           True, True, tp=(64, 0),
                           inc="mms" if tq == 1 else None)
                    p.wait("s", "mms", p.val("mms"))
                    p.op("s", "activation", P_sb[:, 0:2048], pA[:, 0:2048], ACT.Exp,
                         scale=0.125, inc="mme")
                    p.wait("t", "mme", p.val("mme"))
                    for hh in range(2):
                        kv = kvc * 2 + hh
                        lhv = vfull[:, tkc * 260 + kv * 65:tkc * 260 + kv * 65 + 65]
                        for tq in range(2):
                            mm(pB[0:65, (hh * 2 + tq) * 512:(hh * 2 + tq) * 512 + 512],
                               lhv, P_sb[:, hh * 1024 + tq * 512:hh * 1024 + (tq + 1) * 512],
                               tkc == 0, tkc == 31,
                               inc="mma" if (tkc == 31 and hh == 1 and tq == 1) else None)
                p.wait("v", "mma", p.val("mma"))
                for hh in range(2):
                    for tq in range(2):
                        s0 = (hh * 2 + tq) * 512
                        p.op("v", "reciprocal", rcp[0:1, 0:512], pB[64:65, s0:s0 + 512],
                             inc="rcpv")
                        p.wait("t", "rcpv", p.val("rcpv"))
                        mm(pA[0:64, s0:s0 + 512], onesq_s[0:1, 0:64],
                           rcp[0:1, 0:512], True, True, inc="bcm")
                        p.wait("v", "bcm", p.val("bcm"))
                        p.op("v", "tensor_copy",
                             AP(bc64.tensor, bc64.offset, [[bc64.ap[0][0], 64], [1, 512]]),
                             pA[0:64, s0:s0 + 512])
                        p.op("v", "tensor_tensor",
                             qfm[hh * 64:hh * 64 + 64,
                                 i * T + w * 1024 + tq * 512:i * T + w * 1024 + (tq + 1) * 512],
                             pB[0:64, s0:s0 + 512],
                             AP(bc64.tensor, bc64.offset, [[bc64.ap[0][0], 64], [1, 512]]),
                             ALU.mult, inc="nrm")
                p.wait("t", "nrm", p.val("nrm"))
        if S2CUT <= 2:
            pool.release()
            return
        tap("s2att", qfm, [128, C * T], BF16, "v")
        p.barrier()

        wpool.mark()
        owt, owv = load_weight(E["g_owt"], D, D)
        src_fm = seg(SD, 8192, BF16)
        p.wait("t", "wload")
        for c in range(C):
            if c > 0:
                p.wait("t", "evo2", p.val("evo2"))
            for tb in range(NB):
                o = pA[:, tb * 512:(tb + 1) * 512]
                for kc in range(C):
                    mm(o, owt(kc, c * 128, 128), qfm[:, kc * T + tb * 512:kc * T + (tb + 1) * 512],
                       kc == 0, kc == C - 1, inc="mmo2" if kc == C - 1 else None)
            p.wait("v", "mmo2", p.val("mmo2"))
            for tb in range(NB):
                sl = slice(c * T + tb * 512, c * T + (tb + 1) * 512)
                p.op("v", "scalar_tensor_tensor", src_fm[:, sl],
                     pA[:, tb * 512:(tb + 1) * 512], ob[:, c:c + 1],
                     xn_fm[:, sl], ALU.add, ALU.add, inc="evo2")
        p.wait("t", "evo2", p.val("evo2"))
        p.wait("s", "evo2", p.val("evo2"))
        wpool.release()

        tap("s2src", src_fm, [128, C * T], BF16, "v")
        sqb = dump[:].bitcast(BF16)
        for c in range(C):
            p.op("s", "activation", sqb[:, 0:2048], src_fm[:, c * T:(c + 1) * T],
                 ACT.Square, inc="sq")
            p.wait("t", "sq", p.val("sq"))
            for tb in range(NB):
                mm(pB[0:1, tb * 512:(tb + 1) * 512], onescb[:],
                   sqb[:, tb * 512:(tb + 1) * 512], c == 0, c == C - 1,
                   inc="sqm" if (c == C - 1 and tb == NB - 1) else None)
            if c < C - 1:
                tok = p.tick("t")
                p.wait_tok("s", tok)
        p.wait("v", "sqm", p.val("sqm"))
        rr1_row = dump[0:1, 0:2048]
        p.op("v", "tensor_scalar", rr1_row, pB[0:1, 0:2048], 1.0 / D, None, ALU.mult)
        p.op("v", "tensor_scalar", rr1_row, rr1_row, 1e-6, None, ALU.add)
        p.op("v", "reciprocal", rr1_row, rr1_row, inc="rr1v")
        p.wait("s", "rr1v")
        p.op("s", "activation", rr1_row, rr1_row, ACT.Sqrt, inc="rr1s")
        p.wait("t", "rr1s", p.val("rr1s"))
        rmsbc = seg(SB, 2048)
        for w0 in range(0, 2048, 512):
            mm(pA[:, w0:w0 + 512], onesq_s[0:1, 0:128], rr1_row[0:1, w0:w0 + 512],
               True, True, inc="bc1" if w0 == 1536 else None)
        p.wait("v", "bc1", p.val("bc1"))
        p.op("v", "tensor_copy", rmsbc[:, 0:2048], pA[:, 0:2048], inc="bc1c")
        p.wait("t", "bc1c", p.val("bc1c"))
        p.wait("s", "bc1c", p.val("bc1c"))
        srcn_fm = seg(SA, 8192, BF16)
        for c in range(C):
            for tb in range(NB):
                sl = slice(c * T + tb * 512, c * T + (tb + 1) * 512)
                p.op("v", "scalar_tensor_tensor", srcn_fm[:, sl], src_fm[:, sl],
                     nw1[:, c:c + 1], rmsbc[:, tb * 512:(tb + 1) * 512],
                     ALU.mult, ALU.mult, inc="sn" if (c == C - 1 and tb == NB - 1) else None)
        p.wait("t", "sn", p.val("sn"))
        p.wait("g", "sn", p.val("sn"))

        if S2CUT <= 3:
            pool.release()
            return
        tap("s2srcn", srcn_fm, [128, C * T], BF16, "v")
        gwt, gwv = load_weight(E["g_gatew"], D, 8)
        wpool.mark()
        p.wait("t", "wload")
        for tb in range(NB):
            o = pB[0:8, tb * 8:(tb + 1) * 8] if False else pB[0:8, tb * 512:(tb + 1) * 512]
            for kc in range(C):
                mm(o, gwt(kc, 0, 8), srcn_fm[:, kc * T + tb * 512:kc * T + (tb + 1) * 512],
                   kc == 0, kc == C - 1, inc="mmg" if kc == C - 1 else None)
        p.wait("s", "mmg", p.val("mmg"))
        probs_fm = AP(dump[:].tensor, dump[:].offset, [[2048, 8], [1, 2048]])
        probs_fm8 = AP(probs_fm.tensor, probs_fm.offset, [[probs_fm.ap[0][0], 8], [1, 2048]])
        for tb in range(NB):
            p.op("s", "activation",
                 AP(probs_fm.tensor, probs_fm.offset + tb * 512, [[probs_fm.ap[0][0], 8], [1, 512]]),
                 pB[0:8, tb * 512:(tb + 1) * 512], ACT.Sigmoid,
                 inc="evg")
        p.wait("t", "evg", p.val("evg"))
        ptok = seg(SC + 1308, 128)
        for j in range(TT):
            p.op("t", "transpose", pB[:, (j % 4) * 128:(j % 4) * 128 + 8],
                 AP(probs_fm.tensor, probs_fm.offset + j * 128, [[probs_fm.ap[0][0], 8], [1, 128]]),
                 ident[0:8, 0:8], inc="ptr")
            p.wait("v", "ptr", p.val("ptr"))
            p.op("v", "tensor_copy", ptok[:, j * 8:(j + 1) * 8],
                 pB[:, (j % 4) * 128:(j % 4) * 128 + 8], inc="pev")
            if j >= 3:
                p.wait("t", "pev", p.val("pev") - 3)
        p.wait("t", "pev", p.val("pev"))
        G = seg(SC + 1436, 128)
        mx = col(8); w1n = col(1); w2n = col(1); msk = col(8)
        for j in range(TT):
            pj = ptok[:, j * 8:(j + 1) * 8]
            p.op("v", "max", mx, pj)
            p.op("v", "tensor_tensor", w1n, mx[:, 0:1], mx[:, 1:2], ALU.add)
            p.op("v", "tensor_scalar", w1n, w1n, 1e-6, None, ALU.add)
            p.op("v", "reciprocal", w1n, w1n)
            p.op("v", "tensor_tensor", w2n, mx[:, 1:2], w1n, ALU.mult)
            p.op("v", "tensor_tensor", w1n, mx[:, 0:1], w1n, ALU.mult)
            p.op("v", "tensor_scalar", msk, pj, mx[:, 0:1], None, ALU.is_equal)
            p.op("v", "tensor_scalar", G[:, j * 8:(j + 1) * 8], msk, w1n, None, ALU.mult)
            p.op("v", "tensor_scalar", msk, pj, mx[:, 1:2], None, ALU.is_equal)
            p.op("v", "scalar_tensor_tensor", G[:, j * 8:(j + 1) * 8], msk, w2n,
                 G[:, j * 8:(j + 1) * 8], ALU.mult, ALU.add,
                 inc="gw" if j == TT - 1 else None)
        p.wait("t", "gw", p.val("gw"))
        wpool.release()

        if S2CUT <= 4:
            pool.release()
            return
        tap("s2G", G, [128, 128], F32, "v")
        p.barrier()
        acc = seg(SB, 8192, BF16)
        p.wait("y", "loa", p.val("loa"))
        tvx = ydma(tok_dram(d_xsp), X[:])
        for e_ in "tvs":
            p.wait(e_, "ydma", tvx)
        h1f = X[:].bitcast(BF16)
        gb_sb = seg(SD + 8192, 1824)
        gb_sb = None
        gbuf = seg(SD + 2048, 2048)
        wpool.mark()
        wbufs = [wpool.alloc(2048, BF16), wpool.alloc(2048, BF16)]

        def load_into(buf, hbm_ap, K, O):
            tok = p.tick("t")
            p.wait_tok("g", tok)
            p.wait("g", "wload")
            kc_n = K // 128
            rs = hbm_ap.ap[-2][0]
            srcw = AP(hbm_ap.tensor, hbm_ap.offset,
                      [[rs, 128], [128 * rs, kc_n], [1, O]])
            tv = p.op("g", "dma_start", inc="wload", by=16,
                      out=buf[:, 0:kc_n * O], in_=srcw)
            return (lambda kc, o0, ow, b=buf, O=O: b[:, kc * O + o0:kc * O + o0 + ow]), tv

        specs = []
        for e in range(8):
            for ocg in range(4):
                specs.append(("w1", e, ocg))
            for oc in range(C):
                specs.append(("w2", e, oc))

        def issue(sp, buf):
            kind, e_, i_ = sp
            if kind == "w1":
                return load_into(buf, E["g_w1"][e_][:, i_ * 512:(i_ + 1) * 512], D, 512)
            return load_into(buf, E["g_w2"][e_][:, i_ * 128:(i_ + 1) * 128], 2048, 128)

        pend = issue(specs[0], wbufs[0])
        for idx, sp in enumerate(specs):
            kind, e, i_ = sp
            if kind == "w1" and i_ == 0:
                for j in range(TT):
                    p.op("t", "transpose", pB[0:1, j * 128:(j + 1) * 128],
                         G[:, j * 8 + e:j * 8 + e + 1], ident[:],
                         inc="gtr" if j == TT - 1 else None)
                p.wait("v", "gtr", p.val("gtr"))
                p.op("v", "tensor_copy", dump[0:1, 0:2048], pB[0:1, 0:2048], inc="gcp")
                p.wait("t", "gcp", p.val("gcp"))
                for w0 in range(0, 2048, 512):
                    mm(pA[:, w0:w0 + 512], onesq_s[0:1, 0:128], dump[0:1, w0:w0 + 512],
                       True, True, inc="gbm" if w0 == 1536 else None)
                p.wait("v", "gbm", p.val("gbm"))
                p.op("v", "tensor_copy", gbuf[:, 0:2048], pA[:, 0:2048], inc="gbc")
                p.wait("t", "gbc", p.val("gbc"))
            wfn, tvw = pend
            if idx + 1 < len(specs):
                pend = issue(specs[idx + 1], wbufs[(idx + 1) % 2])
            p.wait("t", "wload", tvw)
            if kind == "w1":
                ocg = i_
                for oci in range(4):
                    oc = ocg * 4 + oci
                    for tb in range(NB):
                        o = pA[:, tb * 512:(tb + 1) * 512]
                        for kc in range(C):
                            mm(o, wfn(kc, oci * 128, 128),
                               srcn_fm[:, kc * T + tb * 512:kc * T + (tb + 1) * 512],
                               kc == 0, kc == C - 1, inc="mh1" if kc == C - 1 else None)
                    p.wait("s", "mh1", p.val("mh1"))
                    for tb in range(NB):
                        p.op("s", "activation",
                             h1f[:, oc * 2048 + tb * 512:oc * 2048 + (tb + 1) * 512],
                             pA[:, tb * 512:(tb + 1) * 512], ACT.Gelu, inc="eh1")
                    p.wait("t", "eh1", p.val("eh1"))
            else:
                oc = i_
                for tb in range(NB):
                    o = pA[:, tb * 512:(tb + 1) * 512]
                    for hc in range(16):
                        mm(o, wfn(hc, 0, 128),
                           h1f[:, hc * 2048 + tb * 512:hc * 2048 + (tb + 1) * 512],
                           hc == 0, hc == 15, inc="ma" if hc == 15 else None)
                p.wait("v", "ma", p.val("ma"))
                for tb in range(NB):
                    sl = slice(oc * T + tb * 512, oc * T + (tb + 1) * 512)
                    if e == 0:
                        p.op("v", "tensor_tensor", acc[:, sl],
                             pA[:, tb * 512:(tb + 1) * 512],
                             gbuf[:, tb * 512:(tb + 1) * 512], ALU.mult, inc="ea")
                    else:
                        p.op("v", "tensor_tensor", seg(SD, 512)[:, 0:512],
                             pA[:, tb * 512:(tb + 1) * 512],
                             gbuf[:, tb * 512:(tb + 1) * 512], ALU.mult)
                        p.op("v", "tensor_tensor", acc[:, sl], acc[:, sl],
                             seg(SD, 512)[:, 0:512], ALU.add, inc="ea")
                p.wait("t", "ea", p.val("ea"))
        wpool.release()
        p.barrier()

        if S2CUT <= 5:
            tvr5 = ydma(X[:], tok_dram(d_xsp))
            for e_ in "tvsg":
                p.wait(e_, "ydma", tvr5)
            pool.release()
            return
        tap("s2acc", acc, [128, TT * D], BF16, "v")
        for c in range(C):
            p.op("v", "tensor_tensor", acc[:, c * T:(c + 1) * T],
                 acc[:, c * T:(c + 1) * T], srcn_fm[:, c * T:(c + 1) * T], ALU.add,
                 inc="s2" if c == C - 1 else None)
        p.wait("s", "s2", p.val("s2"))
        p.wait("t", "s2", p.val("s2"))
        sqb2 = dump[:].bitcast(BF16)
        for c in range(C):
            p.op("s", "activation", sqb2[:, 0:2048], acc[:, c * T:(c + 1) * T],
                 ACT.Square, inc="sq2")
            p.wait("t", "sq2", p.val("sq2"))
            for tb in range(NB):
                mm(pB[0:1, tb * 512:(tb + 1) * 512], onescb[:],
                   sqb2[:, tb * 512:(tb + 1) * 512], c == 0, c == C - 1,
                   inc="sqm2" if (c == C - 1 and tb == NB - 1) else None)
            if c < C - 1:
                tok = p.tick("t")
                p.wait_tok("s", tok)
        p.wait("v", "sqm2", p.val("sqm2"))
        rr2_row = dump[0:1, 0:2048]
        p.op("v", "tensor_scalar", rr2_row, pB[0:1, 0:2048], 1.0 / D, None, ALU.mult)
        p.op("v", "tensor_scalar", rr2_row, rr2_row, 1e-6, None, ALU.add)
        p.op("v", "reciprocal", rr2_row, rr2_row, inc="rr2v")
        p.wait("s", "rr2v")
        p.op("s", "activation", rr2_row, rr2_row, ACT.Sqrt, inc="rr2s")
        p.wait("t", "rr2s", p.val("rr2s"))
        rms2bc = seg(SD, 2048)
        for w0 in range(0, 2048, 512):
            mm(pA[:, w0:w0 + 512], onesq_s[0:1, 0:128], rr2_row[0:1, w0:w0 + 512],
               True, True, inc="bc2" if w0 == 1536 else None)
        p.wait("v", "bc2", p.val("bc2"))
        p.op("v", "tensor_copy", rms2bc[:, 0:2048], pA[:, 0:2048], inc="bc2c")
        p.wait("t", "bc2c", p.val("bc2c"))
        p.wait("s", "bc2c", p.val("bc2c"))
        out2 = seg(SA, 8192, BF16)
        for c in range(C):
            for tb in range(NB):
                sl = slice(c * T + tb * 512, c * T + (tb + 1) * 512)
                p.op("v", "scalar_tensor_tensor", out2[:, sl], acc[:, sl],
                     nw2[:, c:c + 1], rms2bc[:, tb * 512:(tb + 1) * 512],
                     ALU.mult, ALU.mult,
                     inc="o2" if (c == C - 1 and tb == NB - 1) else None)
        p.wait("t", "o2", p.val("o2"))
        p.wait("g", "o2", p.val("o2"))
        tap("s2out2", out2, [128, C * T], BF16, "v")
        tvr = ydma(X[:], tok_dram(d_xsp))
        for e_ in "tvs":
            p.wait(e_, "ydma", tvr)
        p.barrier()
        if S2CUT <= 6:
            pool.release()
            return
        sj = col(4)
        ot2 = seg(SD + 2048, 1024)
        for j in range(min(TT, S2TAIL)):
            for c in range(C):
                mm_transpose_bf(out2[:, c * T + j * 128:c * T + (j + 1) * 128],
                                pA[:, c * 128:(c + 1) * 128])
                if c == C - 1:
                    tok = p.tick("t")
                    p.wait_tok("v", tok)
            p.op("v", "tensor_copy", ot2, pA[:, 0:1024], inc="oc2")
            p.wait("s", "oc2", p.val("oc2"))
            p.wait("t", "oc2", p.val("oc2"))
            ln_add_tile(ot2, j, sj)
        pool.release()


    def slot3():
        ccur[0] = 0
        p.barrier()
        pool.mark()
        reg = pool.alloc(29000)

        def seg(off, n, dtype=F32):
            a = reg[:, off:off + n]
            return a.bitcast(BF16) if dtype == BF16 else a
        r, nmr, mean, var = ln_cols(X_t, TT, D, 1e-5)
        sv3 = p.tick("s")
        p.wait_tok("v", sv3)
        finish_nmr(r, nmr, mean)
        vs_sync()
        xn_fm = seg(0, 8192, BF16)
        ln_transpose(X_t, r, nmr, xn_fm)
        tap("xn3", xn_fm, [128, 16384], BF16, "v")
        p.wait("y", "loa", p.val("loa"))
        tvx = ydma(tok_dram(d_xsp), X[:])
        for e_ in "tvsg":
            p.wait(e_, "ydma", tvx)
        kfull = X[:].bitcast(BF16)
        qfm = seg(8192, 8192, BF16)
        vloc = seg(16384, 8320, BF16)
        cstile = seg(24704, 1024, BF16)
        P_sb = dump[:, 0:1024].bitcast(BF16)

        def fmc(c):
            return xn_fm[:, c * T:(c + 1) * T]

        wpool.mark()
        qwt, _ = load_weight(E["a_qwt"], D, D)
        p.wait("t", "wload")

        def qk_pass(wt, dst_fm):
            for j in range(TT):
                half = j % 2
                ct = cstile[:, half * 1024:half * 1024 + 512]
                st_ = cstile[:, half * 1024 + 512:half * 1024 + 1024]
                if p.val("rps") > 1:
                    p.wait("y", "rps", p.val("rps") - 1)
                ydma(ct, E["a_cos"][:, j * 512:(j + 1) * 512])
                tvcs = ydma(st_, E["a_sin"][:, j * 512:(j + 1) * 512])
                o = pA[:, (j % 2) * 1024:(j % 2) * 1024 + 1024]
                if j >= 2:
                    p.wait("t", "rtp", p.val("rtp") - 1)
                for kc in range(C):
                    mm(o[:, 0:512], fmc(kc)[:, j * 128:(j + 1) * 128], wt(kc, 0, 512),
                       kc == 0, kc == C - 1)
                    mm(o[:, 512:1024], fmc(kc)[:, j * 128:(j + 1) * 128], wt(kc, 512, 512),
                       kc == 0, kc == C - 1, inc="mmr" if kc == C - 1 else None)
                p.wait("v", "mmr", p.val("mmr"))
                p.wait("v", "ydma", tvcs)
                ev = AP(o.tensor, o.offset, [o.ap[0], [2, 512]])
                od = AP(o.tensor, o.offset + 1, [o.ap[0], [2, 512]])
                cv = ct
                sv_ = st_
                t1 = dump[:, 0:512]
                t2 = dump[:, 512:1024]
                rt_ = dump[:, 1024:2048].bitcast(BF16)
                rte = AP(rt_.tensor, rt_.offset, [rt_.ap[0], [2, 512]])
                rto = AP(rt_.tensor, rt_.offset + 1, [rt_.ap[0], [2, 512]])
                p.op("v", "tensor_tensor", t1, ev, cv, ALU.mult)
                p.op("v", "tensor_tensor", t2, od, sv_, ALU.mult)
                p.op("v", "tensor_tensor", t1, t1, t2, ALU.subtract)
                p.op("v", "tensor_copy", rte, t1)
                p.op("v", "tensor_tensor", t1, od, cv, ALU.mult)
                p.op("v", "tensor_tensor", t2, ev, sv_, ALU.mult)
                p.op("v", "tensor_tensor", t1, t1, t2, ALU.add)
                p.op("v", "tensor_copy", rto, t1, inc="rps")
                p.wait("t", "rps", p.val("rps"))
                half = j % 2
                pdst = pB[:, half * 1024:half * 1024 + 1024]
                if j >= 2:
                    p.wait("t", "rpe", p.val("rpe") - 1)
                for c in range(C):
                    mm(pdst[:, c * 128:(c + 1) * 128],
                       rt_[:, c * 128:(c + 1) * 128], identb[:],
                       True, True, inc="rtp" if c == C - 1 else None)
                p.wait("v", "rtp", p.val("rtp"))
                dst_ap = AP(dst_fm.tensor, dst_fm.offset + j * 128,
                            [dst_fm.ap[0], [dst_fm_w[0], C], [1, 128]])
                p.op("v", "tensor_copy", dst_ap, pdst[:, 0:C * 128], inc="rpe")
            p.wait("t", "rpe", p.val("rpe"))

        dst_fm_w = [T]
        qk_pass(qwt, qfm)
        tap("qfm3", qfm, [128, 16384], BF16, "v")
        wpool.release()
        wpool.mark()
        kwt, _ = load_weight(E["a_kwt"], D, D)
        p.wait("t", "wload")
        dst_fm_w[0] = 4096
        klocal = kfull
        qk_pass(kwt, kfull)
        wpool.release()
        wpool.mark()
        vwt, _ = load_weight(E["a_vwt"], D, D)
        p.wait("t", "wload")
        p.op("v", "memset", fview(vloc, 64, [[1040, 16], [65, 16], [1, 1]]), 1.0)
        for j in range(TT):
            o = pA[:, (j % 2) * 1024:(j % 2) * 1024 + 1024]
            if j >= 2:
                p.wait("t", "evv3", p.val("evv3") - 1)
            for kc in range(C):
                mm(o[:, 0:512], fmc(kc)[:, j * 128:(j + 1) * 128], vwt(kc, 0, 512),
                   kc == 0, kc == C - 1)
                mm(o[:, 512:1024], fmc(kc)[:, j * 128:(j + 1) * 128], vwt(kc, 512, 512),
                   kc == 0, kc == C - 1, inc="mmv3" if kc == C - 1 else None)
            p.wait("v", "mmv3", p.val("mmv3"))
            dst = fview(vloc, j * 1040, [[65, 16], [1, 64]])
            p.op("v", "tensor_copy", dst, o[:, 0:1024], inc="evv3")
        p.wait("t", "evv3", p.val("evv3"))
        wpool.release()
        p.barrier()

        kvm = seg(28752, 4)
        tvm = ydma(kvm, E["a_kvmask"])
        p.wait("v", "ydma", tvm)
        stg = seg(8192, 4096, BF16)
        for rr2 in range(2):
            for hf in range(2):
                mcol = kvm[:, rr2 * 2 + hf:rr2 * 2 + hf + 1]
                for c in range(C):
                    srcap = fview(kfull, c * 4096, [[1, 2048]])
                    dst = dump[:, 0:1024].bitcast(BF16)
                    p.op("v", "tensor_scalar", dst[:, 0:2048], srcap, mcol, None,
                         ALU.mult, inc="msk")
                    p.wait("g", "msk", p.val("msk"))
                    gdma(d_rs_in[rr2][:, hf * KVW + c * 2048:hf * KVW + (c + 1) * 2048],
                         dst[:, 0:2048])
                    tokg = p.tick("g")
                    p.wait_tok("v", tokg)
                off = 0
                while off < 16640:
                    wlen = min(2048, 16640 - off)
                    srcap = fview(vloc, off, [[1, wlen]])
                    dst = dump[:, 0:1024].bitcast(BF16)
                    p.op("v", "tensor_scalar", dst[:, 0:wlen], srcap, mcol, None,
                         ALU.mult, inc="msk")
                    p.wait("g", "msk", p.val("msk"))
                    gdma(d_rs_in[rr2][:, hf * KVW + 16384 + off:hf * KVW + 16384 + off + wlen],
                         dst[:, 0:wlen])
                    tokg = p.tick("g")
                    p.wait_tok("v", tokg)
                    off += wlen
        p.wait("g", "gdma")
        cc("ReduceScatter", [d_rs_in], [d_rs_out], ALU.add)
        vfA = seg(0, 6240, BF16)
        vfB = seg(16384, 10400, BF16)

        def vfh(tile, hd):
            base, off = ((vfA, tile * 1040) if tile < 12 else
                         (vfB, (tile - 12) * 1040))
            return fview(base, off + hd * 65, [[1, 65]])
        for hf in range(2):
            for c in range(C):
                gdma(fview(kfull, c * 4096 + hf * 2048, [[1, 2048]]),
                     d_rs_out[:, hf * KVW + c * 2048:hf * KVW + (c + 1) * 2048])
            if hf == 0:
                gdma(fview(vfA, 0, [[1, 12480]]), d_rs_out[:, 16384:16384 + 12480])
                gdma(fview(vfB, 0, [[1, 4160]]),
                     d_rs_out[:, 16384 + 12480:16384 + 16640])
            else:
                gdma(fview(vfB, 4160, [[1, 16640]]),
                     d_rs_out[:, KVW + 16384:KVW + 16384 + 16640])
        tvkv = p.val("gdma")
        for e_ in "tvs":
            p.wait(e_, "gdma", tvkv)
        tap("kfull3", kfull, [128, 32768], BF16, "g")
        tap("vfA3", vfA, [128, 12480], BF16, "g")
        tap("vfB3", vfB, [128, 20800], BF16, "g")

        poi = seg(28756, 32)
        ydma(poi, E["a_poison"])
        diagm = seg(6240, 1280, BF16)
        tvp = ydma(diagm, E["a_diagm"])
        p.wait("s", "ydma", tvp)
        p.wait("v", "ydma", tvp)
        att_fm = qfm
        rcp3 = dump[:, 1024:1536]
        bc643 = dump[:, 1536:2048]
        for i in range(C):
            for w in range(2):
                cklist = [c_ for c_ in range(32)
                          if not (c_ >= 16 and (c_ - 16) * 128 > w * 1024 + 1023)]
                cklast = cklist[-1]
                for ckg in range(32):
                    own = ckg >= 16
                    if own:
                        krel = (ckg - 16) * 128
                        qlo = w * 1024
                        if krel > qlo + 1023:
                            continue
                    lhA = kfull[0:64, i * 4096 + ckg * 128:i * 4096 + (ckg + 1) * 128]
                    lhB = kfull[64:128, i * 4096 + ckg * 128:i * 4096 + (ckg + 1) * 128]
                    for tq in range(2):
                        qA = qfm[0:64, i * T + w * 1024 + tq * 512:i * T + w * 1024 + (tq + 1) * 512]
                        qB = qfm[64:128, i * T + w * 1024 + tq * 512:i * T + w * 1024 + (tq + 1) * 512]
                        mm(pA[:, tq * 512:(tq + 1) * 512], lhA, qA, True, True, tp=(0, 0))
                        mm(pA[:, 1024 + tq * 512:1024 + (tq + 1) * 512], lhB, qB,
                           True, True, tp=(64, 0), inc="mms3" if tq == 1 else None)
                    p.wait("s", "mms3", p.val("mms3"))
                    p.op("s", "activation", P_sb[:, 0:2048], pA[:, 0:2048], ACT.Exp,
                         scale=0.125, bias=poi[:, ckg:ckg + 1], inc="mme3")
                    need_mask = own and any(
                        krel <= w * 1024 + tq * 512 + 511 and krel + 127 > w * 1024 + tq * 512
                        for tq in range(2))
                    p.wait("t", "mme3", p.val("mme3"))
                    if own:
                        p.wait("v", "mme3", p.val("mme3"))
                        done_any = False
                        for hh in range(2):
                            for tq in range(2):
                                base = w * 1024 + tq * 512 - krel
                                if base >= 128:
                                    continue
                                m4 = min((-base) // 128, 4) if base < 0 else 0
                                sl = P_sb[:, hh * 1024 + tq * 512:hh * 1024 + (tq + 1) * 512]
                                p.op("v", "tensor_tensor", sl, sl,
                                     diagm[:, m4 * 512:(m4 + 1) * 512], ALU.mult,
                                     inc="msk3")
                                done_any = True
                        if done_any:
                            p.wait("t", "msk3", p.val("msk3"))
                            p.wait("s", "msk3", p.val("msk3"))
                    for hh in range(2):
                        h = 2 * i + hh
                        lhv = vfh(ckg, h)
                        for tq in range(2):
                            mm(pB[0:65, (hh * 2 + tq) * 512:(hh * 2 + tq) * 512 + 512],
                               lhv, P_sb[:, hh * 1024 + tq * 512:hh * 1024 + (tq + 1) * 512],
                               ckg == 0, ckg == cklast,
                               inc="mma3" if (ckg == cklast and hh == 1 and tq == 1) else None)
                p.wait("v", "mma3", p.val("mma3"))
                for hh in range(2):
                    for tq in range(2):
                        s0 = (hh * 2 + tq) * 512
                        p.op("v", "reciprocal", rcp3[0:1, 0:512], pB[64:65, s0:s0 + 512],
                             inc="rcp3")
                        p.wait("t", "rcp3", p.val("rcp3"))
                        mm(pA[0:64, s0 % 2048:s0 % 2048 + 512], onesq_s[0:1, 0:64],
                           rcp3[0:1, 0:512], True, True, inc="bcm3")
                        p.wait("v", "bcm3", p.val("bcm3"))
                        p.op("v", "tensor_copy",
                             AP(bc643.tensor, bc643.offset, [[bc643.ap[0][0], 64], [1, 512]]),
                             pA[0:64, s0 % 2048:s0 % 2048 + 512])
                        p.op("v", "tensor_tensor",
                             att_fm[hh * 64:hh * 64 + 64,
                                    i * T + w * 1024 + tq * 512:i * T + w * 1024 + (tq + 1) * 512],
                             pB[0:64, s0:s0 + 512],
                             AP(bc643.tensor, bc643.offset, [[bc643.ap[0][0], 64], [1, 512]]),
                             ALU.mult, inc="nrm3")
                p.wait("t", "nrm3", p.val("nrm3"))
        tap("att3", att_fm, [128, 16384], BF16, "v")
        p.barrier()
        tvr = ydma(X[:], tok_dram(d_xsp))
        for e_ in "tvsg":
            p.wait(e_, "ydma", tvr)
        p.barrier()

        wpool.mark()
        owt, _ = load_weight(E["a_owt"], D, D)
        p.wait("t", "wload")
        slot_out(lambda kc, j: att_fm[:, kc * T + j * 128:kc * T + (j + 1) * 128],
                 owt, C, otok_ap=seg(24704, 1024))
        wpool.release()
        pool.release()

    def slot4():
        ccur[0] = 0
        p.barrier()
        pool.mark()
        reg = pool.alloc(29000)

        def seg(off, n, dtype=F32):
            a = reg[:, off:off + n]
            return a.bitcast(BF16) if dtype == BF16 else a
        r, nmr, mean, var = ln_cols(X_t, TT, D, 1e-5)
        t2 = col(TT)
        qn = col(TT)
        p.op("v", "tensor_scalar", t2, var, 1e-5, None, ALU.add)
        p.op("v", "reciprocal", t2, t2)
        p.op("v", "tensor_tensor", t2, var, t2, ALU.mult)
        p.op("v", "tensor_scalar", t2, t2, float(D), None, ALU.mult, inc="s4a")
        p.wait("s", "s4a", p.val("s4a"))
        p.op("s", "activation", qn, t2, ACT.Sqrt, inc="s4b")
        p.wait("v", "s4b", p.val("s4b"))
        p.op("v", "tensor_scalar", t2, qn, 1e-12, None, ALU.max)
        p.op("v", "reciprocal", t2, t2)
        p.op("v", "tensor_scalar", t2, t2, 32.0, None, ALU.mult)
        rhat = col(TT)
        nmh = col(TT)
        p.op("v", "tensor_tensor", rhat, r, t2, ALU.mult)
        p.op("v", "tensor_tensor", nmh, mean, rhat, ALU.mult)
        p.op("v", "tensor_scalar", nmh, nmh, -1.0, None, ALU.mult)
        finish_nmr(r, nmr, mean)
        vs_sync()
        xhat = seg(0, 8192, BF16)
        ln_transpose(X_t, rhat, nmh, xhat)
        p.barrier()

        def fmx(c):
            return xhat[:, c * T:(c + 1) * T]

        b1l = seg(8192, 4); gl = seg(8200, C); gh = seg(8208, C)
        b1h = seg(8216, 32); b2l = seg(8248, D); b2h = seg(9272, D)
        rtc = seg(10296, C // 2, BF16)
        selr = seg(10300, 2)
        for dst, name in ((b1l, "f_b1l"), (gl, "f_gl"), (gh, "f_gh"),
                          (b1h, "f_b1h"), (b2l, "f_b2lrow"), (b2h, "f_b2hrow"),
                          (selr, "f_selrow")):
            ydma(dst if name != "f_selrow" else selr[0:1, 0:2], E[name])
        ydma(rtc, E["f_rt"])
        tvc4 = p.val("ydma")
        for e_ in "tvsg":
            p.wait(e_, "ydma", tvc4)

        for tb in range(NB):
            for kc in range(C):
                mm(pB[0:1, tb * 512:(tb + 1) * 512],
                   rtc[:, kc:kc + 1].bitcast(BF16) if False else rtc[:, kc:kc + 1],
                   fmx(kc)[:, tb * 512:(tb + 1) * 512],
                   kc == 0, kc == C - 1, inc="mmsr" if kc == C - 1 else None)
        p.wait("v", "mmsr", p.val("mmsr"))
        for j in range(TT):
            p.op("t", "transpose", pA[0:1, j * 128:(j + 1) * 128],
                 qn[:, j:j + 1], ident[:], inc="qtr" if j == TT - 1 else None)
        p.wait("v", "qtr", p.val("qtr"))
        srow = dump[0:1, 0:2048]
        p.op("v", "tensor_scalar", srow, pA[0:1, 0:2048], 1.0 / 32.0, None, ALU.mult)
        p.op("v", "tensor_tensor", srow, srow, pB[0:1, 0:2048], ALU.mult, inc="sr4")
        p.wait("g", "sr4", p.val("sr4"))
        p.wait("t", "sr4", p.val("sr4"))
        gdma(d_s_in[0][0:1, :], srow)
        gdma(d_s_in[1][0:1, :], srow)
        p.wait("g", "gdma")
        cc("AllGather", [d_s_in[0]], [d_s_out])
        sfull = seg(18500, 4096)[0:1, 0:4096]
        tvsf = gdma(sfull, AP(d_s_out.tensor, d_s_out.offset, [[1, 1], [1, 4096]]))
        p.wait("v", "gdma", tvsf)
        p.wait("t", "gdma", tvsf)

        s32 = seg(22596, 32)
        for f in range(32):
            p.op("t", "transpose", pA[:, (f % 16) * 128:(f % 16) * 128 + 1] if False
                 else pA[:, f:f + 1], sfull[0:1, f * 128:(f + 1) * 128], ident[0:1, 0:1],
                 inc="str" if f == 31 else None)
        p.wait("v", "str", p.val("str"))
        p.op("v", "tensor_copy", s32, pA[:, 0:32], inc="sev")
        p.wait("t", "sev", p.val("sev"))
        acol = col(1)
        ecol = col(1)
        LOGK = float(np.log(1152.0))
        LOGN = float(np.log(4096.0))
        p.op("v", "memset", acol, LOGK - LOGN)
        e32 = seg(22628, 32)
        for it in range(49):
            p.op("v", "tensor_scalar", e32, s32, acol, 0.0, ALU.add, op1=ALU.min,
                 inc="cdv")
            p.wait("s", "cdv", p.val("cdv"))
            p.op("s", "activation", e32, e32, ACT.Exp, accum_out=ecol, inc="cds")
            p.wait("t", "cds", p.val("cds"))
            mm(pB[:, 0:1], onesq_s[:], ecol, True, True, inc="cdm")
            p.wait("s", "cdm", p.val("cdm"))
            p.op("s", "activation", ecol, pB[:, 0:1], ACT.Ln, scale=1.0 / 1152.0,
                 inc="cd2")
            p.wait("v", "cd2", p.val("cd2"))
            p.op("v", "tensor_tensor", acol, acol, ecol, ALU.subtract, inc="cd3")
            p.wait("s", "cd3", p.val("cd3"))
            p.wait("t", "cd3", p.val("cd3"))
        arow = col(1)[0:1, 0:1]
        p.op("v", "tensor_scalar", sfull, sfull, acol[0:1, 0:1], 0.0, ALU.add,
             op1=ALU.min, inc="se1")
        p.wait("s", "se1", p.val("se1"))
        p.op("s", "activation", sfull, sfull, ACT.Exp, inc="se2")
        p.wait("v", "se2", p.val("se2"))
        mrow = seg(22660, 4096)[0:1, 0:4096]
        p.op("v", "tensor_scalar", mrow, sfull, 1.0, None, ALU.is_ge)
        csr = seg(18500, 4096)[0:1, 0:4096]
        zc1 = col(1)[0:1, 0:1]
        p.op("v", "memset", zc1, 0.0)
        p.op("v", "tensor_tensor_scan", csr, mrow, bc(zc1, 4096), 0.0,
             ALU.add, ALU.add)
        p.op("v", "tensor_scalar", csr, csr, 1024.0, None, ALU.is_le)
        p.op("v", "tensor_tensor", mrow, mrow, csr, ALU.mult)
        mloc = csr[0:1, 0:2048]
        p.op("v", "tensor_scalar", mloc, mrow[0:1, 0:2048], selr[0:1, 0:1], None,
             ALU.mult)
        p.op("v", "scalar_tensor_tensor", mloc, mrow[0:1, 2048:4096],
             selr[0:1, 1:2], mloc, ALU.mult, ALU.add, inc="se3")
        p.wait("t", "se3", p.val("se3"))
        mcols = col(TT)
        for j in range(TT):
            p.op("t", "transpose", pA[:, j:j + 1], mloc[0:1, j * 128:(j + 1) * 128],
                 ident[0:1, 0:1], inc="mtr" if j == TT - 1 else None)
        p.wait("v", "mtr", p.val("mtr"))
        p.op("v", "tensor_copy", mcols, pA[:, 0:TT], inc="mev")
        p.wait("t", "mev", p.val("mev"))
        p.barrier()

        wpool.mark()
        w1l, _ = load_weight(E["f_w1lt"], D, 512)
        hl = seg(18500, 4096, BF16)
        p.wait("t", "wload")
        p.wait("v", "wload")
        for kc in range(C):
            p.op("v", "tensor_scalar", w1l(kc, 0, 512), w1l(kc, 0, 512),
                 gl[:, kc:kc + 1], None, ALU.mult, inc="glw" if kc == C - 1 else None)
        p.wait("t", "glw", p.val("glw"))
        for oc4 in range(4):
            for tb in range(NB):
                o = pA[:, tb * 512:(tb + 1) * 512]
                for kc in range(C):
                    mm(o, w1l(kc, oc4 * 128, 128), fmx(kc)[:, tb * 512:(tb + 1) * 512],
                       kc == 0, kc == C - 1, inc="ml1" if kc == C - 1 else None)
            p.wait("s", "ml1", p.val("ml1"))
            for tb in range(NB):
                p.op("s", "activation", hl[:, oc4 * 2048 + tb * 512:oc4 * 2048 + (tb + 1) * 512],
                     pA[:, tb * 512:(tb + 1) * 512], ACT.Gelu,
                     bias=b1l[:, oc4 // 2 * 0 + 0:1] if False else b1l[:, oc4:oc4 + 1],
                     inc="el1")
            p.wait("t", "el1", p.val("el1"))
        wpool.release()
        wpool.mark()
        w2l, _ = load_weight(E["f_w2lt"], 512, D)
        light = seg(10308, TT * D // 2, BF16)
        p.wait("t", "wload")
        for j in range(TT):
            ob2 = pB[:, (j % 2) * 1024:(j % 2) * 1024 + 1024]
            if j >= 2:
                p.wait("t", "el2", p.val("el2") - 1)
            for oh in range(2):
                for hc in range(4):
                    mm(ob2[:, oh * 512:oh * 512 + 512],
                       hl[:, hc * 2048 + j * 128:hc * 2048 + j * 128 + 128],
                       w2l(hc, oh * 512, 512), hc == 0, hc == 3,
                       inc="ml2" if (hc == 3 and oh == 1) else None)
            p.wait("v", "ml2", p.val("ml2"))
            p.op("v", "tensor_tensor", light[:, j * D:(j + 1) * D], ob2[:, 0:1024],
                 b2l[:, 0:D], ALU.add, inc="el2")
        p.wait("t", "el2", p.val("el2"))
        wpool.release()
        p.barrier()

        p.wait("y", "loa", p.val("loa"))
        tvx = ydma(tok_dram(d_xsp), X[:])
        for e_ in "tvsg":
            p.wait(e_, "ydma", tvx)
        hh_ = X[:].bitcast(BF16)
        heavy = seg(18500, 8192, BF16)
        for th in range(2):
            for ocg in range(8):
                wpool.mark()
                w1h, _ = load_weight(E["f_w1ht"][:, ocg * 512:(ocg + 1) * 512], D, 512)
                p.wait("t", "wload")
                p.wait("v", "wload")
                for kc in range(C):
                    p.op("v", "tensor_scalar", w1h(kc, 0, 512), w1h(kc, 0, 512),
                         gh[:, kc:kc + 1], None, ALU.mult,
                         inc="ghw" if kc == C - 1 else None)
                p.wait("t", "ghw", p.val("ghw"))
                for oci in range(4):
                    oc = ocg * 4 + oci
                    for tb in range(2):
                        o = pA[:, tb * 512:(tb + 1) * 512]
                        for kc in range(C):
                            mm(o, w1h(kc, oci * 128, 128),
                               fmx(kc)[:, th * 1024 + tb * 512:th * 1024 + (tb + 1) * 512],
                               kc == 0, kc == C - 1, inc="mh4" if kc == C - 1 else None)
                    p.wait("s", "mh4", p.val("mh4"))
                    for tb in range(2):
                        p.op("s", "activation",
                             hh_[:, oc * 1024 + tb * 512:oc * 1024 + (tb + 1) * 512],
                             pA[:, tb * 512:(tb + 1) * 512], ACT.Gelu,
                             bias=b1h[:, oc:oc + 1], inc="eh4")
                    p.wait("t", "eh4", p.val("eh4"))
                wpool.release()
            for oc in range(C):
                wpool.mark()
                w2h, _ = load_weight(E["f_w2ht"][:, oc * 128:(oc + 1) * 128], 4096, 128)
                p.wait("t", "wload")
                for tb in range(2):
                    o = pA[:, tb * 512:(tb + 1) * 512]
                    for hc in range(32):
                        mm(o, w2h(hc, 0, 128),
                           hh_[:, hc * 1024 + tb * 512:hc * 1024 + (tb + 1) * 512],
                           hc == 0, hc == 31, inc="mo4" if hc == 31 else None)
                    p.wait("s", "mo4", p.val("mo4"))
                    p.op("s", "activation",
                         heavy[:, oc * T + th * 1024 + tb * 512:
                               oc * T + th * 1024 + (tb + 1) * 512],
                         pA[:, tb * 512:(tb + 1) * 512], ACT.Identity, inc="eo4")
                p.wait("t", "eo4", p.val("eo4"))
                wpool.release()
        p.barrier()
        tvr = ydma(X[:], tok_dram(d_xsp))
        for e_ in "tvsg":
            p.wait(e_, "ydma", tvr)
        p.barrier()
        sj = col(4)
        otok4 = seg(26756, 1024)
        for j in range(TT):
            for c in range(C):
                mm_transpose_bf(heavy[:, c * T + j * 128:c * T + (j + 1) * 128],
                                pA[:, c * 128:(c + 1) * 128])
                if c == C - 1:
                    tok = p.tick("t")
                    p.wait_tok("v", tok)
            p.op("v", "tensor_tensor", otok4[:, 0:D], pA[:, 0:D], b2h[:, 0:D], ALU.add)
            p.op("v", "tensor_scalar", otok4[:, 0:D], otok4[:, 0:D], mcols[:, j:j + 1],
                 None, ALU.mult)
            p.op("v", "tensor_tensor", otok4[:, 0:D], otok4[:, 0:D],
                 light[:, j * D:(j + 1) * D], ALU.add, inc="cb4")
            p.wait("t", "cb4", p.val("cb4"))
            p.wait("s", "cb4", p.val("cb4"))
            ln_add_tile(otok4[:, 0:D], j, sj)
        pool.release()

    if N_SLOTS >= 1:
        slot1()
    if N_SLOTS >= 2:
        slot2()
    if N_SLOTS >= 3:
        slot3()
    if N_SLOTS >= 4:
        slot4()

    p.barrier()
    for name, ap_ in dbg.items():
        ydma(ap_, X[:])
    ydma(tok_dram(out_x), X[:])
    p.wait("y", "ydma")
    p.emit()
    print(f"[build] pool high {pool.high * 4 / 1024:.0f}KB/p, wpool {wpool.high * 4 / 1024:.0f}KB/p, "
          f"nops: {[len(p.ops[e]) for e in p.ENGS]}, sbuf left {nc.sbuf_bytes_remaining}")
    p.ctx.close()
    return nc


def make_inputs(inputs, core):
    b, h = core // 2, core % 2
    t = lambda a: np.ascontiguousarray(a.T)
    m = {
        "x_in": np.ascontiguousarray(inputs["L0_attn_x"][b, h * T:(h + 1) * T, :]),
        "identity": np.eye(128, dtype=np.float32),
        "identity_bf": np.eye(128, dtype=BF),
        "ones_bf": np.ones((1, 128), BF),
        "ones_cbf": np.ones((128, 1), BF),
        "selcol": np.ascontiguousarray(
            np.broadcast_to(np.array([[1.0 - h, float(h)]], np.float32), (64, 2))),
        "halomask": np.ascontiguousarray(np.broadcast_to(
            np.array([[0.0, 1.0]] if h == 0 else [[0.0, 0.0]], np.float32), (128, 2))),
        "m_w1t": t(inputs["L0_attn_W1"]),
        "m_w2t": t(inputs["L0_attn_W2"]),
        "m_wlastt": t(inputs["L0_attn_Wlast"]),
        "m_dwt": t(inputs["L0_attn_dw"]),
        "m_bwt": t(inputs["L0_attn_Bw"]),
        "m_cwt": t(inputs["L0_attn_Cw"]),
        "m_dwmt": t(inputs["L0_attn_Dw"]),
        "m_convw": np.ascontiguousarray(
            inputs["L0_attn_conv_w"].reshape(C, 128, 3).transpose(1, 0, 2).reshape(128, C * 3)),
        "m_convb": np.ascontiguousarray(inputs["L0_attn_conv_b"].reshape(C, 128).T),
        "m_A": np.ascontiguousarray(inputs["L0_attn_A"].reshape(64, 1)),
        "m_Bb": np.ascontiguousarray(inputs["L0_attn_Bb"].reshape(64, 1)),
        "m_db": np.ascontiguousarray(inputs["L0_attn_db"].reshape(64, 1)),
        "m_Cb": np.ascontiguousarray(inputs["L0_attn_Cb"].reshape(C, 128).T),
        "m_Db": np.ascontiguousarray(inputs["L0_attn_Db"].reshape(C, 128).T),
        "m_normw": np.ascontiguousarray(inputs["L0_attn_norm_w"].reshape(C, 128).T),
    }
    perm = [0, 4, 1, 5, 2, 6, 3, 7, 8, 12, 9, 13, 10, 14, 11, 15]
    qw = inputs["L0_ffn_qw"]
    qcols = np.concatenate([qw[:, 64 * h:64 * h + 64] for h in perm], axis=1)
    m["g_qwt"] = np.ascontiguousarray(qcols)
    qb = inputs["L0_ffn_qb"]
    qbp = np.concatenate([qb[64 * h:64 * h + 64] for h in perm])
    m["g_qb"] = np.ascontiguousarray(qbp.reshape(C, 128).T)
    m["g_kwt"] = np.ascontiguousarray(inputs["L0_ffn_kw"])
    m["g_kb"] = np.ascontiguousarray(inputs["L0_ffn_kb"].reshape(2, 128).T)
    m["g_vwt"] = np.ascontiguousarray(inputs["L0_ffn_vw"])
    m["g_vbrow"] = np.ascontiguousarray(
        np.broadcast_to(inputs["L0_ffn_vb"][None, :], (128, 256)))
    ow = inputs["L0_ffn_ow"]
    orows = np.concatenate([ow[64 * h:64 * h + 64, :] for h in perm], axis=0)
    m["g_owt"] = np.ascontiguousarray(orows)
    m["g_ob"] = np.ascontiguousarray(inputs["L0_ffn_ob"].reshape(C, 128).T)
    m["g_gatew"] = np.ascontiguousarray(inputs["L0_ffn_gate_w"])
    m["g_nw1"] = np.ascontiguousarray(inputs["L0_ffn_norm1_w"].reshape(C, 128).T)
    m["g_nw2"] = np.ascontiguousarray(inputs["L0_ffn_norm2_w"].reshape(C, 128).T)
    m["g_w1"] = np.ascontiguousarray(inputs["L0_ffn_w1"])
    m["g_w2"] = np.ascontiguousarray(inputs["L0_ffn_w2"])
    for nm, key in (("a_qwt", "L1_attn_Wq"), ("a_kwt", "L1_attn_Wk"),
                    ("a_vwt", "L1_attn_Wv")):
        m[nm] = np.ascontiguousarray(
            inputs[key].transpose(1, 0, 2).reshape(D, D))
    m["a_owt"] = np.ascontiguousarray(inputs["L1_attn_Wo"])
    inv = 10000.0 ** (-(np.arange(0, 64, 2, dtype=np.float64) / 64.0))
    pos = (h * T + np.arange(T, dtype=np.float64))
    ang = pos[:, None] * inv[None, :]
    for nm, fn in (("a_cos", np.cos), ("a_sin", np.sin)):
        v3 = fn(ang).astype(np.float32).reshape(TT, 128, 1, 32)
        v3 = np.broadcast_to(v3, (TT, 128, 16, 32))
        m[nm] = np.ascontiguousarray(
            v3.reshape(TT, 128, 512).transpose(1, 0, 2).reshape(128, TT * 512)
        ).astype(BF)
    poi = np.zeros((128, 32), np.float32)
    if h == 0:
        poi[:, 0:16] = -1e30
    m["a_poison"] = poi
    ff = np.arange(512)[None, :]
    pp = np.arange(128)[:, None]
    dm = np.zeros((128, 2560), np.float32)
    for m4 in range(4):
        dm[:, m4 * 512:(m4 + 1) * 512] = (ff - pp - m4 * 128 >= 0)
    m["a_diagm"] = dm.astype(BF)
    kvm = np.array([0.0, 1.0, 1.0, 0.0] if h == 0 else [0.0, 0.0, 0.0, 1.0],
                   np.float32)
    m["a_kvmask"] = np.ascontiguousarray(np.broadcast_to(kvm[None, :], (128, 4)))
    m["f_rt"] = np.ascontiguousarray(
        inputs["L1_ffn_routing_token"].reshape(C, 128).T).astype(BF)
    m["f_w1lt"] = t(inputs["L1_ffn_w1_l"])
    m["f_b1l"] = np.ascontiguousarray(inputs["L1_ffn_b1_l"].reshape(4, 128).T)
    m["f_w2lt"] = t(inputs["L1_ffn_w2_l"])
    m["f_b2lrow"] = np.ascontiguousarray(
        np.broadcast_to(inputs["L1_ffn_b2_l"][None, :], (128, D)))
    m["f_w1ht"] = t(inputs["L1_ffn_w1_h"])
    m["f_b1h"] = np.ascontiguousarray(inputs["L1_ffn_b1_h"].reshape(32, 128).T)
    m["f_w2ht"] = t(inputs["L1_ffn_w2_h"])
    m["f_b2hrow"] = np.ascontiguousarray(
        np.broadcast_to(inputs["L1_ffn_b2_h"][None, :], (128, D)))
    m["f_gl"] = np.ascontiguousarray(inputs["L1_ffn_gamma_l"].reshape(C, 128).T)
    m["f_gh"] = np.ascontiguousarray(inputs["L1_ffn_gamma_h"].reshape(C, 128).T)
    m["f_selrow"] = np.array([[1.0, 0.0]] if h == 0 else [[0.0, 1.0]], np.float32)
    m["ones_sq"] = np.ones((128, 128), np.float32)
    return {k: np.asarray(v) for k, v in m.items()}


_NC_CACHE = {}


def kernel(**inputs):
    key = "prog" + ",".join(TAPS)
    if key not in _NC_CACHE:
        _NC_CACHE[key] = build_program()
    nc = _NC_CACHE[key]
    in_maps = [make_inputs(inputs, c) for c in range(8)]
    res = run_bass_kernel_spmd(nc, in_maps, core_ids=list(range(8)))
    out = np.zeros((4, 4096, 1024), np.float32)
    for c in range(8):
        b, h = c // 2, c % 2
        o = res.results[c]["out"]
        out[b, h * T:(h + 1) * T, :] = o
    if TAPS:
        kernel.tapped = [{k: v for k, v in res.results[c].items() if k.startswith("tap_")}
                         for c in range(8)]
    return out


# version 03
# speedup vs baseline: 1.0184x; 1.0147x over previous
import numpy as np
import ml_dtypes
from contextlib import ExitStack

import concourse.bass as bass
import concourse.mybir as mybir
from concourse.ap import AP
from concourse.bass_utils import run_bass_kernel_spmd
from concourse import library_config

F32 = mybir.dt.float32
BF16 = mybir.dt.bfloat16
ALU = mybir.AluOpType
ACT = mybir.ActivationFunctionType
AX = mybir.AxisListType
BF = ml_dtypes.bfloat16

T = 2048
TT = 16
D = 1024
C = 8
NB = 4
GROUPS = [[0, 1], [2, 3], [4, 5], [6, 7]]
EPS_F32 = float(np.finfo(np.float32).eps)

N_SLOTS = 4
S2CUT = 99
S2TAIL = 99
LNPH = 99
TAPS = []


def bc(col_ap, n):
    return AP(col_ap.tensor, col_ap.offset, [col_ap.ap[0], [0, n]])


def fview(ap_, off, dims):
    return AP(ap_.tensor, ap_.offset + off, [ap_.ap[0]] + dims)


def tok_dram(ap2d, rows=T, width=D):
    return AP(ap2d.tensor, ap2d.offset,
              [[width, 128], [128 * width, rows // 128], [1, width]])


class Prog:
    ENGS = ("t", "v", "s", "g", "y")

    def __init__(self, nc):
        self.nc = nc
        self.ops = {e: [] for e in self.ENGS}
        self.last = {e: None for e in self.ENGS}
        self.ctx = ExitStack()
        self.sems = {}
        self.cnt = {}

    DEDICATED = {"ydma", "gdma", "wload", "ccs", "pbg", "eh1", "ea",
                 "el2", "evv", "evv3", "pev", "rpe", "rps", "rtp", "tpev", "tptr",
                 "bar_t", "bar_v", "bar_s", "bar_g", "bar_y",
                 "fen_t", "fen_v", "fen_s", "fen_g", "fen_y"}

    def sem(self, name):
        if name not in self.cnt:
            self.cnt[name] = 0
        if name in self.DEDICATED:
            key = name
        else:
            import zlib
            key = "sh%02d" % (zlib.crc32(name.encode()) % 40)
        if key not in self.sems:
            self.sems[key] = self.ctx.enter_context(self.nc.semaphore(key))
        return self.sems[key]

    def key(self, name):
        if name in self.DEDICATED:
            return name
        import zlib
        return "sh%02d" % (zlib.crc32(name.encode()) % 40)

    def op(self, e, meth, *args, inc=None, by=1, **kw):
        import sys as _sys
        f = _sys._getframe(1)
        ln = f.f_lineno
        if f.f_code.co_name in ("op", "gdma", "ydma", "mm", "tap"):
            ln = f.f_back.f_lineno
        rec = {"meth": meth, "args": args, "kw": kw, "incs": [], "line": ln}
        tv = None
        if inc is not None:
            sm = self.sem(inc)
            k = self.key(inc)
            self.cnt[k] = self.cnt.get(k, 0) + by
            tv = self.cnt[k]
            rec["incs"].append((sm, by, tv))
        self.ops[e].append(rec)
        self.last[e] = rec
        return tv

    def val(self, name):
        return self.cnt.get(self.key(name), 0)

    def wait(self, e, name, v=None):
        s = self.sem(name)
        tv = self.cnt.get(self.key(name), 0) if v is None else v
        if tv <= 0:
            return
        self.ops[e].append({"wait": (s, tv)})

    DMA_METHS = ("dma_start", "dma_start_transpose", "dma_gather", "dma_scatter_add")

    def tick(self, e, name=None):
        rec = self.last[e]
        if rec is not None and rec["incs"]:
            sm, by, tv = rec["incs"][0]
            return (sm, tv)
        nm = "fen_" + e
        s = self.sem(nm)
        by = 16 if (rec is not None and rec["meth"] in self.DMA_METHS) else 1
        prev = self.cnt.get(nm, 0)
        self.cnt[nm] = prev + by
        tv = self.cnt[nm]
        if rec is not None:
            if e == "g" and prev > 0:
                idx = len(self.ops[e]) - 1 - self.ops[e][::-1].index(rec)
                self.ops[e].insert(idx, {"wait": (s, prev)})
            rec["incs"].append((s, by, tv))
        else:
            if prev > 0:
                self.ops[e].append({"wait": (s, prev)})
            self.ops[e].append({"seminc": (s, 1)})
        return (s, tv)

    def wait_tok(self, e, tok):
        self.ops[e].append({"wait": tok})

    def barrier(self, engs="tvsgy"):
        toks = {e: self.tick(e) for e in engs}
        for e in engs:
            for o in engs:
                if o != e:
                    self.wait_tok(e, toks[o])
        for e in engs:
            self.last[e] = None

    def emit(self):
        with self.nc.Block() as block:
            for attr, key in (("tensor", "t"), ("vector", "v"), ("scalar", "s"),
                              ("gpsimd", "g"), ("sync", "y")):
                ops = self.ops[key]
                autodrain = key in ("v", "s", "g")

                def body(eng, ops=ops, autodrain=autodrain):
                    pending = False
                    for rec in ops:
                        if "wait" in rec:
                            eng.wait_ge(*rec["wait"])
                        elif "seminc" in rec:
                            eng.sem_inc(*rec["seminc"])
                        else:
                            if autodrain and pending and rec["meth"] not in Prog.DMA_METHS:
                                eng.drain()
                            r = getattr(eng, rec["meth"])(*rec["args"], **rec["kw"])
                            for sm, by, _tv in rec["incs"]:
                                r.then_inc(sm, by)
                            if rec["meth"] not in Prog.DMA_METHS:
                                pending = True
                getattr(block, attr)(body)


class Pool:
    def __init__(self, nc, name, cols_f32):
        self.t = nc.alloc_sbuf_tensor(name, [128, cols_f32], F32)
        self.cols = cols_f32
        self.off = 0
        self.marks = []
        self.high = 0

    def alloc(self, cols_f32, dtype=F32):
        cols_f32 = (cols_f32 + 1) & ~1
        off = self.off
        self.off += cols_f32
        self.high = max(self.high, self.off)
        assert self.off <= self.cols, f"pool overflow {self.off} > {self.cols}"
        ap_ = self.t[:, off:off + cols_f32]
        if dtype == BF16:
            ap_ = ap_.bitcast(BF16)
        return ap_

    def mark(self):
        self.marks.append(self.off)

    def release(self):
        self.off = self.marks.pop()


def build_program(dbg_names=()):
    nc = bass.Bass("TRN2", target_bir_lowering=False)
    E = {}

    def inp(name, shape, dtype=F32):
        E[name] = nc.dram_tensor(name, list(shape), dtype, kind="ExternalInput").ap()
        return E[name]

    inp("x_in", (T, D))
    inp("identity", (128, 128))
    inp("identity_bf", (128, 128), BF16)
    inp("ones_bf", (1, 128), BF16)
    inp("ones_cbf", (128, 1), BF16)
    inp("selcol", (64, 2))
    inp("halomask", (128, 2))
    inp("m_w1t", (D, D)); inp("m_w2t", (D, D)); inp("m_wlastt", (D, D))
    inp("m_dwt", (D, 64)); inp("m_bwt", (D, 64)); inp("m_cwt", (64, D))
    inp("m_dwmt", (D, D))
    inp("m_convw", (128, C * 3)); inp("m_convb", (128, C))
    inp("m_A", (64, 1)); inp("m_Bb", (64, 1)); inp("m_db", (64, 1))
    inp("m_Cb", (128, C)); inp("m_Db", (128, C)); inp("m_normw", (128, C))

    inp("g_qwt", (D, D)); inp("g_qb", (128, C))
    inp("g_kwt", (D, 256)); inp("g_kb", (128, 2))
    inp("g_vwt", (D, 256)); inp("g_vbrow", (128, 256))
    inp("g_owt", (D, D)); inp("g_ob", (128, C))
    inp("g_gatew", (D, 8))
    inp("g_nw1", (128, C)); inp("g_nw2", (128, C))
    inp("g_w1", (8, D, 2048)); inp("g_w2", (8, 2048, D))

    inp("a_qwt", (D, D)); inp("a_kwt", (D, D)); inp("a_vwt", (D, D))
    inp("a_owt", (D, D))
    inp("a_cos", (128, TT * 512), BF16); inp("a_sin", (128, TT * 512), BF16)
    inp("a_poison", (128, 32))
    inp("a_diagm", (128, 2560), BF16)
    inp("a_kvmask", (128, 4))
    inp("f_rt", (128, C), BF16)
    inp("f_w1lt", (D, 512)); inp("f_b1l", (128, 4))
    inp("f_w2lt", (512, D)); inp("f_b2lrow", (128, D))
    inp("f_w1ht", (D, 4096)); inp("f_b1h", (128, 32))
    inp("f_w2ht", (4096, D)); inp("f_b2hrow", (128, D))
    inp("f_gl", (128, C)); inp("f_gh", (128, C))
    inp("f_selrow", (1, 2))
    inp("ones_sq", (128, 128))

    out_x = nc.dram_tensor("out", [T, D], F32, kind="ExternalOutput").ap()
    dbg = {}
    for name in dbg_names:
        dbg[name] = nc.dram_tensor("dbg_" + name, [128, TT * D], F32,
                                   kind="ExternalOutput").ap()

    d_h_in = nc.dram_tensor("d_h_in", [2, 128, 16], F32).ap()
    d_h_out = nc.dram_tensor("d_h_out", [128, 16], F32).ap()
    d_sc_in = nc.dram_tensor("d_sc_in", [2, 64, T], F32).ap()
    d_sc_out = nc.dram_tensor("d_sc_out", [2, 2, 64, T], F32).ap()
    d_xsp = nc.dram_tensor("d_xsp", [T, D], F32).ap()
    d_kv_in = nc.dram_tensor("d_kv_in", [128, 4096 + 4160], BF16).ap()
    d_kv_out = nc.dram_tensor("d_kv_out", [2, 128, 4096 + 4160], BF16).ap()
    KVW = 16384 + 16640
    d_rs_in = nc.dram_tensor("d_rs_in", [2, 128, 2 * KVW], BF16).ap()
    d_rs_out = nc.dram_tensor("d_rs_out", [128, 2 * KVW], BF16).ap()
    d_s_in = nc.dram_tensor("d_s_in", [2, 1, T], F32).ap()
    d_s_out = nc.dram_tensor("d_s_out", [2, 1, T], F32).ap()

    p = Prog(nc)

    X = nc.alloc_sbuf_tensor("X", [128, TT * D], F32)
    ident = nc.alloc_sbuf_tensor("ident", [128, 128], F32)
    identb = nc.alloc_sbuf_tensor("identb", [128, 128], BF16)
    onesb = nc.alloc_sbuf_tensor("onesb", [1, 128], BF16)
    onescb = nc.alloc_sbuf_tensor("onescb", [128, 1], BF16)
    onesq_s = nc.alloc_sbuf_tensor("onesq_s", [128, 128], F32)
    pool = Pool(nc, "pool", 29000)
    wpool = Pool(nc, "wpool", 4736)
    cols = nc.alloc_sbuf_tensor("cols", [128, 384], F32)
    dump = nc.alloc_sbuf_tensor("dump", [128, 2048], F32)
    ccur = [0]

    def col(n=1):
        o = ccur[0]
        ccur[0] += n
        assert ccur[0] <= 384
        return cols[:, o:o + n]

    pA = nc.alloc_psum_tensor("pA", [128, 2048], F32)
    pB = nc.alloc_psum_tensor("pB", [128, 2048], F32)

    def X_t(j):
        return X[:, j * D:(j + 1) * D]

    wslot = [0]

    def load_weight(hbm_ap, K, O):
        tok = p.tick("t")
        p.wait_tok("g", tok)
        if K % 128 == 0:
            kc_n = K // 128
            buf = wpool.alloc(kc_n * O // 2, BF16)
            rs = hbm_ap.ap[-2][0]
            src = AP(hbm_ap.tensor, hbm_ap.offset,
                     [[rs, 128], [128 * rs, kc_n], [1, O]])
            tv = p.op("g", "dma_start", inc="wload", by=16, out=buf, in_=src)
            return (lambda kc, o0, ow, b=buf, O=O: b[:, kc * O + o0:kc * O + o0 + ow]), tv
        else:
            buf0 = wpool.alloc(O // 2, BF16)
            buf = AP(buf0.tensor, buf0.offset, [[buf0.ap[0][0], K], [1, O]])
            tv = p.op("g", "dma_start", inc="wload", by=16, out=buf, in_=hbm_ap)
            return (lambda kc, o0, ow, b=buf: b[0:K, o0:o0 + ow]), tv

    def vs_sync():
        a = p.tick("v")
        b = p.tick("s")
        p.wait_tok("s", a)
        p.wait_tok("v", b)

    def ln_cols(srcfn, ntiles, width, eps):
        ssum = col(ntiles)
        qsum = col(ntiles)
        for j in range(ntiles):
            p.op("s", "activation", dump[:, 0:width], srcfn(j), ACT.Identity,
                 accum_out=ssum[:, j:j + 1])
            p.op("v", "scalar_tensor_tensor", dump[:, 1024:1024 + width], srcfn(j), 1.0,
                 srcfn(j), ALU.mult, ALU.mult, accum_out=qsum[:, j:j + 1])
        sv = p.tick("s")
        p.wait_tok("v", sv)
        mean = col(ntiles); var = col(ntiles); r = col(ntiles); nmr = col(ntiles)
        inv = 1.0 / width
        p.op("v", "tensor_scalar", mean, ssum, inv, None, ALU.mult)
        p.op("v", "tensor_scalar", var, qsum, inv, None, ALU.mult)
        p.op("v", "tensor_tensor", nmr, mean, mean, ALU.mult)
        p.op("v", "tensor_tensor", var, var, nmr, ALU.subtract)
        p.op("v", "tensor_scalar", r, var, eps, None, ALU.add)
        p.op("v", "reciprocal", r, r)
        vs_sync()
        p.op("s", "activation", r, r, ACT.Sqrt)
        return r, nmr, mean, var

    def finish_nmr(r, nmr, mean):
        p.op("v", "tensor_tensor", nmr, mean, r, ALU.mult)
        p.op("v", "tensor_scalar", nmr, nmr, -1.0, None, ALU.mult)

    def transpose_tok_fm(src_tile, dst_fm, ntiles=TT, nchunks=C, dst_w=T):
        for j in range(ntiles):
            half = j % 2
            pdst = pT[:, half * 1024:half * 1024 + nchunks * 128]
            if j >= 2:
                p.wait("t", "tpev", p.val("tpev") - 1)
            for c in range(nchunks):
                inc = "tptr" if c == nchunks - 1 else None
                p.op("t", "transpose", pdst[:, c * 128:(c + 1) * 128],
                     src_tile(j, c), ident[:], inc=inc)
            p.wait("v", "tptr", p.val("tptr"))
            dst_ap = fview(dst_fm, j * 128, [[dst_w, nchunks], [1, 128]])
            p.op("v", "tensor_copy", dst_ap, pdst, inc="tpev")
        p.wait("t", "tpev", p.val("tpev"))

    def ln_transpose(srcfn, r, nmr, dst_fm, ntiles=TT, nchunks=C, dst_w=T):
        stage = dump[:, 0:2048]
        for j in range(ntiles):
            st = stage[:, (j % 2) * 1024:(j % 2) * 1024 + 1024]
            if j >= 2:
                p.wait("s", "tptr", p.val("tptr") - 1)
            p.op("s", "activation", st[:, 0:nchunks * 128], srcfn(j), ACT.Identity,
                 scale=r[:, j:j + 1], bias=nmr[:, j:j + 1], inc="lnn")
            p.wait("t", "lnn", p.val("lnn"))
            half = j % 2
            pdst = pB[:, half * 1024:half * 1024 + 1024]
            if j >= 2:
                p.wait("t", "tpev", p.val("tpev") - 1)
            for c in range(nchunks):
                p.op("t", "transpose", pdst[:, c * 128:(c + 1) * 128],
                     st[:, c * 128:(c + 1) * 128], ident[:],
                     inc="tptr" if c == nchunks - 1 else None)
            p.wait("v", "tptr", p.val("tptr"))
            dst_ap = fview(dst_fm, j * 128, [[dst_w, nchunks], [1, 128]])
            p.op("v", "tensor_copy", dst_ap, pdst[:, 0:nchunks * 128], inc="tpev")
        p.wait("t", "tpev", p.val("tpev"))

    def mm_transpose_bf(src_tile_bf, out_psum):
        return mm(out_psum, src_tile_bf, identb[:], True, True)

    def ln_add_tile(src_ap, j, sj):
        if LNPH < 1:
            return
        p.op("v", "tensor_reduce", sj[:, 0:1], src_ap, AX.X, ALU.add, inc="lrd")
        if LNPH < 1.5:
            return
        p.op("s", "activation", dump[:, 1024:2048], src_ap, ACT.Square,
             accum_out=sj[:, 1:2], inc="lsq")
        p.wait("v", "lsq", p.val("lsq"))
        if LNPH < 2:
            return
        p.op("v", "tensor_scalar", sj[:, 0:1], sj[:, 0:1], 1.0 / D, None, ALU.mult)
        p.op("v", "tensor_scalar", sj[:, 1:2], sj[:, 1:2], 1.0 / D, None, ALU.mult)
        p.op("v", "tensor_tensor", sj[:, 2:3], sj[:, 0:1], sj[:, 0:1], ALU.mult)
        p.op("v", "tensor_tensor", sj[:, 1:2], sj[:, 1:2], sj[:, 2:3], ALU.subtract)
        p.op("v", "tensor_scalar", sj[:, 1:2], sj[:, 1:2], 1e-5, None, ALU.add)
        p.op("v", "reciprocal", sj[:, 1:2], sj[:, 1:2])
        p.op("v", "tensor_scalar", sj[:, 0:1], sj[:, 0:1], -1.0, None, ALU.mult,
             inc="lov")
        p.wait("s", "lov", p.val("lov"))
        if LNPH < 3:
            return
        p.op("s", "activation", sj[:, 1:2], sj[:, 1:2], ACT.Sqrt)
        p.op("s", "activation", sj[:, 3:4], sj[:, 0:1], ACT.Identity,
             scale=sj[:, 1:2])
        if LNPH < 4:
            return
        p.op("s", "activation", dump[:, 0:D], src_ap, ACT.Identity,
             scale=sj[:, 1:2], bias=sj[:, 3:4], inc="los")
        p.wait("v", "los", p.val("los"))
        if LNPH < 5:
            return
        p.op("v", "tensor_tensor", X_t(j), X_t(j), dump[:, 0:D], ALU.add, inc="loa")
        p.wait("s", "loa", p.val("loa"))
        p.wait("t", "loa", p.val("loa"))

    def cc(kind, ins, outs, alu=ALU.bypass):
        tv = p.op("g", "collective_compute", kind, alu, inc="ccs",
                  replica_groups=GROUPS, ins=ins, outs=outs)
        p.wait("g", "ccs", tv)

    def gdma(out, in_):
        return p.op("g", "dma_start", inc="gdma", by=16, out=out, in_=in_)

    def ydma(out, in_):
        p.wait("y", "ydma")
        return p.op("y", "dma_start", inc="ydma", by=16, out=out, in_=in_)

    taps = {}

    def tap(name, ap_, shape, dtype, prod_eng, prod_sem=None):
        if name not in TAPS:
            return
        o = nc.dram_tensor("tap_" + name, list(shape), dtype,
                           kind="ExternalOutput").ap()
        tok = p.tick(prod_eng)
        p.wait_tok("y", tok)
        tv = ydma(o, ap_)
        for e in "tvsg":
            p.wait(e, "ydma", tv)

    def mm(out, w, r, start, stop, inc=None, tp=None):
        kw = {}
        if tp is not None:
            kw["tile_position"] = tp
        return p.op("t", "matmul", out, w, r, inc=inc, start=start, stop=stop, **kw)

    ydma(X[:], tok_dram(E["x_in"]))
    ydma(ident[:], E["identity"])
    ydma(identb[:], E["identity_bf"])
    ydma(onesb[:], E["ones_bf"])
    ydma(onescb[:], E["ones_cbf"])
    ydma(onesq_s[:], E["ones_sq"])
    selc = nc.alloc_sbuf_tensor("selc", [64, 2], F32)
    ydma(selc[:], E["selcol"])
    hmask = nc.alloc_sbuf_tensor("hmask", [128, 2], F32)
    ydma(hmask[:], E["halomask"])
    convw = nc.alloc_sbuf_tensor("convw", [128, C * 3], F32)
    convb = nc.alloc_sbuf_tensor("convb", [128, C], F32)
    m_nw = nc.alloc_sbuf_tensor("m_nw", [128, C], F32)
    mCb = nc.alloc_sbuf_tensor("mCb", [128, C], F32)
    mDb = nc.alloc_sbuf_tensor("mDb", [128, C], F32)
    smA = nc.alloc_sbuf_tensor("smA", [64, 4], F32)
    ydma(convw[:], E["m_convw"])
    ydma(convb[:], E["m_convb"])
    ydma(m_nw[:], E["m_normw"])
    ydma(mCb[:], E["m_Cb"])
    ydma(mDb[:], E["m_Db"])
    ydma(smA[:, 0:1], E["m_A"])
    ydma(smA[:, 2:3], E["m_Bb"])
    ydma(smA[:, 3:4], E["m_db"])
    for e in "tvsg":
        p.wait(e, "ydma")
    p.op("s", "activation", smA[:, 1:2], smA[:, 0:1], ACT.Sigmoid, scale=-1.0)
    p.op("s", "activation", smA[:, 1:2], smA[:, 1:2], ACT.Ln)
    sv = p.tick("s")
    p.wait_tok("v", sv)
    p.op("v", "tensor_scalar", smA[:, 1:2], smA[:, 1:2], -1.0, None, ALU.mult)
    p.op("v", "tensor_tensor", mCb[:], mCb[:], mDb[:], ALU.add)
    p.barrier()

    def slot1():
        ccur[0] = 0
        p.barrier()
        pool.mark()
        r, nmr, mean, var = ln_cols(X_t, TT, D, 1e-5)
        t2 = col(TT)
        p.op("v", "tensor_scalar", t2, var, 1e-5, None, ALU.add)
        p.op("v", "reciprocal", t2, t2)
        p.op("v", "tensor_tensor", t2, var, t2, ALU.mult)
        p.op("v", "tensor_scalar", t2, t2, EPS_F32, None, ALU.add)
        p.op("v", "reciprocal", t2, t2)
        vs_sync()
        p.op("s", "activation", t2, t2, ACT.Sqrt)
        sv = p.tick("s")
        p.wait_tok("v", sv)
        r1c = col(TT)
        nm1c = col(TT)
        p.op("v", "tensor_copy", r1c, r)
        p.op("v", "tensor_tensor", nm1c, mean, r1c, ALU.mult)
        p.op("v", "tensor_scalar", nm1c, nm1c, -1.0, None, ALU.mult)
        p.op("v", "tensor_tensor", r, r, t2, ALU.mult)
        finish_nmr(r, nmr, mean)
        tap("cols", cols[:, 0:128], [128, 128], F32, "v", "tapv")
        vs_sync()

        sz_bf = pool.alloc(C * T // 2, BF16)
        xc_bf = pool.alloc(C * T // 2, BF16)
        pool.mark()
        xn_fm = pool.alloc(C * T // 2, BF16)

        def fmc(c):
            return xn_fm[:, c * T:(c + 1) * T]

        ln_transpose(X_t, r, nmr, xn_fm)
        p.barrier()
        for c in range(C):
            p.op("v", "tensor_scalar", fmc(c), fmc(c), m_nw[:, c:c + 1], None,
                 ALU.mult, inc="nwm" if c == C - 1 else None)
        p.wait("g", "nwm")
        p.wait("t", "nwm")
        tap("xn_fm", xn_fm, [128, C * T], BF16, "v", "tapv")
        halos = col(32)
        xnhalo = fview(xn_fm, 2046, [[T, C], [1, 2]])
        for reg in range(2):
            p.op("v", "tensor_scalar", halos[:, reg * 16:reg * 16 + 16], xnhalo,
                 hmask[:, reg:reg + 1], None, ALU.mult,
                 inc="hal" if reg == 1 else None)
        p.wait("g", "hal")
        gdma(d_h_in[0], halos[:, 0:16])
        gdma(d_h_in[1], halos[:, 16:32])
        p.wait("g", "gdma")
        cc("ReduceScatter", [d_h_in], [d_h_out], ALU.add)
        xnh = col(16)
        tvh = gdma(xnh, d_h_out)
        p.wait("v", "gdma", tvh)
        xnh_bf = col(8).bitcast(BF16)
        p.op("v", "tensor_copy", xnh_bf, xnh, inc="halb")
        p.wait("t", "halb")

        wpool.mark()
        w2t, w2v = load_weight(E["m_w2t"], D, D)
        p.wait("t", "wload")
        for c in range(C):
            if c > 0:
                p.wait("t", "evz", p.val("evz"))
            for tb in range(NB):
                o = pA[:, tb * 512:(tb + 1) * 512]
                for kc in range(C):
                    mm(o, w2t(kc, c * 128, 128), fmc(kc)[:, tb * 512:(tb + 1) * 512],
                       kc == 0, kc == C - 1, inc="mmz" if kc == C - 1 else None)
            p.wait("s", "mmz", p.val("mmz"))
            for tb in range(NB):
                p.op("s", "activation", sz_bf[:, c * T + tb * 512:c * T + (tb + 1) * 512],
                     pA[:, tb * 512:(tb + 1) * 512], ACT.Silu, inc="evz")
        p.wait("t", "evz", p.val("evz"))
        tap("sz", sz_bf, [128, C * T], BF16, "s", "taps")
        wpool.release()

        wpool.mark()
        w1t, w1v = load_weight(E["m_w1t"], D, D)
        pool.mark()
        xb = pool.alloc(2050)
        cvt = pool.alloc(2048)
        p.wait("t", "wload")
        for c in range(C):
            for tb in range(NB):
                o = pA[:, tb * 512:(tb + 1) * 512]
                for kc in range(C):
                    mm(o, w1t(kc, c * 128, 128), fmc(kc)[:, tb * 512:(tb + 1) * 512],
                       kc == 0, kc == C - 1, inc="mmb" if kc == C - 1 else None)
            for kc in range(C):
                mm(pB[:, 0:2], w1t(kc, c * 128, 128), xnh_bf[:, kc:kc + 1][:, 0:1] if False else
                   AP(xnh_bf.tensor, xnh_bf.offset + kc * 2, [xnh_bf.ap[0], [1, 2]]),
                   kc == 0, kc == C - 1, inc="mmb" if kc == C - 1 else None)
            p.wait("s", "mmb", p.val("mmb"))
            p.op("s", "activation", xb[:, 0:2], pB[:, 0:2], ACT.Identity)
            p.op("s", "activation", xb[:, 2:2050], pA[:, 0:2048], ACT.Identity,
                 inc="evb")
            p.wait("v", "evb", p.val("evb"))
            p.wait("t", "evb", p.val("evb"))
            p.op("v", "tensor_scalar", cvt[:], xb[:, 0:2048],
                 convw[:, c * 3:c * 3 + 1], None, ALU.mult)
            p.op("v", "scalar_tensor_tensor", cvt[:], xb[:, 1:2049],
                 convw[:, c * 3 + 1:c * 3 + 2], cvt[:], ALU.mult, ALU.add)
            p.op("v", "scalar_tensor_tensor", cvt[:], xb[:, 2:2050],
                 convw[:, c * 3 + 2:c * 3 + 3], cvt[:], ALU.mult, ALU.add,
                 inc="cvv")
            p.wait("s", "cvv", p.val("cvv"))
            p.op("s", "activation", xc_bf[:, c * T:(c + 1) * T], cvt[:], ACT.Silu,
                 bias=convb[:, c:c + 1], inc="evc")
            p.wait("v", "evc", p.val("evc"))
        tap("xc", xc_bf, [128, C * T], BF16, "s", "taps")
        pool.release()
        wpool.release()
        pool.release()
        p.barrier()

        wpool.mark()
        dwt, dwv = load_weight(E["m_dwt"], D, 64)
        bwt, bwv = load_weight(E["m_bwt"], D, 64)
        pool.mark()
        fullsA = pool.alloc(2 * T)
        fullsB = pool.alloc(2 * T)
        wks = pool.alloc(2 * T)

        def rows64(buf, o, n):
            return AP(buf.tensor, buf.offset + o, [[buf.ap[0][0], 64], [1, n]])

        abar_l = rows64(fullsA, 0, T)
        bbar_l = rows64(fullsB, 0, T)
        abar_f = rows64(fullsA, 0, 2 * T)
        bbar_f = rows64(fullsB, 0, 2 * T)
        delta = rows64(dump[:], 0, T)
        negA = rows64(smA[:], 1, 1)
        Bb64 = rows64(smA[:], 2, 1)
        db64 = rows64(smA[:], 3, 1)
        ndb = col(1)
        ndb64 = AP(ndb.tensor, ndb.offset, [[ndb.ap[0][0], 64], [1, 1]])
        p.op("v", "tensor_scalar", ndb64, db64, -1.0, None, ALU.mult, inc="ndbv")
        p.wait("s", "ndbv")
        p.wait("t", "wload")
        for tb in range(NB):
            o = pA[0:64, tb * 512:(tb + 1) * 512]
            for kc in range(C):
                mm(o, dwt(kc, 0, 64), xc_bf[:, kc * T + tb * 512:kc * T + (tb + 1) * 512],
                   kc == 0, kc == C - 1, inc="mmd" if kc == C - 1 else None)
        p.wait("s", "mmd", p.val("mmd"))
        p.op("s", "activation", delta, pA[0:64, 0:2048], ACT.Sigmoid,
             scale=-1.0, bias=ndb64)
        p.op("s", "activation", delta, delta, ACT.Ln, inc="evd")
        p.wait("v", "evd", p.val("evd"))
        p.wait("t", "evd", p.val("evd"))
        p.op("v", "tensor_scalar", abar_l, delta, negA, None, ALU.mult)
        p.op("v", "tensor_scalar", delta, delta, -1.0, None, ALU.mult, inc="dlv")
        p.wait("t", "dlv")
        p.wait("t", "wload")
        for tb in range(NB):
            o = pB[0:64, (tb % 2) * 512:(tb % 2) * 512 + 512]
            if tb >= 2:
                p.wait("t", "evB", p.val("evB") - 1)
            for kc in range(C):
                mm(o, bwt(kc, 0, 64), xc_bf[:, kc * T + tb * 512:kc * T + (tb + 1) * 512],
                   kc == 0, kc == C - 1, inc="mmB" if kc == C - 1 else None)
            p.wait("v", "mmB", p.val("mmB"))
            p.op("v", "scalar_tensor_tensor",
                 AP(bbar_l.tensor, bbar_l.offset + tb * 512, [bbar_l.ap[0], [1, 512]]),
                 o, Bb64,
                 AP(delta.tensor, delta.offset + tb * 512, [delta.ap[0], [1, 512]]),
                 ALU.add, ALU.mult, inc="evB")
        p.wait("g", "evB", p.val("evB"))
        tvv = p.tick("v")
        p.wait_tok("g", tvv)
        gdma(d_sc_in[0], abar_l)
        gdma(d_sc_in[1], bbar_l)
        p.wait("g", "gdma")
        cc("AllGather", [d_sc_in], [d_sc_out])
        gdma(rows64(fullsA, 0, T), d_sc_out[0][0])
        gdma(rows64(fullsA, T, T), d_sc_out[1][0])
        gdma(rows64(fullsB, 0, T), d_sc_out[0][1])
        gdma(rows64(fullsB, T, T), d_sc_out[1][1])
        p.wait("v", "gdma", p.val("gdma"))
        tap("abarf", fullsA, [128, 2 * T], F32, "g", "tapg")
        tap("bbarf", fullsB, [128, 2 * T], F32, "g", "tapg")
        zc = col(1)
        zc64 = AP(zc.tensor, zc.offset, [[zc.ap[0][0], 64], [1, 1]])
        scum = rows64(wks, 0, 2 * T)
        p.op("v", "memset", zc64, 0.0)
        p.op("v", "tensor_tensor_scan", scum, abar_f, bc(zc64, 2 * T), 0.0,
             ALU.add, ALU.add)
        p.op("v", "tensor_scalar", scum, scum, -20.0, None, ALU.max)
        adel = abar_f
        p.op("v", "tensor_copy",
             AP(adel.tensor, adel.offset, [adel.ap[0], [1, 1]]),
             AP(scum.tensor, scum.offset, [scum.ap[0], [1, 1]]))
        p.op("v", "tensor_tensor",
             AP(adel.tensor, adel.offset + 1, [adel.ap[0], [1, 2 * T - 1]]),
             AP(scum.tensor, scum.offset + 1, [scum.ap[0], [1, 2 * T - 1]]),
             AP(scum.tensor, scum.offset, [scum.ap[0], [1, 2 * T - 1]]),
             ALU.subtract, inc="scv")
        p.wait("s", "scv", p.val("scv"))
        p.op("s", "activation", adel, adel, ACT.Exp, inc="sce")
        p.wait("v", "sce", p.val("sce"))
        hfull = scum
        p.op("v", "tensor_tensor_scan", hfull, bbar_f, adel, 0.0, ALU.add, ALU.mult)
        tap("hfull", wks, [128, 2 * T], F32, "v", "tapv")
        hmine_f32cols = dump[:, 0:1024]
        hmine = hmine_f32cols.bitcast(BF16)
        hm64 = AP(hmine.tensor, hmine.offset, [[hmine.ap[0][0], 64], [1, T]])
        sel0 = AP(selc[:, 0:1].tensor, selc[:, 0:1].offset, [[selc[:, 0:1].ap[0][0], 64], [1, 1]])
        sel1 = AP(selc[:, 1:2].tensor, selc[:, 1:2].offset, [[selc[:, 1:2].ap[0][0], 64], [1, 1]])
        tmp64 = rows64(fullsB, 0, T)
        p.op("v", "tensor_scalar", tmp64,
             AP(hfull.tensor, hfull.offset, [hfull.ap[0], [1, T]]), sel0, None, ALU.mult)
        p.op("v", "scalar_tensor_tensor", hm64,
             AP(hfull.tensor, hfull.offset + T, [hfull.ap[0], [1, T]]), sel1, tmp64,
             ALU.mult, ALU.add, inc="hmv")
        p.wait("t", "hmv")
        pool.release()
        wpool.release()

        wpool.mark()
        cwt, cwv = load_weight(E["m_cwt"], 64, D)
        dmt, dmv = load_weight(E["m_dwmt"], D, D)
        gated = pool.alloc(C * T // 2, BF16)
        p.wait("t", "wload")
        for c in range(C):
            if c > 0:
                p.wait("t", "evy", p.val("evy"))
            for tb in range(NB):
                o = pA[:, tb * 512:(tb + 1) * 512]
                mm(o, cwt(0, c * 128, 128),
                   AP(hm64.tensor, hm64.offset + tb * 512, [hm64.ap[0], [1, 512]]),
                   True, False)
                for kc in range(C):
                    mm(o, dmt(kc, c * 128, 128),
                       xc_bf[:, kc * T + tb * 512:kc * T + (tb + 1) * 512],
                       False, kc == C - 1, inc="mmy" if kc == C - 1 else None)
            p.wait("v", "mmy", p.val("mmy"))
            for tb in range(NB):
                p.op("v", "scalar_tensor_tensor",
                     gated[:, c * T + tb * 512:c * T + (tb + 1) * 512],
                     pA[:, tb * 512:(tb + 1) * 512], mCb[:, c:c + 1],
                     sz_bf[:, c * T + tb * 512:c * T + (tb + 1) * 512],
                     ALU.add, ALU.mult, inc="evy")
        p.wait("t", "evy", p.val("evy"))
        tap("gated", gated, [128, C * T], BF16, "v", "tapv")
        wpool.release()
        p.barrier()

        wpool.mark()
        wlt, wlv = load_weight(E["m_wlastt"], D, D)
        p.wait("t", "wload")
        slot_out(lambda kc, j: gated[:, kc * T + j * 128:kc * T + (j + 1) * 128],
                 wlt, C, res_scale=(r1c, nm1c))
        wpool.release()
        pool.release()

    def slot_out(act_tile, wt, kc_n, res_scale=None, res_tok=None, otok_ap=None):
        otok = pool.alloc(D) if otok_ap is None else otok_ap
        sj = col(4)
        for j in range(TT):
            for oh in range(2):
                o = pA[:, oh * 512:oh * 512 + 512]
                for kc in range(kc_n):
                    mm(o, act_tile(kc, j), wt(kc, oh * 512, 512),
                       kc == 0, kc == kc_n - 1,
                       inc="mmo" if kc == kc_n - 1 else None)
            if res_scale is not None:
                r1c, nm1c = res_scale
                p.op("s", "activation", dump[:, 1024:2048], X_t(j), ACT.Identity,
                     scale=r1c[:, j:j + 1], bias=nm1c[:, j:j + 1], inc="res")
            p.wait("v", "mmo", p.val("mmo"))
            p.wait("v", "res", p.val("res"))
            if res_scale is not None:
                p.op("v", "tensor_tensor", otok[:, 0:512], pA[:, 0:512],
                     dump[:, 1024:1536], ALU.add)
                p.op("v", "tensor_tensor", otok[:, 512:1024], pA[:, 512:1024],
                     dump[:, 1536:2048], ALU.add, inc="evo")
            elif res_tok is not None:
                p.op("v", "tensor_tensor", otok[:, 0:512], pA[:, 0:512],
                     res_tok(j)[:, 0:512], ALU.add)
                p.op("v", "tensor_tensor", otok[:, 512:1024], pA[:, 512:1024],
                     res_tok(j)[:, 512:1024], ALU.add, inc="evo")
            else:
                p.op("v", "tensor_copy", otok[:, 0:512], pA[:, 0:512])
                p.op("v", "tensor_copy", otok[:, 512:1024], pA[:, 512:1024],
                     inc="evo")
            p.wait("t", "evo", p.val("evo"))
            p.wait("s", "evo", p.val("evo"))
            ln_add_tile(otok[:], j, sj)



    def slot2():
        ccur[0] = 0
        p.barrier()
        pool.mark()
        reg = pool.alloc(29000)

        def seg(off, n, dtype=F32):
            a = reg[:, off:off + n]
            return a.bitcast(BF16) if dtype == BF16 else a
        SA, SB, SC, SD = 0, 8192, 16384, 18984
        r, nmr, mean, var = ln_cols(X_t, TT, D, 1e-5)
        r2c = col(TT)
        nm2c = col(TT)
        sv = p.tick("s")
        p.wait_tok("v", sv)
        p.op("v", "tensor_copy", r2c, r)
        finish_nmr(r, nmr, mean)
        p.op("v", "tensor_copy", nm2c, nmr, inc="cpv")
        p.wait("s", "cpv")
        xn_fm = seg(SA, 8192, BF16)
        ln_transpose(X_t, r, nmr, xn_fm)
        p.barrier()

        def fmc(c):
            return xn_fm[:, c * T:(c + 1) * T]

        qb = seg(SC, C); kb = seg(SC + 8, 2); vbrow = seg(SC + 12, 256)
        ob = seg(SC + 268, C); nw1 = seg(SC + 276, C); nw2 = seg(SC + 284, C)
        for dst, name in ((qb, "g_qb"), (kb, "g_kb"), (vbrow, "g_vbrow"),
                          (ob, "g_ob"), (nw1, "g_nw1"), (nw2, "g_nw2")):
            ydma(dst, E[name])
        tvc = p.val("ydma")
        for e in "tvs":
            p.wait(e, "ydma", tvc)

        qfm = seg(SB, 8192, BF16)
        kfull = seg(SD, 4096, BF16)
        vfull = seg(SD + 4096, 4160, BF16)
        wpool.mark()
        qwt, qwv = load_weight(E["g_qwt"], D, D)
        p.wait("t", "wload")
        for c in range(C):
            if c > 0:
                p.wait("t", "evq", p.val("evq"))
            for tb in range(NB):
                o = pA[:, tb * 512:(tb + 1) * 512]
                for kc in range(C):
                    mm(o, qwt(kc, c * 128, 128), fmc(kc)[:, tb * 512:(tb + 1) * 512],
                       kc == 0, kc == C - 1, inc="mmq" if kc == C - 1 else None)
            p.wait("s", "mmq", p.val("mmq"))
            for tb in range(NB):
                p.op("s", "activation", qfm[:, c * T + tb * 512:c * T + (tb + 1) * 512],
                     pA[:, tb * 512:(tb + 1) * 512], ACT.Identity,
                     bias=qb[:, c:c + 1], inc="evq")
        p.wait("t", "evq", p.val("evq"))
        wpool.release()
        wpool.mark()
        kwt, kwv = load_weight(E["g_kwt"], D, 256)
        vwt, vwv = load_weight(E["g_vwt"], D, 256)
        p.wait("t", "wload")
        for c in range(2):
            for tb in range(NB):
                o = pA[:, tb * 512:(tb + 1) * 512]
                for kc in range(C):
                    mm(o, kwt(kc, c * 128, 128), fmc(kc)[:, tb * 512:(tb + 1) * 512],
                       kc == 0, kc == C - 1, inc="mmk" if kc == C - 1 else None)
            p.wait("s", "mmk", p.val("mmk"))
            for tb in range(NB):
                p.op("s", "activation",
                     kfull[:, c * 4096 + tb * 512:c * 4096 + (tb + 1) * 512],
                     pA[:, tb * 512:(tb + 1) * 512], ACT.Identity,
                     bias=kb[:, c:c + 1], inc="evk")
            p.wait("t", "evk", p.val("evk"))
        p.op("v", "memset", fview(vfull, 260 * 16 * 0 + 64, [[260, 32], [65, 4], [1, 1]]), 1.0)
        for j in range(TT):
            o = pB[:, (j % 2) * 1024:(j % 2) * 1024 + 256]
            if j >= 2:
                p.wait("t", "evv", p.val("evv") - 1)
            for kc in range(C):
                mm(o, fmc(kc)[:, j * 128:(j + 1) * 128], vwt(kc, 0, 256),
                   kc == 0, kc == C - 1, inc="mmv" if kc == C - 1 else None)
            p.wait("v", "mmv", p.val("mmv"))
            dst = fview(vfull, j * 260, [[65, 4], [1, 64]])
            p.op("v", "scalar_tensor_tensor", dst, o, 1.0, vbrow[:, 0:256],
                 ALU.mult, ALU.add, inc="evv")
        p.wait("t", "evv", p.val("evv"))
        p.wait("g", "evv", p.val("evv"))
        p.wait("g", "evk", p.val("evk"))
        wpool.release()

        tap("s2xn", xn_fm, [128, C * T], BF16, "v")
        tap("s2q", qfm, [128, C * T], BF16, "s")
        gdma(d_kv_in[:, 0:2048], kfull[:, 0:2048])
        gdma(d_kv_in[:, 2048:4096], kfull[:, 4096:4096 + 2048])
        gdma(d_kv_in[:, 4096:4096 + 4160], vfull[:, 0:16 * 260])
        p.wait("g", "gdma")
        cc("AllGather", [d_kv_in], [d_kv_out])
        for rr in range(2):
            gdma(kfull[:, rr * 2048:rr * 2048 + 2048], d_kv_out[rr][:, 0:2048])
            gdma(kfull[:, 4096 + rr * 2048:4096 + rr * 2048 + 2048],
                 d_kv_out[rr][:, 2048:4096])
            gdma(vfull[:, rr * 16 * 260:(rr + 1) * 16 * 260],
                 d_kv_out[rr][:, 4096:4096 + 4160])
        tvkv = p.val("gdma")
        p.wait("t", "gdma", tvkv)
        p.wait("v", "gdma", tvkv)
        p.wait("s", "gdma", tvkv)

        if S2CUT <= 1:
            pool.release()
            return
        tap("s2k", kfull, [128, 2 * 4096], BF16, "g")
        tap("s2v", vfull, [128, 32 * 260], BF16, "g")
        P_sb = seg(SD + 8256, 1024, BF16)
        rcp = dump[:, 0:512]
        bc64 = dump[:, 512:1024]
        for i in range(C):
            kvc = i // 4
            for w in range(2):
                for tkc in range(32):
                    lhA = kfull[0:64, kvc * 4096 + tkc * 128:kvc * 4096 + (tkc + 1) * 128]
                    lhB = kfull[64:128, kvc * 4096 + tkc * 128:kvc * 4096 + (tkc + 1) * 128]
                    for tq in range(2):
                        qA = qfm[0:64, i * T + w * 1024 + tq * 512:i * T + w * 1024 + (tq + 1) * 512]
                        qB = qfm[64:128, i * T + w * 1024 + tq * 512:i * T + w * 1024 + (tq + 1) * 512]
                        mm(pA[:, tq * 512:(tq + 1) * 512], lhA, qA, True, True, tp=(0, 0))
                        mm(pA[:, 1024 + tq * 512:1024 + (tq + 1) * 512], lhB, qB,
                           True, True, tp=(64, 0),
                           inc="mms" if tq == 1 else None)
                    p.wait("s", "mms", p.val("mms"))
                    p.op("s", "activation", P_sb[:, 0:2048], pA[:, 0:2048], ACT.Exp,
                         scale=0.125, inc="mme")
                    p.wait("t", "mme", p.val("mme"))
                    for hh in range(2):
                        kv = kvc * 2 + hh
                        lhv = vfull[:, tkc * 260 + kv * 65:tkc * 260 + kv * 65 + 65]
                        for tq in range(2):
                            mm(pB[0:65, (hh * 2 + tq) * 512:(hh * 2 + tq) * 512 + 512],
                               lhv, P_sb[:, hh * 1024 + tq * 512:hh * 1024 + (tq + 1) * 512],
                               tkc == 0, tkc == 31,
                               inc="mma" if (tkc == 31 and hh == 1 and tq == 1) else None)
                p.wait("v", "mma", p.val("mma"))
                for hh in range(2):
                    for tq in range(2):
                        s0 = (hh * 2 + tq) * 512
                        p.op("v", "reciprocal", rcp[0:1, 0:512], pB[64:65, s0:s0 + 512],
                             inc="rcpv")
                        p.wait("t", "rcpv", p.val("rcpv"))
                        mm(pA[0:64, s0:s0 + 512], onesq_s[0:1, 0:64],
                           rcp[0:1, 0:512], True, True, inc="bcm")
                        p.wait("v", "bcm", p.val("bcm"))
                        p.op("v", "tensor_copy",
                             AP(bc64.tensor, bc64.offset, [[bc64.ap[0][0], 64], [1, 512]]),
                             pA[0:64, s0:s0 + 512])
                        p.op("v", "tensor_tensor",
                             qfm[hh * 64:hh * 64 + 64,
                                 i * T + w * 1024 + tq * 512:i * T + w * 1024 + (tq + 1) * 512],
                             pB[0:64, s0:s0 + 512],
                             AP(bc64.tensor, bc64.offset, [[bc64.ap[0][0], 64], [1, 512]]),
                             ALU.mult, inc="nrm")
                p.wait("t", "nrm", p.val("nrm"))
        if S2CUT <= 2:
            pool.release()
            return
        tap("s2att", qfm, [128, C * T], BF16, "v")
        p.barrier()

        wpool.mark()
        owt, owv = load_weight(E["g_owt"], D, D)
        src_fm = seg(SD, 8192, BF16)
        p.wait("t", "wload")
        for c in range(C):
            if c > 0:
                p.wait("t", "evo2", p.val("evo2"))
            for tb in range(NB):
                o = pA[:, tb * 512:(tb + 1) * 512]
                for kc in range(C):
                    mm(o, owt(kc, c * 128, 128), qfm[:, kc * T + tb * 512:kc * T + (tb + 1) * 512],
                       kc == 0, kc == C - 1, inc="mmo2" if kc == C - 1 else None)
            p.wait("v", "mmo2", p.val("mmo2"))
            for tb in range(NB):
                sl = slice(c * T + tb * 512, c * T + (tb + 1) * 512)
                p.op("v", "scalar_tensor_tensor", src_fm[:, sl],
                     pA[:, tb * 512:(tb + 1) * 512], ob[:, c:c + 1],
                     xn_fm[:, sl], ALU.add, ALU.add, inc="evo2")
        p.wait("t", "evo2", p.val("evo2"))
        p.wait("s", "evo2", p.val("evo2"))
        wpool.release()

        tap("s2src", src_fm, [128, C * T], BF16, "v")
        sqb = dump[:].bitcast(BF16)
        for c in range(C):
            p.op("s", "activation", sqb[:, 0:2048], src_fm[:, c * T:(c + 1) * T],
                 ACT.Square, inc="sq")
            p.wait("t", "sq", p.val("sq"))
            for tb in range(NB):
                mm(pB[0:1, tb * 512:(tb + 1) * 512], onescb[:],
                   sqb[:, tb * 512:(tb + 1) * 512], c == 0, c == C - 1,
                   inc="sqm" if (c == C - 1 and tb == NB - 1) else None)
            if c < C - 1:
                tok = p.tick("t")
                p.wait_tok("s", tok)
        p.wait("v", "sqm", p.val("sqm"))
        rr1_row = dump[0:1, 0:2048]
        p.op("v", "tensor_scalar", rr1_row, pB[0:1, 0:2048], 1.0 / D, None, ALU.mult)
        p.op("v", "tensor_scalar", rr1_row, rr1_row, 1e-6, None, ALU.add)
        p.op("v", "reciprocal", rr1_row, rr1_row, inc="rr1v")
        p.wait("s", "rr1v")
        p.op("s", "activation", rr1_row, rr1_row, ACT.Sqrt, inc="rr1s")
        p.wait("t", "rr1s", p.val("rr1s"))
        rmsbc = seg(SB, 2048)
        for w0 in range(0, 2048, 512):
            mm(pA[:, w0:w0 + 512], onesq_s[0:1, 0:128], rr1_row[0:1, w0:w0 + 512],
               True, True, inc="bc1" if w0 == 1536 else None)
        p.wait("v", "bc1", p.val("bc1"))
        p.op("v", "tensor_copy", rmsbc[:, 0:2048], pA[:, 0:2048], inc="bc1c")
        p.wait("t", "bc1c", p.val("bc1c"))
        p.wait("s", "bc1c", p.val("bc1c"))
        srcn_fm = seg(SA, 8192, BF16)
        for c in range(C):
            for tb in range(NB):
                sl = slice(c * T + tb * 512, c * T + (tb + 1) * 512)
                p.op("v", "scalar_tensor_tensor", srcn_fm[:, sl], src_fm[:, sl],
                     nw1[:, c:c + 1], rmsbc[:, tb * 512:(tb + 1) * 512],
                     ALU.mult, ALU.mult, inc="sn" if (c == C - 1 and tb == NB - 1) else None)
        p.wait("t", "sn", p.val("sn"))
        p.wait("g", "sn", p.val("sn"))

        if S2CUT <= 3:
            pool.release()
            return
        tap("s2srcn", srcn_fm, [128, C * T], BF16, "v")
        gwt, gwv = load_weight(E["g_gatew"], D, 8)
        wpool.mark()
        p.wait("t", "wload")
        for tb in range(NB):
            o = pB[0:8, tb * 8:(tb + 1) * 8] if False else pB[0:8, tb * 512:(tb + 1) * 512]
            for kc in range(C):
                mm(o, gwt(kc, 0, 8), srcn_fm[:, kc * T + tb * 512:kc * T + (tb + 1) * 512],
                   kc == 0, kc == C - 1, inc="mmg" if kc == C - 1 else None)
        p.wait("s", "mmg", p.val("mmg"))
        probs_fm = AP(dump[:].tensor, dump[:].offset, [[2048, 8], [1, 2048]])
        probs_fm8 = AP(probs_fm.tensor, probs_fm.offset, [[probs_fm.ap[0][0], 8], [1, 2048]])
        for tb in range(NB):
            p.op("s", "activation",
                 AP(probs_fm.tensor, probs_fm.offset + tb * 512, [[probs_fm.ap[0][0], 8], [1, 512]]),
                 pB[0:8, tb * 512:(tb + 1) * 512], ACT.Sigmoid,
                 inc="evg")
        p.wait("t", "evg", p.val("evg"))
        ptok = seg(SC + 1308, 128)
        for j in range(TT):
            p.op("t", "transpose", pB[:, (j % 4) * 128:(j % 4) * 128 + 8],
                 AP(probs_fm.tensor, probs_fm.offset + j * 128, [[probs_fm.ap[0][0], 8], [1, 128]]),
                 ident[0:8, 0:8], inc="ptr")
            p.wait("v", "ptr", p.val("ptr"))
            p.op("v", "tensor_copy", ptok[:, j * 8:(j + 1) * 8],
                 pB[:, (j % 4) * 128:(j % 4) * 128 + 8], inc="pev")
            if j >= 3:
                p.wait("t", "pev", p.val("pev") - 3)
        p.wait("t", "pev", p.val("pev"))
        G = seg(SC + 1436, 128)
        mx = col(8); w1n = col(1); w2n = col(1); msk = col(8)
        for j in range(TT):
            pj = ptok[:, j * 8:(j + 1) * 8]
            p.op("v", "max", mx, pj)
            p.op("v", "tensor_tensor", w1n, mx[:, 0:1], mx[:, 1:2], ALU.add)
            p.op("v", "tensor_scalar", w1n, w1n, 1e-6, None, ALU.add)
            p.op("v", "reciprocal", w1n, w1n)
            p.op("v", "tensor_tensor", w2n, mx[:, 1:2], w1n, ALU.mult)
            p.op("v", "tensor_tensor", w1n, mx[:, 0:1], w1n, ALU.mult)
            p.op("v", "tensor_scalar", msk, pj, mx[:, 0:1], None, ALU.is_equal)
            p.op("v", "tensor_scalar", G[:, j * 8:(j + 1) * 8], msk, w1n, None, ALU.mult)
            p.op("v", "tensor_scalar", msk, pj, mx[:, 1:2], None, ALU.is_equal)
            p.op("v", "scalar_tensor_tensor", G[:, j * 8:(j + 1) * 8], msk, w2n,
                 G[:, j * 8:(j + 1) * 8], ALU.mult, ALU.add,
                 inc="gw" if j == TT - 1 else None)
        p.wait("t", "gw", p.val("gw"))
        wpool.release()

        if S2CUT <= 4:
            pool.release()
            return
        tap("s2G", G, [128, 128], F32, "v")
        p.barrier()
        acc = seg(SB, 8192, BF16)
        p.wait("y", "loa", p.val("loa"))
        tvx = ydma(tok_dram(d_xsp), X[:])
        for e_ in "tvs":
            p.wait(e_, "ydma", tvx)
        h1f = X[:].bitcast(BF16)
        gb_sb = seg(SD + 8192, 1824)
        gb_sb = None
        gbuf = seg(SD + 2048, 2048)
        for e in range(8):
            for j in range(TT):
                p.op("t", "transpose", pB[0:1, j * 128:(j + 1) * 128],
                     G[:, j * 8 + e:j * 8 + e + 1], ident[:],
                     inc="gtr" if j == TT - 1 else None)
            p.wait("v", "gtr", p.val("gtr"))
            p.op("v", "tensor_copy", dump[0:1, 0:2048], pB[0:1, 0:2048], inc="gcp")
            p.wait("t", "gcp", p.val("gcp"))
            for w0 in range(0, 2048, 512):
                mm(pA[:, w0:w0 + 512], onesq_s[0:1, 0:128], dump[0:1, w0:w0 + 512],
                   True, True, inc="gbm" if w0 == 1536 else None)
            p.wait("v", "gbm", p.val("gbm"))
            p.op("v", "tensor_copy", gbuf[:, 0:2048], pA[:, 0:2048], inc="gbc")
            p.wait("t", "gbc", p.val("gbc"))
            for ocg in range(4):
                wpool.mark()
                w1s, _ = load_weight(E["g_w1"][e][:, ocg * 512:(ocg + 1) * 512], D, 512)
                p.wait("t", "wload")
                for oci in range(4):
                    oc = ocg * 4 + oci
                    ps = pA if oci % 2 == 0 else pB
                    if p.val("eh1") > 4:
                        p.wait("t", "eh1", p.val("eh1") - 4)
                    for tb in range(NB):
                        o = ps[:, tb * 512:(tb + 1) * 512]
                        for kc in range(C):
                            mm(o, w1s(kc, oci * 128, 128),
                               srcn_fm[:, kc * T + tb * 512:kc * T + (tb + 1) * 512],
                               kc == 0, kc == C - 1, inc="mh1" if kc == C - 1 else None)
                    p.wait("s", "mh1", p.val("mh1"))
                    for tb in range(NB):
                        p.op("s", "activation",
                             h1f[:, oc * 2048 + tb * 512:oc * 2048 + (tb + 1) * 512],
                             ps[:, tb * 512:(tb + 1) * 512], ACT.Gelu, inc="eh1")
                p.wait("t", "eh1", p.val("eh1"))
                wpool.release()
            for oc in range(C):
                wpool.mark()
                w2s, _ = load_weight(E["g_w2"][e][:, oc * 128:(oc + 1) * 128], 2048, 128)
                p.wait("t", "wload")
                ps = pA if oc % 2 == 0 else pB
                if p.val("ea") > 1:
                    p.wait("t", "ea", p.val("ea") - 1)
                for tb in range(NB):
                    o = ps[:, tb * 512:(tb + 1) * 512]
                    for hc in range(16):
                        mm(o, w2s(hc, 0, 128),
                           h1f[:, hc * 2048 + tb * 512:hc * 2048 + (tb + 1) * 512],
                           hc == 0, hc == 15, inc="ma" if hc == 15 else None)
                p.wait("v", "ma", p.val("ma"))
                for tb in range(NB):
                    sl = slice(oc * T + tb * 512, oc * T + (tb + 1) * 512)
                    lastt = tb == NB - 1
                    if e == 0:
                        p.op("v", "tensor_tensor", acc[:, sl],
                             ps[:, tb * 512:(tb + 1) * 512],
                             gbuf[:, tb * 512:(tb + 1) * 512], ALU.mult,
                             inc="ea" if lastt else None)
                    else:
                        p.op("v", "tensor_tensor", seg(SD, 512)[:, 0:512],
                             ps[:, tb * 512:(tb + 1) * 512],
                             gbuf[:, tb * 512:(tb + 1) * 512], ALU.mult)
                        p.op("v", "tensor_tensor", acc[:, sl], acc[:, sl],
                             seg(SD, 512)[:, 0:512], ALU.add,
                             inc="ea" if lastt else None)
                wpool.release()
            p.wait("t", "ea", p.val("ea"))
        p.barrier()

        if S2CUT <= 5:
            tvr5 = ydma(X[:], tok_dram(d_xsp))
            for e_ in "tvsg":
                p.wait(e_, "ydma", tvr5)
            pool.release()
            return
        tap("s2acc", acc, [128, TT * D], BF16, "v")
        for c in range(C):
            p.op("v", "tensor_tensor", acc[:, c * T:(c + 1) * T],
                 acc[:, c * T:(c + 1) * T], srcn_fm[:, c * T:(c + 1) * T], ALU.add,
                 inc="s2" if c == C - 1 else None)
        p.wait("s", "s2", p.val("s2"))
        p.wait("t", "s2", p.val("s2"))
        sqb2 = dump[:].bitcast(BF16)
        for c in range(C):
            p.op("s", "activation", sqb2[:, 0:2048], acc[:, c * T:(c + 1) * T],
                 ACT.Square, inc="sq2")
            p.wait("t", "sq2", p.val("sq2"))
            for tb in range(NB):
                mm(pB[0:1, tb * 512:(tb + 1) * 512], onescb[:],
                   sqb2[:, tb * 512:(tb + 1) * 512], c == 0, c == C - 1,
                   inc="sqm2" if (c == C - 1 and tb == NB - 1) else None)
            if c < C - 1:
                tok = p.tick("t")
                p.wait_tok("s", tok)
        p.wait("v", "sqm2", p.val("sqm2"))
        rr2_row = dump[0:1, 0:2048]
        p.op("v", "tensor_scalar", rr2_row, pB[0:1, 0:2048], 1.0 / D, None, ALU.mult)
        p.op("v", "tensor_scalar", rr2_row, rr2_row, 1e-6, None, ALU.add)
        p.op("v", "reciprocal", rr2_row, rr2_row, inc="rr2v")
        p.wait("s", "rr2v")
        p.op("s", "activation", rr2_row, rr2_row, ACT.Sqrt, inc="rr2s")
        p.wait("t", "rr2s", p.val("rr2s"))
        rms2bc = seg(SD, 2048)
        for w0 in range(0, 2048, 512):
            mm(pA[:, w0:w0 + 512], onesq_s[0:1, 0:128], rr2_row[0:1, w0:w0 + 512],
               True, True, inc="bc2" if w0 == 1536 else None)
        p.wait("v", "bc2", p.val("bc2"))
        p.op("v", "tensor_copy", rms2bc[:, 0:2048], pA[:, 0:2048], inc="bc2c")
        p.wait("t", "bc2c", p.val("bc2c"))
        p.wait("s", "bc2c", p.val("bc2c"))
        out2 = seg(SA, 8192, BF16)
        for c in range(C):
            for tb in range(NB):
                sl = slice(c * T + tb * 512, c * T + (tb + 1) * 512)
                p.op("v", "scalar_tensor_tensor", out2[:, sl], acc[:, sl],
                     nw2[:, c:c + 1], rms2bc[:, tb * 512:(tb + 1) * 512],
                     ALU.mult, ALU.mult,
                     inc="o2" if (c == C - 1 and tb == NB - 1) else None)
        p.wait("t", "o2", p.val("o2"))
        p.wait("g", "o2", p.val("o2"))
        tap("s2out2", out2, [128, C * T], BF16, "v")
        tvr = ydma(X[:], tok_dram(d_xsp))
        for e_ in "tvs":
            p.wait(e_, "ydma", tvr)
        p.barrier()
        if S2CUT <= 6:
            pool.release()
            return
        sj = col(4)
        ot2 = seg(SD + 2048, 1024)
        for j in range(min(TT, S2TAIL)):
            for c in range(C):
                mm_transpose_bf(out2[:, c * T + j * 128:c * T + (j + 1) * 128],
                                pA[:, c * 128:(c + 1) * 128])
                if c == C - 1:
                    tok = p.tick("t")
                    p.wait_tok("v", tok)
            p.op("v", "tensor_copy", ot2, pA[:, 0:1024], inc="oc2")
            p.wait("s", "oc2", p.val("oc2"))
            p.wait("t", "oc2", p.val("oc2"))
            ln_add_tile(ot2, j, sj)
        pool.release()


    def slot3():
        ccur[0] = 0
        p.barrier()
        pool.mark()
        reg = pool.alloc(29000)

        def seg(off, n, dtype=F32):
            a = reg[:, off:off + n]
            return a.bitcast(BF16) if dtype == BF16 else a
        r, nmr, mean, var = ln_cols(X_t, TT, D, 1e-5)
        sv3 = p.tick("s")
        p.wait_tok("v", sv3)
        finish_nmr(r, nmr, mean)
        vs_sync()
        xn_fm = seg(0, 8192, BF16)
        ln_transpose(X_t, r, nmr, xn_fm)
        tap("xn3", xn_fm, [128, 16384], BF16, "v")
        p.wait("y", "loa", p.val("loa"))
        tvx = ydma(tok_dram(d_xsp), X[:])
        for e_ in "tvsg":
            p.wait(e_, "ydma", tvx)
        kfull = X[:].bitcast(BF16)
        qfm = seg(8192, 8192, BF16)
        vloc = seg(16384, 8320, BF16)
        cstile = seg(24704, 1024, BF16)
        P_sb = dump[:, 0:1024].bitcast(BF16)

        def fmc(c):
            return xn_fm[:, c * T:(c + 1) * T]

        wpool.mark()
        qwt, _ = load_weight(E["a_qwt"], D, D)
        p.wait("t", "wload")

        def qk_pass(wt, dst_fm):
            for j in range(TT):
                half = j % 2
                ct = cstile[:, half * 1024:half * 1024 + 512]
                st_ = cstile[:, half * 1024 + 512:half * 1024 + 1024]
                if p.val("rps") > 1:
                    p.wait("y", "rps", p.val("rps") - 1)
                ydma(ct, E["a_cos"][:, j * 512:(j + 1) * 512])
                tvcs = ydma(st_, E["a_sin"][:, j * 512:(j + 1) * 512])
                o = pA[:, (j % 2) * 1024:(j % 2) * 1024 + 1024]
                if j >= 2:
                    p.wait("t", "rtp", p.val("rtp") - 1)
                for kc in range(C):
                    mm(o[:, 0:512], fmc(kc)[:, j * 128:(j + 1) * 128], wt(kc, 0, 512),
                       kc == 0, kc == C - 1)
                    mm(o[:, 512:1024], fmc(kc)[:, j * 128:(j + 1) * 128], wt(kc, 512, 512),
                       kc == 0, kc == C - 1, inc="mmr" if kc == C - 1 else None)
                p.wait("v", "mmr", p.val("mmr"))
                p.wait("v", "ydma", tvcs)
                ev = AP(o.tensor, o.offset, [o.ap[0], [2, 512]])
                od = AP(o.tensor, o.offset + 1, [o.ap[0], [2, 512]])
                cv = ct
                sv_ = st_
                t1 = dump[:, 0:512]
                t2 = dump[:, 512:1024]
                rt_ = dump[:, 1024:2048].bitcast(BF16)
                rte = AP(rt_.tensor, rt_.offset, [rt_.ap[0], [2, 512]])
                rto = AP(rt_.tensor, rt_.offset + 1, [rt_.ap[0], [2, 512]])
                p.op("v", "tensor_tensor", t1, ev, cv, ALU.mult)
                p.op("v", "tensor_tensor", t2, od, sv_, ALU.mult)
                p.op("v", "tensor_tensor", t1, t1, t2, ALU.subtract)
                p.op("v", "tensor_copy", rte, t1)
                p.op("v", "tensor_tensor", t1, od, cv, ALU.mult)
                p.op("v", "tensor_tensor", t2, ev, sv_, ALU.mult)
                p.op("v", "tensor_tensor", t1, t1, t2, ALU.add)
                p.op("v", "tensor_copy", rto, t1, inc="rps")
                p.wait("t", "rps", p.val("rps"))
                half = j % 2
                pdst = pB[:, half * 1024:half * 1024 + 1024]
                if j >= 2:
                    p.wait("t", "rpe", p.val("rpe") - 1)
                for c in range(C):
                    mm(pdst[:, c * 128:(c + 1) * 128],
                       rt_[:, c * 128:(c + 1) * 128], identb[:],
                       True, True, inc="rtp" if c == C - 1 else None)
                p.wait("v", "rtp", p.val("rtp"))
                dst_ap = AP(dst_fm.tensor, dst_fm.offset + j * 128,
                            [dst_fm.ap[0], [dst_fm_w[0], C], [1, 128]])
                p.op("v", "tensor_copy", dst_ap, pdst[:, 0:C * 128], inc="rpe")
            p.wait("t", "rpe", p.val("rpe"))

        dst_fm_w = [T]
        qk_pass(qwt, qfm)
        tap("qfm3", qfm, [128, 16384], BF16, "v")
        wpool.release()
        wpool.mark()
        kwt, _ = load_weight(E["a_kwt"], D, D)
        p.wait("t", "wload")
        dst_fm_w[0] = 4096
        klocal = kfull
        qk_pass(kwt, kfull)
        wpool.release()
        wpool.mark()
        vwt, _ = load_weight(E["a_vwt"], D, D)
        p.wait("t", "wload")
        p.op("v", "memset", fview(vloc, 64, [[1040, 16], [65, 16], [1, 1]]), 1.0)
        for j in range(TT):
            o = pA[:, (j % 2) * 1024:(j % 2) * 1024 + 1024]
            if j >= 2:
                p.wait("t", "evv3", p.val("evv3") - 1)
            for kc in range(C):
                mm(o[:, 0:512], fmc(kc)[:, j * 128:(j + 1) * 128], vwt(kc, 0, 512),
                   kc == 0, kc == C - 1)
                mm(o[:, 512:1024], fmc(kc)[:, j * 128:(j + 1) * 128], vwt(kc, 512, 512),
                   kc == 0, kc == C - 1, inc="mmv3" if kc == C - 1 else None)
            p.wait("v", "mmv3", p.val("mmv3"))
            dst = fview(vloc, j * 1040, [[65, 16], [1, 64]])
            p.op("v", "tensor_copy", dst, o[:, 0:1024], inc="evv3")
        p.wait("t", "evv3", p.val("evv3"))
        wpool.release()
        p.barrier()

        kvm = seg(28752, 4)
        tvm = ydma(kvm, E["a_kvmask"])
        p.wait("v", "ydma", tvm)
        stg = seg(8192, 4096, BF16)
        for rr2 in range(2):
            for hf in range(2):
                mcol = kvm[:, rr2 * 2 + hf:rr2 * 2 + hf + 1]
                for c in range(C):
                    srcap = fview(kfull, c * 4096, [[1, 2048]])
                    dst = dump[:, 0:1024].bitcast(BF16)
                    p.op("v", "tensor_scalar", dst[:, 0:2048], srcap, mcol, None,
                         ALU.mult, inc="msk")
                    p.wait("g", "msk", p.val("msk"))
                    gdma(d_rs_in[rr2][:, hf * KVW + c * 2048:hf * KVW + (c + 1) * 2048],
                         dst[:, 0:2048])
                    tokg = p.tick("g")
                    p.wait_tok("v", tokg)
                off = 0
                while off < 16640:
                    wlen = min(2048, 16640 - off)
                    srcap = fview(vloc, off, [[1, wlen]])
                    dst = dump[:, 0:1024].bitcast(BF16)
                    p.op("v", "tensor_scalar", dst[:, 0:wlen], srcap, mcol, None,
                         ALU.mult, inc="msk")
                    p.wait("g", "msk", p.val("msk"))
                    gdma(d_rs_in[rr2][:, hf * KVW + 16384 + off:hf * KVW + 16384 + off + wlen],
                         dst[:, 0:wlen])
                    tokg = p.tick("g")
                    p.wait_tok("v", tokg)
                    off += wlen
        p.wait("g", "gdma")
        cc("ReduceScatter", [d_rs_in], [d_rs_out], ALU.add)
        vfA = seg(0, 6240, BF16)
        vfB = seg(16384, 10400, BF16)

        def vfh(tile, hd):
            base, off = ((vfA, tile * 1040) if tile < 12 else
                         (vfB, (tile - 12) * 1040))
            return fview(base, off + hd * 65, [[1, 65]])
        for hf in range(2):
            for c in range(C):
                gdma(fview(kfull, c * 4096 + hf * 2048, [[1, 2048]]),
                     d_rs_out[:, hf * KVW + c * 2048:hf * KVW + (c + 1) * 2048])
            if hf == 0:
                gdma(fview(vfA, 0, [[1, 12480]]), d_rs_out[:, 16384:16384 + 12480])
                gdma(fview(vfB, 0, [[1, 4160]]),
                     d_rs_out[:, 16384 + 12480:16384 + 16640])
            else:
                gdma(fview(vfB, 4160, [[1, 16640]]),
                     d_rs_out[:, KVW + 16384:KVW + 16384 + 16640])
        tvkv = p.val("gdma")
        for e_ in "tvs":
            p.wait(e_, "gdma", tvkv)
        tap("kfull3", kfull, [128, 32768], BF16, "g")
        tap("vfA3", vfA, [128, 12480], BF16, "g")
        tap("vfB3", vfB, [128, 20800], BF16, "g")

        poi = seg(28756, 32)
        ydma(poi, E["a_poison"])
        diagm = seg(6240, 1280, BF16)
        tvp = ydma(diagm, E["a_diagm"])
        p.wait("s", "ydma", tvp)
        p.wait("v", "ydma", tvp)
        att_fm = qfm
        rcp3 = dump[:, 1024:1536]
        bc643 = dump[:, 1536:2048]
        for i in range(C):
            for w in range(2):
                cklist = [c_ for c_ in range(32)
                          if not (c_ >= 16 and (c_ - 16) * 128 > w * 1024 + 1023)]
                cklast = cklist[-1]
                for ckg in range(32):
                    own = ckg >= 16
                    if own:
                        krel = (ckg - 16) * 128
                        qlo = w * 1024
                        if krel > qlo + 1023:
                            continue
                    lhA = kfull[0:64, i * 4096 + ckg * 128:i * 4096 + (ckg + 1) * 128]
                    lhB = kfull[64:128, i * 4096 + ckg * 128:i * 4096 + (ckg + 1) * 128]
                    for tq in range(2):
                        qA = qfm[0:64, i * T + w * 1024 + tq * 512:i * T + w * 1024 + (tq + 1) * 512]
                        qB = qfm[64:128, i * T + w * 1024 + tq * 512:i * T + w * 1024 + (tq + 1) * 512]
                        mm(pA[:, tq * 512:(tq + 1) * 512], lhA, qA, True, True, tp=(0, 0))
                        mm(pA[:, 1024 + tq * 512:1024 + (tq + 1) * 512], lhB, qB,
                           True, True, tp=(64, 0), inc="mms3" if tq == 1 else None)
                    p.wait("s", "mms3", p.val("mms3"))
                    p.op("s", "activation", P_sb[:, 0:2048], pA[:, 0:2048], ACT.Exp,
                         scale=0.125, bias=poi[:, ckg:ckg + 1], inc="mme3")
                    need_mask = own and any(
                        krel <= w * 1024 + tq * 512 + 511 and krel + 127 > w * 1024 + tq * 512
                        for tq in range(2))
                    p.wait("t", "mme3", p.val("mme3"))
                    if own:
                        p.wait("v", "mme3", p.val("mme3"))
                        done_any = False
                        for hh in range(2):
                            for tq in range(2):
                                base = w * 1024 + tq * 512 - krel
                                if base >= 128:
                                    continue
                                m4 = min((-base) // 128, 4) if base < 0 else 0
                                sl = P_sb[:, hh * 1024 + tq * 512:hh * 1024 + (tq + 1) * 512]
                                p.op("v", "tensor_tensor", sl, sl,
                                     diagm[:, m4 * 512:(m4 + 1) * 512], ALU.mult,
                                     inc="msk3")
                                done_any = True
                        if done_any:
                            p.wait("t", "msk3", p.val("msk3"))
                            p.wait("s", "msk3", p.val("msk3"))
                    for hh in range(2):
                        h = 2 * i + hh
                        lhv = vfh(ckg, h)
                        for tq in range(2):
                            mm(pB[0:65, (hh * 2 + tq) * 512:(hh * 2 + tq) * 512 + 512],
                               lhv, P_sb[:, hh * 1024 + tq * 512:hh * 1024 + (tq + 1) * 512],
                               ckg == 0, ckg == cklast,
                               inc="mma3" if (ckg == cklast and hh == 1 and tq == 1) else None)
                p.wait("v", "mma3", p.val("mma3"))
                for hh in range(2):
                    for tq in range(2):
                        s0 = (hh * 2 + tq) * 512
                        p.op("v", "reciprocal", rcp3[0:1, 0:512], pB[64:65, s0:s0 + 512],
                             inc="rcp3")
                        p.wait("t", "rcp3", p.val("rcp3"))
                        mm(pA[0:64, s0 % 2048:s0 % 2048 + 512], onesq_s[0:1, 0:64],
                           rcp3[0:1, 0:512], True, True, inc="bcm3")
                        p.wait("v", "bcm3", p.val("bcm3"))
                        p.op("v", "tensor_copy",
                             AP(bc643.tensor, bc643.offset, [[bc643.ap[0][0], 64], [1, 512]]),
                             pA[0:64, s0 % 2048:s0 % 2048 + 512])
                        p.op("v", "tensor_tensor",
                             att_fm[hh * 64:hh * 64 + 64,
                                    i * T + w * 1024 + tq * 512:i * T + w * 1024 + (tq + 1) * 512],
                             pB[0:64, s0:s0 + 512],
                             AP(bc643.tensor, bc643.offset, [[bc643.ap[0][0], 64], [1, 512]]),
                             ALU.mult, inc="nrm3")
                p.wait("t", "nrm3", p.val("nrm3"))
        tap("att3", att_fm, [128, 16384], BF16, "v")
        p.barrier()
        tvr = ydma(X[:], tok_dram(d_xsp))
        for e_ in "tvsg":
            p.wait(e_, "ydma", tvr)
        p.barrier()

        wpool.mark()
        owt, _ = load_weight(E["a_owt"], D, D)
        p.wait("t", "wload")
        slot_out(lambda kc, j: att_fm[:, kc * T + j * 128:kc * T + (j + 1) * 128],
                 owt, C, otok_ap=seg(24704, 1024))
        wpool.release()
        pool.release()

    def slot4():
        ccur[0] = 0
        p.barrier()
        pool.mark()
        reg = pool.alloc(29000)

        def seg(off, n, dtype=F32):
            a = reg[:, off:off + n]
            return a.bitcast(BF16) if dtype == BF16 else a
        r, nmr, mean, var = ln_cols(X_t, TT, D, 1e-5)
        t2 = col(TT)
        qn = col(TT)
        p.op("v", "tensor_scalar", t2, var, 1e-5, None, ALU.add)
        p.op("v", "reciprocal", t2, t2)
        p.op("v", "tensor_tensor", t2, var, t2, ALU.mult)
        p.op("v", "tensor_scalar", t2, t2, float(D), None, ALU.mult, inc="s4a")
        p.wait("s", "s4a", p.val("s4a"))
        p.op("s", "activation", qn, t2, ACT.Sqrt, inc="s4b")
        p.wait("v", "s4b", p.val("s4b"))
        p.op("v", "tensor_scalar", t2, qn, 1e-12, None, ALU.max)
        p.op("v", "reciprocal", t2, t2)
        p.op("v", "tensor_scalar", t2, t2, 32.0, None, ALU.mult)
        rhat = col(TT)
        nmh = col(TT)
        p.op("v", "tensor_tensor", rhat, r, t2, ALU.mult)
        p.op("v", "tensor_tensor", nmh, mean, rhat, ALU.mult)
        p.op("v", "tensor_scalar", nmh, nmh, -1.0, None, ALU.mult)
        finish_nmr(r, nmr, mean)
        vs_sync()
        xhat = seg(0, 8192, BF16)
        ln_transpose(X_t, rhat, nmh, xhat)
        p.barrier()

        def fmx(c):
            return xhat[:, c * T:(c + 1) * T]

        b1l = seg(8192, 4); gl = seg(8200, C); gh = seg(8208, C)
        b1h = seg(8216, 32); b2l = seg(8248, D); b2h = seg(9272, D)
        rtc = seg(10296, C // 2, BF16)
        selr = seg(10300, 2)
        for dst, name in ((b1l, "f_b1l"), (gl, "f_gl"), (gh, "f_gh"),
                          (b1h, "f_b1h"), (b2l, "f_b2lrow"), (b2h, "f_b2hrow"),
                          (selr, "f_selrow")):
            ydma(dst if name != "f_selrow" else selr[0:1, 0:2], E[name])
        ydma(rtc, E["f_rt"])
        tvc4 = p.val("ydma")
        for e_ in "tvsg":
            p.wait(e_, "ydma", tvc4)

        for tb in range(NB):
            for kc in range(C):
                mm(pB[0:1, tb * 512:(tb + 1) * 512],
                   rtc[:, kc:kc + 1].bitcast(BF16) if False else rtc[:, kc:kc + 1],
                   fmx(kc)[:, tb * 512:(tb + 1) * 512],
                   kc == 0, kc == C - 1, inc="mmsr" if kc == C - 1 else None)
        p.wait("v", "mmsr", p.val("mmsr"))
        for j in range(TT):
            p.op("t", "transpose", pA[0:1, j * 128:(j + 1) * 128],
                 qn[:, j:j + 1], ident[:], inc="qtr" if j == TT - 1 else None)
        p.wait("v", "qtr", p.val("qtr"))
        srow = dump[0:1, 0:2048]
        p.op("v", "tensor_scalar", srow, pA[0:1, 0:2048], 1.0 / 32.0, None, ALU.mult)
        p.op("v", "tensor_tensor", srow, srow, pB[0:1, 0:2048], ALU.mult, inc="sr4")
        p.wait("g", "sr4", p.val("sr4"))
        p.wait("t", "sr4", p.val("sr4"))
        gdma(d_s_in[0][0:1, :], srow)
        gdma(d_s_in[1][0:1, :], srow)
        p.wait("g", "gdma")
        cc("AllGather", [d_s_in[0]], [d_s_out])
        sfull = seg(18500, 4096)[0:1, 0:4096]
        tvsf = gdma(sfull, AP(d_s_out.tensor, d_s_out.offset, [[1, 1], [1, 4096]]))
        p.wait("v", "gdma", tvsf)
        p.wait("t", "gdma", tvsf)

        s32 = seg(22596, 32)
        for f in range(32):
            p.op("t", "transpose", pA[:, (f % 16) * 128:(f % 16) * 128 + 1] if False
                 else pA[:, f:f + 1], sfull[0:1, f * 128:(f + 1) * 128], ident[0:1, 0:1],
                 inc="str" if f == 31 else None)
        p.wait("v", "str", p.val("str"))
        p.op("v", "tensor_copy", s32, pA[:, 0:32], inc="sev")
        p.wait("t", "sev", p.val("sev"))
        acol = col(1)
        ecol = col(1)
        LOGK = float(np.log(1152.0))
        LOGN = float(np.log(4096.0))
        p.op("v", "memset", acol, LOGK - LOGN)
        e32 = seg(22628, 32)
        for it in range(49):
            p.op("v", "tensor_scalar", e32, s32, acol, 0.0, ALU.add, op1=ALU.min,
                 inc="cdv")
            p.wait("s", "cdv", p.val("cdv"))
            p.op("s", "activation", e32, e32, ACT.Exp, accum_out=ecol, inc="cds")
            p.wait("t", "cds", p.val("cds"))
            mm(pB[:, 0:1], onesq_s[:], ecol, True, True, inc="cdm")
            p.wait("s", "cdm", p.val("cdm"))
            p.op("s", "activation", ecol, pB[:, 0:1], ACT.Ln, scale=1.0 / 1152.0,
                 inc="cd2")
            p.wait("v", "cd2", p.val("cd2"))
            p.op("v", "tensor_tensor", acol, acol, ecol, ALU.subtract, inc="cd3")
            p.wait("s", "cd3", p.val("cd3"))
            p.wait("t", "cd3", p.val("cd3"))
        arow = col(1)[0:1, 0:1]
        p.op("v", "tensor_scalar", sfull, sfull, acol[0:1, 0:1], 0.0, ALU.add,
             op1=ALU.min, inc="se1")
        p.wait("s", "se1", p.val("se1"))
        p.op("s", "activation", sfull, sfull, ACT.Exp, inc="se2")
        p.wait("v", "se2", p.val("se2"))
        mrow = seg(22660, 4096)[0:1, 0:4096]
        p.op("v", "tensor_scalar", mrow, sfull, 1.0, None, ALU.is_ge)
        csr = seg(18500, 4096)[0:1, 0:4096]
        zc1 = col(1)[0:1, 0:1]
        p.op("v", "memset", zc1, 0.0)
        p.op("v", "tensor_tensor_scan", csr, mrow, bc(zc1, 4096), 0.0,
             ALU.add, ALU.add)
        p.op("v", "tensor_scalar", csr, csr, 1024.0, None, ALU.is_le)
        p.op("v", "tensor_tensor", mrow, mrow, csr, ALU.mult)
        mloc = csr[0:1, 0:2048]
        p.op("v", "tensor_scalar", mloc, mrow[0:1, 0:2048], selr[0:1, 0:1], None,
             ALU.mult)
        p.op("v", "scalar_tensor_tensor", mloc, mrow[0:1, 2048:4096],
             selr[0:1, 1:2], mloc, ALU.mult, ALU.add, inc="se3")
        p.wait("t", "se3", p.val("se3"))
        mcols = col(TT)
        for j in range(TT):
            p.op("t", "transpose", pA[:, j:j + 1], mloc[0:1, j * 128:(j + 1) * 128],
                 ident[0:1, 0:1], inc="mtr" if j == TT - 1 else None)
        p.wait("v", "mtr", p.val("mtr"))
        p.op("v", "tensor_copy", mcols, pA[:, 0:TT], inc="mev")
        p.wait("t", "mev", p.val("mev"))
        p.barrier()

        wpool.mark()
        w1l, _ = load_weight(E["f_w1lt"], D, 512)
        hl = seg(18500, 4096, BF16)
        p.wait("t", "wload")
        p.wait("v", "wload")
        for kc in range(C):
            p.op("v", "tensor_scalar", w1l(kc, 0, 512), w1l(kc, 0, 512),
                 gl[:, kc:kc + 1], None, ALU.mult, inc="glw" if kc == C - 1 else None)
        p.wait("t", "glw", p.val("glw"))
        for oc4 in range(4):
            for tb in range(NB):
                o = pA[:, tb * 512:(tb + 1) * 512]
                for kc in range(C):
                    mm(o, w1l(kc, oc4 * 128, 128), fmx(kc)[:, tb * 512:(tb + 1) * 512],
                       kc == 0, kc == C - 1, inc="ml1" if kc == C - 1 else None)
            p.wait("s", "ml1", p.val("ml1"))
            for tb in range(NB):
                p.op("s", "activation", hl[:, oc4 * 2048 + tb * 512:oc4 * 2048 + (tb + 1) * 512],
                     pA[:, tb * 512:(tb + 1) * 512], ACT.Gelu,
                     bias=b1l[:, oc4 // 2 * 0 + 0:1] if False else b1l[:, oc4:oc4 + 1],
                     inc="el1")
            p.wait("t", "el1", p.val("el1"))
        wpool.release()
        wpool.mark()
        w2l, _ = load_weight(E["f_w2lt"], 512, D)
        light = seg(10308, TT * D // 2, BF16)
        p.wait("t", "wload")
        for j in range(TT):
            ob2 = pB[:, (j % 2) * 1024:(j % 2) * 1024 + 1024]
            if j >= 2:
                p.wait("t", "el2", p.val("el2") - 1)
            for oh in range(2):
                for hc in range(4):
                    mm(ob2[:, oh * 512:oh * 512 + 512],
                       hl[:, hc * 2048 + j * 128:hc * 2048 + j * 128 + 128],
                       w2l(hc, oh * 512, 512), hc == 0, hc == 3,
                       inc="ml2" if (hc == 3 and oh == 1) else None)
            p.wait("v", "ml2", p.val("ml2"))
            p.op("v", "tensor_tensor", light[:, j * D:(j + 1) * D], ob2[:, 0:1024],
                 b2l[:, 0:D], ALU.add, inc="el2")
        p.wait("t", "el2", p.val("el2"))
        wpool.release()
        p.barrier()

        p.wait("y", "loa", p.val("loa"))
        tvx = ydma(tok_dram(d_xsp), X[:])
        for e_ in "tvsg":
            p.wait(e_, "ydma", tvx)
        hh_ = X[:].bitcast(BF16)
        heavy = seg(18500, 8192, BF16)
        for th in range(2):
            for ocg in range(8):
                wpool.mark()
                w1h, _ = load_weight(E["f_w1ht"][:, ocg * 512:(ocg + 1) * 512], D, 512)
                p.wait("t", "wload")
                p.wait("v", "wload")
                for kc in range(C):
                    p.op("v", "tensor_scalar", w1h(kc, 0, 512), w1h(kc, 0, 512),
                         gh[:, kc:kc + 1], None, ALU.mult,
                         inc="ghw" if kc == C - 1 else None)
                p.wait("t", "ghw", p.val("ghw"))
                for oci in range(4):
                    oc = ocg * 4 + oci
                    for tb in range(2):
                        o = pA[:, tb * 512:(tb + 1) * 512]
                        for kc in range(C):
                            mm(o, w1h(kc, oci * 128, 128),
                               fmx(kc)[:, th * 1024 + tb * 512:th * 1024 + (tb + 1) * 512],
                               kc == 0, kc == C - 1, inc="mh4" if kc == C - 1 else None)
                    p.wait("s", "mh4", p.val("mh4"))
                    for tb in range(2):
                        p.op("s", "activation",
                             hh_[:, oc * 1024 + tb * 512:oc * 1024 + (tb + 1) * 512],
                             pA[:, tb * 512:(tb + 1) * 512], ACT.Gelu,
                             bias=b1h[:, oc:oc + 1], inc="eh4")
                    p.wait("t", "eh4", p.val("eh4"))
                wpool.release()
            for oc in range(C):
                wpool.mark()
                w2h, _ = load_weight(E["f_w2ht"][:, oc * 128:(oc + 1) * 128], 4096, 128)
                p.wait("t", "wload")
                for tb in range(2):
                    o = pA[:, tb * 512:(tb + 1) * 512]
                    for hc in range(32):
                        mm(o, w2h(hc, 0, 128),
                           hh_[:, hc * 1024 + tb * 512:hc * 1024 + (tb + 1) * 512],
                           hc == 0, hc == 31, inc="mo4" if hc == 31 else None)
                    p.wait("s", "mo4", p.val("mo4"))
                    p.op("s", "activation",
                         heavy[:, oc * T + th * 1024 + tb * 512:
                               oc * T + th * 1024 + (tb + 1) * 512],
                         pA[:, tb * 512:(tb + 1) * 512], ACT.Identity, inc="eo4")
                p.wait("t", "eo4", p.val("eo4"))
                wpool.release()
        p.barrier()
        tvr = ydma(X[:], tok_dram(d_xsp))
        for e_ in "tvsg":
            p.wait(e_, "ydma", tvr)
        p.barrier()
        sj = col(4)
        otok4 = seg(26756, 1024)
        for j in range(TT):
            for c in range(C):
                mm_transpose_bf(heavy[:, c * T + j * 128:c * T + (j + 1) * 128],
                                pA[:, c * 128:(c + 1) * 128])
                if c == C - 1:
                    tok = p.tick("t")
                    p.wait_tok("v", tok)
            p.op("v", "tensor_tensor", otok4[:, 0:D], pA[:, 0:D], b2h[:, 0:D], ALU.add)
            p.op("v", "tensor_scalar", otok4[:, 0:D], otok4[:, 0:D], mcols[:, j:j + 1],
                 None, ALU.mult)
            p.op("v", "tensor_tensor", otok4[:, 0:D], otok4[:, 0:D],
                 light[:, j * D:(j + 1) * D], ALU.add, inc="cb4")
            p.wait("t", "cb4", p.val("cb4"))
            p.wait("s", "cb4", p.val("cb4"))
            ln_add_tile(otok4[:, 0:D], j, sj)
        pool.release()

    if N_SLOTS >= 1:
        slot1()
    if N_SLOTS >= 2:
        slot2()
    if N_SLOTS >= 3:
        slot3()
    if N_SLOTS >= 4:
        slot4()

    p.barrier()
    for name, ap_ in dbg.items():
        ydma(ap_, X[:])
    ydma(tok_dram(out_x), X[:])
    p.wait("y", "ydma")
    p.emit()
    print(f"[build] pool high {pool.high * 4 / 1024:.0f}KB/p, wpool {wpool.high * 4 / 1024:.0f}KB/p, "
          f"nops: {[len(p.ops[e]) for e in p.ENGS]}, sbuf left {nc.sbuf_bytes_remaining}")
    p.ctx.close()
    return nc


def make_inputs(inputs, core):
    b, h = core // 2, core % 2
    t = lambda a: np.ascontiguousarray(a.T)
    m = {
        "x_in": np.ascontiguousarray(inputs["L0_attn_x"][b, h * T:(h + 1) * T, :]),
        "identity": np.eye(128, dtype=np.float32),
        "identity_bf": np.eye(128, dtype=BF),
        "ones_bf": np.ones((1, 128), BF),
        "ones_cbf": np.ones((128, 1), BF),
        "selcol": np.ascontiguousarray(
            np.broadcast_to(np.array([[1.0 - h, float(h)]], np.float32), (64, 2))),
        "halomask": np.ascontiguousarray(np.broadcast_to(
            np.array([[0.0, 1.0]] if h == 0 else [[0.0, 0.0]], np.float32), (128, 2))),
        "m_w1t": t(inputs["L0_attn_W1"]),
        "m_w2t": t(inputs["L0_attn_W2"]),
        "m_wlastt": t(inputs["L0_attn_Wlast"]),
        "m_dwt": t(inputs["L0_attn_dw"]),
        "m_bwt": t(inputs["L0_attn_Bw"]),
        "m_cwt": t(inputs["L0_attn_Cw"]),
        "m_dwmt": t(inputs["L0_attn_Dw"]),
        "m_convw": np.ascontiguousarray(
            inputs["L0_attn_conv_w"].reshape(C, 128, 3).transpose(1, 0, 2).reshape(128, C * 3)),
        "m_convb": np.ascontiguousarray(inputs["L0_attn_conv_b"].reshape(C, 128).T),
        "m_A": np.ascontiguousarray(inputs["L0_attn_A"].reshape(64, 1)),
        "m_Bb": np.ascontiguousarray(inputs["L0_attn_Bb"].reshape(64, 1)),
        "m_db": np.ascontiguousarray(inputs["L0_attn_db"].reshape(64, 1)),
        "m_Cb": np.ascontiguousarray(inputs["L0_attn_Cb"].reshape(C, 128).T),
        "m_Db": np.ascontiguousarray(inputs["L0_attn_Db"].reshape(C, 128).T),
        "m_normw": np.ascontiguousarray(inputs["L0_attn_norm_w"].reshape(C, 128).T),
    }
    perm = [0, 4, 1, 5, 2, 6, 3, 7, 8, 12, 9, 13, 10, 14, 11, 15]
    qw = inputs["L0_ffn_qw"]
    qcols = np.concatenate([qw[:, 64 * h:64 * h + 64] for h in perm], axis=1)
    m["g_qwt"] = np.ascontiguousarray(qcols)
    qb = inputs["L0_ffn_qb"]
    qbp = np.concatenate([qb[64 * h:64 * h + 64] for h in perm])
    m["g_qb"] = np.ascontiguousarray(qbp.reshape(C, 128).T)
    m["g_kwt"] = np.ascontiguousarray(inputs["L0_ffn_kw"])
    m["g_kb"] = np.ascontiguousarray(inputs["L0_ffn_kb"].reshape(2, 128).T)
    m["g_vwt"] = np.ascontiguousarray(inputs["L0_ffn_vw"])
    m["g_vbrow"] = np.ascontiguousarray(
        np.broadcast_to(inputs["L0_ffn_vb"][None, :], (128, 256)))
    ow = inputs["L0_ffn_ow"]
    orows = np.concatenate([ow[64 * h:64 * h + 64, :] for h in perm], axis=0)
    m["g_owt"] = np.ascontiguousarray(orows)
    m["g_ob"] = np.ascontiguousarray(inputs["L0_ffn_ob"].reshape(C, 128).T)
    m["g_gatew"] = np.ascontiguousarray(inputs["L0_ffn_gate_w"])
    m["g_nw1"] = np.ascontiguousarray(inputs["L0_ffn_norm1_w"].reshape(C, 128).T)
    m["g_nw2"] = np.ascontiguousarray(inputs["L0_ffn_norm2_w"].reshape(C, 128).T)
    m["g_w1"] = np.ascontiguousarray(inputs["L0_ffn_w1"])
    m["g_w2"] = np.ascontiguousarray(inputs["L0_ffn_w2"])
    for nm, key in (("a_qwt", "L1_attn_Wq"), ("a_kwt", "L1_attn_Wk"),
                    ("a_vwt", "L1_attn_Wv")):
        m[nm] = np.ascontiguousarray(
            inputs[key].transpose(1, 0, 2).reshape(D, D))
    m["a_owt"] = np.ascontiguousarray(inputs["L1_attn_Wo"])
    inv = 10000.0 ** (-(np.arange(0, 64, 2, dtype=np.float64) / 64.0))
    pos = (h * T + np.arange(T, dtype=np.float64))
    ang = pos[:, None] * inv[None, :]
    for nm, fn in (("a_cos", np.cos), ("a_sin", np.sin)):
        v3 = fn(ang).astype(np.float32).reshape(TT, 128, 1, 32)
        v3 = np.broadcast_to(v3, (TT, 128, 16, 32))
        m[nm] = np.ascontiguousarray(
            v3.reshape(TT, 128, 512).transpose(1, 0, 2).reshape(128, TT * 512)
        ).astype(BF)
    poi = np.zeros((128, 32), np.float32)
    if h == 0:
        poi[:, 0:16] = -1e30
    m["a_poison"] = poi
    ff = np.arange(512)[None, :]
    pp = np.arange(128)[:, None]
    dm = np.zeros((128, 2560), np.float32)
    for m4 in range(4):
        dm[:, m4 * 512:(m4 + 1) * 512] = (ff - pp - m4 * 128 >= 0)
    m["a_diagm"] = dm.astype(BF)
    kvm = np.array([0.0, 1.0, 1.0, 0.0] if h == 0 else [0.0, 0.0, 0.0, 1.0],
                   np.float32)
    m["a_kvmask"] = np.ascontiguousarray(np.broadcast_to(kvm[None, :], (128, 4)))
    m["f_rt"] = np.ascontiguousarray(
        inputs["L1_ffn_routing_token"].reshape(C, 128).T).astype(BF)
    m["f_w1lt"] = t(inputs["L1_ffn_w1_l"])
    m["f_b1l"] = np.ascontiguousarray(inputs["L1_ffn_b1_l"].reshape(4, 128).T)
    m["f_w2lt"] = t(inputs["L1_ffn_w2_l"])
    m["f_b2lrow"] = np.ascontiguousarray(
        np.broadcast_to(inputs["L1_ffn_b2_l"][None, :], (128, D)))
    m["f_w1ht"] = t(inputs["L1_ffn_w1_h"])
    m["f_b1h"] = np.ascontiguousarray(inputs["L1_ffn_b1_h"].reshape(32, 128).T)
    m["f_w2ht"] = t(inputs["L1_ffn_w2_h"])
    m["f_b2hrow"] = np.ascontiguousarray(
        np.broadcast_to(inputs["L1_ffn_b2_h"][None, :], (128, D)))
    m["f_gl"] = np.ascontiguousarray(inputs["L1_ffn_gamma_l"].reshape(C, 128).T)
    m["f_gh"] = np.ascontiguousarray(inputs["L1_ffn_gamma_h"].reshape(C, 128).T)
    m["f_selrow"] = np.array([[1.0, 0.0]] if h == 0 else [[0.0, 1.0]], np.float32)
    m["ones_sq"] = np.ones((128, 128), np.float32)
    return {k: np.asarray(v) for k, v in m.items()}


_NC_CACHE = {}


def kernel(**inputs):
    key = "prog" + ",".join(TAPS)
    if key not in _NC_CACHE:
        _NC_CACHE[key] = build_program()
    nc = _NC_CACHE[key]
    in_maps = [make_inputs(inputs, c) for c in range(8)]
    res = run_bass_kernel_spmd(nc, in_maps, core_ids=list(range(8)))
    out = np.zeros((4, 4096, 1024), np.float32)
    for c in range(8):
        b, h = c // 2, c % 2
        o = res.results[c]["out"]
        out[b, h * T:(h + 1) * T, :] = o
    if TAPS:
        kernel.tapped = [{k: v for k, v in res.results[c].items() if k.startswith("tap_")}
                         for c in range(8)]
    return out
